# version 05
# baseline (speedup 1.0000x reference)
import contextlib
import numpy as np
import ml_dtypes
import concourse.bass as bass
import concourse.mybir as mybir
from concourse.bass_utils import run_bass_kernel_spmd

F32 = mybir.dt.float32
BF16 = mybir.dt.bfloat16
I32 = mybir.dt.int32
AF = mybir.ActivationFunctionType
ALU = mybir.AluOpType
AX = mybir.AxisListType
NBF = ml_dtypes.bfloat16

NSEQ, L, DM, DF, NST = 4, 2048, 1024, 4096, 64
RNE_MAGIC = float(np.float32(12582912.0))
NCH, NTIL, NDC = 16, 16, 8
EPS = 1e-5
SUB_ALL = ("l0attn", "l0ffn", "l1attn", "l1ffn")
DEBUG_DUMPS = []
FFN_STAGE = 4


ENG_SEMS = {"pe": "pe", "dve": "dve", "act": "act", "pop": "popc"}
DMA_SEMS = {"sp": "spd", "pop": "popd"}
POP_SEMS = ("popd", "popd1", "popd2", "popd3")
SP_SEMS = ("spd", "spd1", "spd2", "spd3")
ALL_ENG = ("pe", "dve", "act", "sp", "pop")
RACE_STRICT = True


class Prog:
    def __init__(self, nc):
        self.nc = nc
        self.items = {e: [] for e in ALL_ENG}
        self.cnt = {s: 0 for s in list(ENG_SEMS.values()) + list(DMA_SEMS.values())
                    + list(POP_SEMS) + list(SP_SEMS)}
        self.pop_rot = 0
        self.sp_rot = 0
        self.last_wait = {e: {} for e in ALL_ENG}
        self.sems = {}

    def rec(self, eng, fn, waits=(), sig=False, dma=False, tag=""):
        w2 = []
        my_sem = ENG_SEMS.get(eng)
        lw = self.last_wait[eng]

        def add(s, v):
            if lw.get(s, -1) >= v:
                return
            lw[s] = v
            w2.append((s, v))

        for ev in waits:
            if ev is None:
                continue
            s, v = ev
            if s in POP_SEMS or s in SP_SEMS:
                group = POP_SEMS if s in POP_SEMS else SP_SEMS
                for s2 in group:
                    if self.cnt[s2]:
                        add(s2, self.cnt[s2])
                continue
            if s == my_sem and not RACE_STRICT:
                continue
            add(s, v)
        ev = None
        inc = 16 if dma else (1 if sig else 0)
        if inc:
            if dma and eng == "pop":
                s = POP_SEMS[self.pop_rot]
                self.pop_rot = (self.pop_rot + 1) % len(POP_SEMS)
            elif dma and eng == "sp":
                s = SP_SEMS[self.sp_rot]
                self.sp_rot = (self.sp_rot + 1) % len(SP_SEMS)
            else:
                s = DMA_SEMS[eng] if dma else ENG_SEMS[eng]
            if dma:
                add(s, self.cnt[s])
            self.cnt[s] += inc
            ev = (s, self.cnt[s])
        self.items[eng].append((w2, fn, ev, inc))
        return ev

    def emit(self, block):
        sems = self.sems

        def runner(eng):
            def run(e):
                for (waits, fn, ev, inc) in self.items[eng]:
                    for (s, v) in waits:
                        e.wait_ge(sems[s], v)
                    ins = fn(e)
                    if inc:
                        ins.then_inc(sems[ev[0]], inc)
            return run
        block.tensor(runner("pe"))
        block.vector(runner("dve"))
        block.scalar(runner("act"))
        block.sync(runner("sp"))
        block.gpsimd(runner("pop"))



def host_constants():
    c = {}
    c["iden"] = np.eye(128, dtype=NBF)
    c["iden32"] = np.eye(128, dtype=np.float32)
    c["ones"] = np.ones((1, 128), np.float32)
    k1v = np.arange(33)
    ang = -2.0 * np.pi * np.outer(np.arange(32), k1v) / 64.0
    d1 = np.zeros((32, 66), np.float32)
    d1[:, 0:33] = np.cos(ang)
    d1[:, 33:66] = np.sin(ang)
    c["fft_d1"] = np.tile(d1, (4, 1)).astype(NBF)
    a = np.arange(64)
    k2 = np.arange(64)
    tcbd = np.zeros((33, 128, 128), np.float32)
    tsbd = np.zeros((33, 128, 128), np.float32)
    mtsbd = np.zeros((33, 128, 128), np.float32)
    for q in range(33):
        t = np.exp(-2j * np.pi * a * q / 4096.0)[:, None] * \
            np.exp(-2j * np.pi * np.outer(a, k2) / 64.0)
        for chb in range(2):
            rows = chb * 64 + a
            cols = k2 * 2 + chb
            tcbd[q][np.ix_(rows, cols)] = t.real
            tsbd[q][np.ix_(rows, cols)] = t.imag
            mtsbd[q][np.ix_(rows, cols)] = -t.imag
    c["fft_tc"] = tcbd.astype(NBF)
    c["fft_ts"] = tsbd.astype(NBF)
    c["fft_mts"] = mtsbd.astype(NBF)
    b = np.arange(64)
    ecb = np.zeros((33, 128, 128), np.float32)
    esb = np.zeros((33, 128, 128), np.float32)
    mesb = np.zeros((33, 128, 128), np.float32)
    for q in range(33):
        t = np.exp(2j * np.pi * np.outer(k2, b) / 64.0) * \
            np.exp(2j * np.pi * b * q / 4096.0)[None, :]
        for chb in range(2):
            rows = k2 * 2 + chb
            cols = b * 2 + chb
            ecb[q][np.ix_(rows, cols)] = t.real
            esb[q][np.ix_(rows, cols)] = t.imag
            mesb[q][np.ix_(rows, cols)] = -t.imag
    c["fft_ec"] = ecb.astype(NBF)
    c["fft_es"] = esb.astype(NBF)
    c["fft_mes"] = mesb.astype(NBF)
    J = np.arange(32)
    f = np.full(33, 2.0)
    f[0] = 1.0
    f[32] = 1.0
    vrvi = np.zeros((66, 32), np.float32)
    vrvi[0:33] = f[:, None] * np.cos(2 * np.pi * np.outer(k1v, J) / 64.0) / 4096.0
    vrvi[33:66] = -f[:, None] * np.sin(2 * np.pi * np.outer(k1v, J) / 64.0) / 4096.0
    c["fft_v"] = vrvi.astype(NBF)
    c["tlin"] = np.linspace(0.0, 1.0, L).astype(np.float32).reshape(1, L)
    c["encoff"] = np.array([0, 0, 0, .25, .25, .25],
                           np.float32).reshape(6, 1)
    return c



class KB:
    def __init__(self, nc, P, subset):
        self.nc, self.P, self.subset = nc, P, subset
        self.t, self.dram = {}, {}
        self.out_evs = []
        self.scl_cursor = 0
        self.bias_cursor = 0
        self.pt_free = [None, None]
        self.pa_free = [None] * 6
        self.wprep = {}
        self.busy = {}

    def dbg(self, name, ap, waits=(), pieces=1):
        if name not in DEBUG_DUMPS:
            return None
        shp = list(ap.shape)
        t = self.nc.dram_tensor("dbg_" + name, shp, ap.dtype,
                                kind="ExternalOutput").ap()
        ev = self.dma("pop", t, ap, waits=waits, tag="dbg")
        self.out_evs.append(ev)
        return ev

    def load_bias(self, src_ap, bc0, n):
        T = self.t
        stg = T["ROWS"][0:n, 128:256]
        pa = T["PA"][5]
        e1 = self.dma("sp", stg, src_ap.rearrange("(c p) -> c p", p=128),
                      waits=[self.bz("rows_stg")])
        e2 = self.P.rec(
            "pe", lambda e: e.transpose(pa[0:128, 0:n], stg,
                                        T["iden32"][0:n, 0:n]),
            waits=[e1, self.ev_iden32, self.pa_free[5]], sig=True)
        e3 = self.cp("dve", T["BIAS"][:, bc0:bc0 + n], pa[0:128, 0:n],
                     waits=[e2], sig=True)
        self.pa_free[5] = e3
        self.setbz("rows_stg", e3)
        return e3

    def bz(self, key):
        return self.busy.get(key)

    def setbz(self, key, ev):
        self.busy[key] = ev

    def dma(self, eng, dst, src, waits=(), tag="", slow=False):
        kw = {"allow_slow_non_contiguous": True} if slow else {}
        return self.P.rec(eng, lambda e: e.dma_start(out=dst, in_=src, **kw),
                          waits=waits, dma=True, tag=tag)

    def mm(self, out, lhsT, rhs, start, stop, waits=(), sig=False, tp=None, tag=""):
        kw = {"tile_position": tp} if tp is not None else {}
        return self.P.rec(
            "pe", lambda e: e.matmul(out, lhsT, rhs, start=start, stop=stop, **kw),
            waits=waits, sig=sig, tag=tag)

    def tr(self, out_psum, in_ap, waits=(), sig=False, tag=""):
        iden = self.t["iden_sb"]
        p = in_ap.shape[0]
        return self.P.rec(
            "pe", lambda e: e.transpose(out_psum, in_ap, iden[0:p, 0:p]),
            waits=waits, sig=sig, tag=tag)

    def cp(self, eng, out, src, waits=(), sig=False, tag=""):
        if eng == "act":
            return self.P.rec(
                "act", lambda e: e.activation(out, src, AF.Copy),
                waits=waits, sig=sig, tag=tag)
        return self.P.rec(eng, lambda e: e.tensor_copy(out, src),
                          waits=waits, sig=sig, tag=tag)

    def tt(self, out, a, b, op, waits=(), sig=False, tag=""):
        return self.P.rec("dve", lambda e: e.tensor_tensor(out, a, b, op),
                          waits=waits, sig=sig, tag=tag)

    def ts1(self, out, a, s1, op0, waits=(), sig=False, eng="dve", tag=""):
        return self.P.rec(eng, lambda e: e.tensor_single_scalar(out, a, s1, op0),
                          waits=waits, sig=sig, tag=tag)

    def ts2(self, out, a, s1, s2, op0, op1, waits=(), sig=False, accum=None, tag=""):
        kw = {"accum_out": accum} if accum is not None else {}
        return self.P.rec(
            "dve", lambda e: e.tensor_scalar(out, a, s1, s2, op0, op1, **kw),
            waits=waits, sig=sig, tag=tag)

    def stt(self, out, in0, scalar, in1, op0, op1, waits=(), sig=False,
            accum=None, tag=""):
        kw = {"accum_out": accum} if accum is not None else {}
        return self.P.rec(
            "dve", lambda e: e.scalar_tensor_tensor(out, in0, scalar, in1,
                                                    op0, op1, **kw),
            waits=waits, sig=sig, tag=tag)

    def act(self, out, a, func, bias=0.0, scale=1.0, waits=(), sig=False,
            accum=None, tag=""):
        kw = {"accum_out": accum} if accum is not None else {}
        return self.P.rec(
            "act", lambda e: e.activation(out, a, func, bias=bias, scale=scale,
                                          **kw),
            waits=waits, sig=sig, tag=tag)

    def red(self, out, a, op=ALU.add, absval=False, waits=(), sig=False, tag=""):
        return self.P.rec(
            "dve", lambda e: e.tensor_reduce(out, a, AX.X, op,
                                             apply_absolute_value=absval),
            waits=waits, sig=sig, tag=tag)

    def recip(self, out, a, waits=(), sig=True):
        return self.P.rec("dve", lambda e: e.reciprocal(out, a), waits=waits,
                          sig=sig)

    EVAC_ENG = {0: "dve", 1: "act", 2: "dve", 3: "act", 4: "dve", 5: "act"}

    def ecp(self, bank, out, src, waits=(), sig=True, tag=""):
        return self.cp(self.EVAC_ENG[bank], out, src, waits=waits, sig=sig,
                       tag=tag)

    def ets1b(self, bank, out, src, bias_ap, waits=(), sig=True, tag=""):
        if self.EVAC_ENG[bank] == "dve":
            return self.ts1(out, src, bias_ap, ALU.add, waits=waits, sig=sig,
                            tag=tag)
        return self.act(out, src, AF.Identity, bias=bias_ap, waits=waits,
                        sig=sig, tag=tag)

    def bcast_val(self, src11, dst_col, waits=(), tag="bc"):
        pa = self.t["PA"][5]
        ev = self.P.rec(
            "pe", lambda e: e.matmul(pa[:, 2:3], self.t["ones_sb"][0:1, :],
                                     src11, start=True, stop=True),
            waits=list(waits) + [self.pa_free[5],
                                 getattr(self, "ev_ones", None)],
            sig=True, tag=tag)
        ev2 = self.cp("dve", dst_col, pa[:, 2:3], waits=[ev], sig=True, tag=tag)
        self.pa_free[5] = ev2
        return ev2



INPUT_SPECS = [
    ("L0_attn_log_Lam_real", (NST,)), ("L0_attn_Lam_imag", (NST,)),
    ("L0_attn_B_real", (NST, DM)), ("L0_attn_B_imag", (NST, DM)),
    ("L0_attn_C_real", (DM, NST)), ("L0_attn_C_imag", (DM, NST)),
    ("L0_attn_D", (DM,)), ("L0_attn_log_dt", (1,)),
    ("L0_attn_W_out", (DM, DM)), ("L0_attn_b_out", (DM,)),
    ("L0_ffn_W_fc", (DF, DM)), ("L0_ffn_b_fc", (DF,)),
    ("L0_ffn_A_fc", (16, DM)), ("L0_ffn_B_fc", (DF, 16)),
    ("L0_ffn_W_proj", (DM, DF)), ("L0_ffn_b_proj", (DM,)),
    ("L0_ffn_A_proj", (16, DF)), ("L0_ffn_B_proj", (DM, 16)),
    ("L1_attn_in_proj_w", (DM, 3 * DM)), ("L1_attn_in_proj_b", (3 * DM,)),
    ("L1_attn_out_proj_w", (DM, DM)), ("L1_attn_out_proj_b", (DM,)),
    ("L1_attn_freqs", (2, 3)),
    ("L1_attn_mlp_w1", (2, 7, 64)), ("L1_attn_mlp_b1", (2, 64)),
    ("L1_attn_mlp_w2", (2, 64, 64)), ("L1_attn_mlp_b2", (2, 64)),
    ("L1_attn_mlp_w3", (2, 64, DM)), ("L1_attn_mlp_b3", (2, DM)),
    ("L1_attn_decay", (2, DM)), ("L1_attn_conv_w", (2, 3, 1, DM)),
    ("L1_attn_conv_b", (2, DM)), ("L1_attn_gate_w", (2, DM, DM)),
    ("L1_attn_gate_b", (2, DM)),
    ("L1_ffn_W_fc", (DF, DM)), ("L1_ffn_b_fc", (DF,)),
    ("L1_ffn_A_fc", (16, DM)), ("L1_ffn_B_fc", (DF, 16)),
    ("L1_ffn_W_proj", (DM, DF)), ("L1_ffn_b_proj", (DM,)),
    ("L1_ffn_A_proj", (16, DF)), ("L1_ffn_B_proj", (DM, 16)),
]

ARN = 77824

XFR = 0
VFR = 16384
GR = 32768
CR = 49152
WD0 = 73728


def build_program(subset=SUB_ALL):
    nc = bass.Bass("TRN2", target_bir_lowering=False, debug=False)
    C = host_constants()
    P = Prog(nc)
    kb = KB(nc, P, subset)
    kb.C = C
    dram = kb.dram

    def din(name, shape, dtype=F32):
        dram[name] = nc.dram_tensor(name, list(shape), dtype,
                                    kind="ExternalInput").ap()

    def dint(name, shape, dtype=BF16):
        dram[name] = nc.dram_tensor(name, list(shape), dtype, kind="Internal").ap()

    din("x", (NSEQ, L, DM))
    for nm, shp in INPUT_SPECS:
        din(nm, shp)
    for nm, arr in C.items():
        din("c_" + nm, arr.shape, BF16 if arr.dtype == NBF else F32)
    dram["out"] = nc.dram_tensor("out", [NSEQ, L, DM], F32,
                                 kind="ExternalOutput").ap()

    dint("wst_fc", (2, NDC, 128, DF))
    dint("wst_pj", (2, NDC, 32, 128, 128))
    dint("wst_wout", (NDC, 128, DM))
    dint("wst_scan", (NST, 2, 128, 128))
    dint("ctrl1_sp", (NDC, 128, L))
    dint("dscr_c", (64, 32))
    dint("ddiag", (2, 64, 1024))
    dint("hspec", (2, NDC, 128, 33 * 64 * 2))

    stack = contextlib.ExitStack()
    with stack:
        T = kb.t

        def sb(name, shape, dtype):
            T[name] = stack.enter_context(nc.sbuf_tensor(name, list(shape), dtype))

        sb("R", [128, NTIL * DM], BF16)
        sb("ARENA", [128, ARN], BF16)
        sb("XTT", [128, 2 * DM], BF16)
        sb("TMP32", [128, 2 * DM], F32)
        sb("STAT", [128, 160], F32)
        sb("SCL", [128, 32], F32)
        sb("ROWS", [128, 256], F32)
        sb("COL1", [128, 1], F32)
        sb("iden_sb", [128, 128], BF16)
        sb("iden32", [128, 128], F32)
        sb("ones_sb", [1, 128], F32)
        sb("BT_sb", [128, NDC * 128], BF16)
        sb("CT_sb", [128, DM], BF16)
        sb("QPR", [64, 128], BF16)
        sb("QPI", [64, 128], BF16)
        sb("QPMI", [64, 128], BF16)
        sb("Q128", [64, 4], F32)
        sb("CCR", [64, NCH], F32)
        sb("CCI", [64, NCH], F32)
        sb("CRWR", [64, NCH], F32)
        sb("CRWI", [64, NCH], F32)
        sb("CCB", [64, 32], BF16)
        sb("BIAS", [128, 256], F32)
        sb("D1_sb", [128, 66], BF16)
        sb("VRVI", [66, 32], BF16)
        T["PA"] = [stack.enter_context(nc.psum_tensor(f"PA{i}", [128, 512], F32))
                   for i in range(6)]
        T["PT"] = [stack.enter_context(nc.psum_tensor(f"PT{i}", [128, 1024], BF16))
                   for i in range(2)]

        for s in (list(ENG_SEMS.values()) + list(DMA_SEMS.values())
                  + list(POP_SEMS[1:]) + list(SP_SEMS[1:])):
            P.sems[s] = stack.enter_context(nc.semaphore(s))

        with nc.Block() as block:
            build_all(kb)
            P.emit(block)
    return nc



def build_all(kb):
    T, dram = kb.t, kb.dram
    kb.ev_iden = kb.dma("sp", T["iden_sb"][:], dram["c_iden"])
    kb.ev_iden32 = kb.dma("sp", T["iden32"][:],
                          dram["c_iden"].bitcast(BF16) if False else dram["c_iden32"])
    kb.ev_ones = kb.dma("sp", T["ones_sb"][:], dram["c_ones"])
    kb.ev_d1 = kb.dma("sp", T["D1_sb"][:], dram["c_fft_d1"])
    kb.ev_vrvi = kb.dma("sp", T["VRVI"][:], dram["c_fft_v"])
    c0 = kb.bias_cursor
    kb.bias_cursor += 2
    m0 = kb.P.rec("dve", lambda e: e.memset(T["BIAS"][:, c0:c0 + 1],
                                            RNE_MAGIC), sig=True)
    m1 = kb.P.rec("dve", lambda e: e.memset(T["BIAS"][:, c0 + 1:c0 + 2],
                                            -RNE_MAGIC), sig=True)
    kb.magic_ap = (T["BIAS"][:, c0:c0 + 1], T["BIAS"][:, c0 + 1:c0 + 2])
    kb.ev_magic = [m0, m1]

    if "l0ffn" in kb.subset:
        prep_ffn_weights(kb, 0)
    if "l1ffn" in kb.subset:
        prep_ffn_weights(kb, 1)
    if "l0attn" in kb.subset:
        prep_l0_attn(kb)
    if "l1attn" in kb.subset:
        prep_l1_attn(kb)

    for s in range(NSEQ):
        build_seq(kb, s)
    kb.P.rec("pop", lambda e: e.nop(), waits=kb.out_evs)


def build_seq(kb, s):
    T, dram = kb.t, kb.dram
    R = T["R"]
    src = dram["x"][s].rearrange("(j p) d -> p j d", p=128)
    ev_r = kb.dma("pop", R[:].rearrange("p (j d) -> p j d", d=DM), src,
                  waits=getattr(kb, "r_free", []), tag=f"ldx{s}")
    kb.r_tile_ready = [ev_r] * NTIL
    stats_from_scratch(kb, ev_r)
    if s == 0:
        kb.dbg("stat0", kb.t["STAT"][:, 48:80], waits=[kb.stats_ready,
                                                        kb.stats_ready_d])

    if "l0attn" in kb.subset:
        sub_l0attn(kb, s)
    if "l0ffn" in kb.subset:
        sub_ffn(kb, s, 0)
    if "l1attn" in kb.subset:
        sub_l1attn(kb, s)
    if "l1ffn" in kb.subset:
        sub_ffn(kb, s, 1)

    dst = dram["out"][s].rearrange("(j p) d -> p j d", p=128)
    evs = [e for e in kb.r_tile_ready if e is not None]
    ev_o = kb.dma("pop", dst, R[:].rearrange("p (j d) -> p j d", d=DM),
                  waits=evs + [kb.stats_ready], tag=f"sto{s}")
    kb.out_evs.append(ev_o)
    kb.r_free = [ev_o]



def stats_from_scratch(kb, ev_r):
    T = kb.t
    R, STAT = T["R"], T["STAT"]
    SX, SQ = STAT[:, 16:32], STAT[:, 32:48]
    e1 = e2 = None
    for j in range(NTIL):
        tile = R[:, j * DM:(j + 1) * DM]
        e1 = kb.red(SX[:, j:j + 1], tile, ALU.add,
                    waits=[ev_r, kb.bz("statw")], sig=True)
        e2 = kb.act(T["TMP32"][:, DM:2 * DM], tile, AF.Square,
                    waits=[ev_r, e2, kb.bz("statw")],
                    accum=SQ[:, j:j + 1], sig=True)
    finish_stats(kb, [e1, e2])


def finish_stats(kb, evs):
    T = kb.t
    STAT = T["STAT"]
    SX, SQ = STAT[:, 16:32], STAT[:, 32:48]
    MU, RS = STAT[:, 48:64], STAT[:, 64:80]
    V1, V2 = STAT[:, 80:96], STAT[:, 96:112]
    w0 = list(evs) + [kb.bz("lnread")]
    ea = kb.ts1(MU, SX, 1.0 / DM, ALU.mult, waits=w0, sig=True)
    eb = kb.ts1(V1, SQ, 1.0 / DM, ALU.mult, waits=w0 + [ea], sig=True)
    ec = kb.tt(V2, MU, MU, ALU.mult, waits=[ea, eb], sig=True)
    ed = kb.tt(V1, V1, V2, ALU.subtract, waits=[ec], sig=True)
    e5 = kb.ts1(V2, V1, EPS, ALU.add, waits=[ed], sig=True)
    e6 = kb.recip(V1, V2, waits=[e5])
    e7 = kb.act(RS, V1, AF.Sqrt, waits=[e6], sig=True)
    kb.stats_ready = e7
    kb.stats_ready_d = e6
    kb.MU, kb.RS = MU, RS
    kb.setbz("statw", e7)


def ln_tile(kb, j, dst, waits=()):
    T = kb.t
    ev = kb.ts2(dst, T["R"][:, j * DM:(j + 1) * DM], kb.MU[:, j:j + 1],
                kb.RS[:, j:j + 1], ALU.subtract, ALU.mult,
                waits=list(waits) + [kb.stats_ready, kb.r_tile_ready[j]],
                sig=True, tag=f"ln{j}")
    kb.setbz("lnread", ev)
    return ev


def make_xf(kb, dst_region, waits=()):
    T = kb.t
    XF = T["ARENA"][:, dst_region:dst_region + 16384]
    PT = T["PT"]
    evs2 = [None]
    for j in range(NTIL):
        stg = T["XTT"][:, (j % 2) * DM:(j % 2 + 1) * DM]
        ev_ln = ln_tile(kb, j, stg,
                        waits=list(waits) + [kb.xtt_busy(j % 2)])
        bank = j % 2
        last = None
        for k in range(8):
            last = kb.tr(PT[bank][:, k * 128:(k + 1) * 128],
                         stg[:, k * 128:(k + 1) * 128],
                         waits=[ev_ln, kb.pt_free[bank],
                                getattr(kb, "pt_free_act", None), kb.ev_iden]
                         if k == 0 else [],
                         sig=(k == 7), tag=f"mx{j}")
        dst = XF.rearrange("p (k c) -> p k c", c=2048)[
            :, :, j * 128:(j + 1) * 128]
        ev = kb.cp("dve" if bank == 0 else "act", dst, PT[bank][:, 0:1024],
                   waits=[last, kb.bz("xf_w")], sig=True, tag=f"mxe{j}")
        kb.pt_free[bank] = ev
        kb.set_xtt_busy(j % 2, ev)
        evs2.append(ev)
    return XF, evs2[-2:]


def _xtt_busy(kb, i):
    return getattr(kb, "_xtt", {}).get(i)


def _set_xtt_busy(kb, i, ev):
    if not hasattr(kb, "_xtt"):
        kb._xtt = {}
    kb._xtt[i] = ev


KB.xtt_busy = _xtt_busy
KB.set_xtt_busy = _set_xtt_busy


def post_sub_tile(kb, j, piece, waits=(), tag="ps"):
    T = kb.t
    R, STAT, PT = T["R"], T["STAT"], T["PT"]
    bank = j % 2
    stg = T["XTT"][:, (j % 2) * DM:(j % 2 + 1) * DM]
    last = None
    for k in range(8):
        w = list(waits) + [kb.pt_free[bank], kb.ev_iden] if k == 0 else []
        last = kb.tr(PT[bank][:, k * 128:(k + 1) * 128], piece(k), waits=w,
                     sig=(k == 7), tag=f"{tag}T{j}")
    ev_e = kb.cp("dve", stg, PT[bank][:, 0:1024],
                 waits=[last, kb.xtt_busy(j % 2)], sig=True)
    kb.pt_free[bank] = ev_e
    kb.set_xtt_busy(j % 2, ev_e)
    SYX, SYQ = STAT[:, 144:145], STAT[:, 145:146]
    ps = kb.bz("poststat")
    e1 = kb.red(SYX, stg, ALU.add, waits=[ev_e, ps], sig=True)
    e2 = kb.act(T["TMP32"][:, DM:2 * DM], stg, AF.Square,
                waits=[ev_e, ps, kb.bz("sqdum")], accum=SYQ, sig=True)
    kb.setbz("sqdum", e2)
    MYU, MYR = STAT[:, 146:147], STAT[:, 147:148]
    A1, A2 = STAT[:, 148:149], STAT[:, 149:150]
    ea = kb.ts1(MYU, SYX, 1.0 / DM, ALU.mult, waits=[e1, ps], sig=True)
    eb = kb.ts1(A1, SYQ, 1.0 / DM, ALU.mult, waits=[e2, ps], sig=True)
    ec = kb.tt(A2, MYU, MYU, ALU.mult, waits=[ea], sig=True)
    ed = kb.tt(A1, A1, A2, ALU.subtract, waits=[eb, ec], sig=True)
    e5 = kb.ts1(A2, A1, EPS, ALU.add, waits=[ed], sig=True)
    e6 = kb.recip(A1, A2, waits=[e5])
    e7 = kb.act(MYR, A1, AF.Sqrt, waits=[e6, ps], sig=True)
    kb.setbz("poststat", e7)
    et = kb.ts2(T["TMP32"][:, 0:DM], stg, MYU, MYR, ALU.subtract, ALU.mult,
                waits=[e7, kb.bz("tmplo")], sig=True)
    kb.setbz("tmplo", et)
    kb.set_xtt_busy(j % 2, et)
    SX, SQ = STAT[:, 16:32], STAT[:, 32:48]
    er = kb.stt(R[:, j * DM:(j + 1) * DM], T["TMP32"][:, 0:DM], 1.0,
                R[:, j * DM:(j + 1) * DM], ALU.mult, ALU.add,
                waits=[et, kb.r_tile_ready[j], kb.stats_ready],
                accum=SX[:, j:j + 1], sig=True)
    kb.setbz("tmplo", er)
    eq = kb.act(T["TMP32"][:, DM:2 * DM], R[:, j * DM:(j + 1) * DM], AF.Square,
                waits=[er, kb.bz("sqdum"), kb.stats_ready], accum=SQ[:, j:j + 1],
                sig=True)
    kb.setbz("sqdum", eq)
    kb.r_tile_ready[j] = er
    return er, eq



def prep_ffn_weights(kb, li):
    lay = f"L{li}"
    for which in ("fc", "proj"):
        prep_one_ffn_w(kb, li, lay, which)
    for fcp, n in (("fc", 32), ("proj", 8)):
        bname = f"{lay}_ffn_b_{fcp}"
        bc0 = kb.bias_cursor
        kb.bias_cursor += n
        evb = kb.load_bias(kb.dram[bname], bc0, n)
        kb.wprep[bname] = (evb, bc0)


def prep_one_ffn_w(kb, li, lay, which):
    P, T, dram = kb.P, kb.t, kb.dram
    fcp = "fc" if which == "fc" else "proj"
    W, A, B = (dram[f"{lay}_ffn_W_{fcp}"], dram[f"{lay}_ffn_A_{fcp}"],
               dram[f"{lay}_ffn_B_{fcp}"])
    rows, cols = (DF, DM) if which == "fc" else (DM, DF)
    nrc = rows // 128
    STAT, SCL, AR, PA, PT = T["STAT"], T["SCL"], T["ARENA"], T["PA"], T["PT"]
    ARF = AR[:].bitcast(F32)
    WLD = ARF[:, CR // 2:CR // 2 + 4096]
    WI32 = AR[:].bitcast(I32)[:, (CR + 8192) // 2:(CR + 8192) // 2 + 1024]
    bt_sb = AR[:, CR + 10240:CR + 14336]
    a_sb = AR[:, CR + 14336:CR + 18432]
    qtmp = AR[:, WD0:WD0 + 4096]
    qstg = AR[:, CR + 18432:CR + 19456]
    scol = kb.scl_cursor
    kb.scl_cursor += 2

    run, tmpc = STAT[:, 0:1], STAT[:, 1:2]
    ev_mx = getattr(kb, "wld_free", None)
    for rc in range(nrc):
        dst = WLD[:, 0:cols]
        ev = kb.dma("sp", dst, W[rc * 128:(rc + 1) * 128, :], waits=[ev_mx],
                    tag=f"q1{li}{which}")
        if rc == 0:
            ev_mx = kb.red(run, dst, ALU.max, absval=True, waits=[ev], sig=True)
        else:
            e1 = kb.red(tmpc, dst, ALU.max, absval=True, waits=[ev], sig=True)
            ev_mx = kb.tt(run, run, tmpc, ALU.max, waits=[e1], sig=True)
    ev_cp = kb.cp("dve", T["COL1"][:, 0:1], run, waits=[ev_mx], sig=True)
    ev_c = kb.dma("pop", T["ROWS"][0:1, 0:128], T["COL1"][:, 0:1], waits=[ev_cp])
    ev_m2 = kb.red(T["ROWS"][0:1, 128:129], T["ROWS"][0:1, 0:128], ALU.max,
                   waits=[ev_c], sig=True)
    ev_s = kb.ts1(T["ROWS"][0:1, 129:130], T["ROWS"][0:1, 128:129], 1.0 / 127.0,
                  ALU.mult, waits=[ev_m2], sig=True)
    ev_i = kb.recip(T["ROWS"][0:1, 130:131], T["ROWS"][0:1, 129:130],
                    waits=[ev_s])
    ev_bs = kb.bcast_val(T["ROWS"][0:1, 129:130], SCL[:, scol:scol + 1],
                         waits=[ev_s])
    ev_bi = kb.bcast_val(T["ROWS"][0:1, 130:131], SCL[:, scol + 1:scol + 2],
                         waits=[ev_i])
    kb.setbz("rows_low", ev_bi)

    prev = None
    evs_bt = []
    for rc in range(nrc):
        stg = T["XTT"][:, 0:16]
        ev = kb.dma("pop", stg, B[rc * 128:(rc + 1) * 128, :], waits=[prev])
        evt = kb.tr(PT[0][0:16, 0:128], stg, waits=[ev, kb.ev_iden,
                                                    kb.pt_free[0]], sig=True)
        prev = kb.cp("dve", bt_sb[0:16, rc * 128:(rc + 1) * 128],
                     PT[0][0:16, 0:128], waits=[evt], sig=True)
        kb.pt_free[0] = prev
        evs_bt.append(prev)
    ev_a = kb.dma("pop", a_sb[0:16, 0:cols], A)
    ev_a2 = kb.ts1(a_sb[0:16, 0:cols], a_sb[0:16, 0:cols], 2.0, ALU.mult,
                   waits=[ev_a], sig=True)

    stash_fc = dram["wst_fc"]
    stash_pj = dram["wst_pj"]
    ev_wfree = ev_mx
    for rc in range(nrc):
        wld = WLD[:, 0:cols]
        ev_l = kb.dma("sp", wld, W[rc * 128:(rc + 1) * 128, :],
                      waits=[ev_wfree], tag=f"q2{li}{which}")
        ev_sl = []
        for g in range(cols // DM):
            Ta = T["TMP32"][:, 0:DM]
            Tb = T["TMP32"][:, DM:2 * DM]
            e1 = kb.ts1(Ta, wld[:, g * DM:(g + 1) * DM],
                        SCL[:, scol + 1:scol + 2], ALU.mult,
                        waits=[ev_l, ev_bi] + ev_sl[-1:] +
                        [kb.bz("sqdum"), kb.bz("tmplo")], sig=True)
            eb_ = kb.act(Ta, Ta, AF.Identity, bias=kb.magic_ap[0],
                         waits=[e1] + kb.ev_magic, sig=True)
            e2 = kb.act(Ta, Ta, AF.Identity, bias=kb.magic_ap[1],
                        waits=[eb_], sig=True)
            kb.setbz("sqdum", e2)
            kb.setbz("tmplo", e2)
            for h in range(2):
                sl = g * 2 + h
                pa = PA[sl % 2]
                evm = kb.mm(pa[:, 0:512], bt_sb[0:16, rc * 128:(rc + 1) * 128],
                            a_sb[0:16, sl * 512:(sl + 1) * 512], True, True,
                            waits=[evs_bt[rc], ev_a2, kb.pa_free[sl % 2]],
                            sig=True)
                eo = kb.stt(qtmp[:, sl * 512:(sl + 1) * 512],
                            T["TMP32"][:, h * 512:(h + 1) * 512],
                            SCL[:, scol:scol + 1], pa[:, 0:512],
                            ALU.mult, ALU.add,
                            waits=[e2, evm, ev_bs,
                                   getattr(kb, "qtmp_busy", None)],
                            sig=True)
                kb.pa_free[sl % 2] = eo
                ev_sl.append(eo)
        ev_wfree = ev_sl[-1]
        nct = cols // 128
        for g0 in range(0, nct, 8):
            gn = min(8, nct - g0)
            bank = (g0 // 8) % 2
            last = None
            for jj in range(gn):
                i = g0 + jj
                last = kb.tr(PT[bank][:, jj * 128:(jj + 1) * 128],
                             qtmp[:, i * 128:(i + 1) * 128],
                             waits=([ev_sl[-1], kb.pt_free[bank], kb.ev_iden]
                                    if jj == 0 else []),
                             sig=(jj == gn - 1))
            ev_e = kb.cp("dve", qstg[:, 0:gn * 128], PT[bank][:, 0:gn * 128],
                         waits=[last, getattr(kb, "qstg_busy", None)], sig=True)
            kb.pt_free[bank] = ev_e
            if which == "fc":
                dst = stash_fc[li, g0:g0 + gn, :, rc * 128:(rc + 1) * 128]
                dst = dst.transpose([1, 0, 2])
                ev_d = kb.dma("pop", dst,
                              qstg[:, 0:gn * 128].rearrange("p (g c) -> p g c",
                                                            g=gn),
                              waits=[ev_e])
            else:
                dst = stash_pj[li, rc, g0:g0 + gn, :, :].transpose([1, 0, 2])
                ev_d = kb.dma("pop", dst,
                              qstg[:, 0:gn * 128].rearrange("p (g c) -> p g c",
                                                            g=gn),
                              waits=[ev_e])
            kb.qstg_busy = ev_d
        kb.qtmp_busy = ev_sl[-1]
    kb.wld_free = ev_wfree
    kb.wprep[f"{lay}_{which}"] = kb.qstg_busy



def sub_ffn(kb, s, li):
    T, dram = kb.t, kb.dram
    lay = f"L{li}"
    AR, PA, PT = T["ARENA"], T["PA"], T["PT"]
    FCW = AR[:, 0:32768]
    PJW = AR[:, 32768:65536]
    HB = AR[:, 65536:73728]
    XS = AR[:, 73728:75776]
    YS = AR[:, 75776:77824]
    ev_fc = kb.wprep[f"{lay}_fc"]
    ev_pj = kb.wprep[f"{lay}_proj"]
    bfc = kb.wprep[f"{lay}_ffn_b_fc"]
    bpj = kb.wprep[f"{lay}_ffn_b_proj"]

    wl = []
    free = getattr(kb, "ffnw_free", []) + [kb.bz("sub_exit")]
    for dc in range(NDC):
        wl.append(kb.dma("sp", FCW[:, dc * 4096:(dc + 1) * 4096],
                         dram["wst_fc"][li, dc], waits=[ev_fc] + free,
                         tag=f"fcw{dc}"))
    for d2 in range(NDC):
        wl.append(kb.dma("sp",
                         PJW[:, d2 * 4096:(d2 + 1) * 4096].rearrange(
                             "p (f c) -> p f c", f=32),
                         dram["wst_pj"][li, d2].transpose([1, 0, 2]),
                         waits=[ev_pj] + free, tag=f"pjw{d2}"))
    ev_w = wl
    if FFN_STAGE <= 1:
        kb.ffnw_free = list(wl)
        return
    if s == 0 and li == 0:
        kb.dbg("fcw0", FCW[:, 0:32768], waits=list(ev_w[:NDC]))
        kb.dbg("pjw0", PJW[:, 0:4096], waits=[ev_w[NDC]])

    for tt in range(8):
        ev_x = []
        for half in range(2):
            j = tt * 2 + half
            stg = T["XTT"][:, (j % 2) * DM:(j % 2 + 1) * DM]
            ev_ln = ln_tile(kb, j, stg, waits=[kb.xtt_busy(j % 2)])
            if FFN_STAGE == 16:
                kb.set_xtt_busy(j % 2, ev_ln)
                ev_x.append(ev_ln)
                continue
            bank = j % 2
            last = None
            for k in range(8):
                last = kb.tr(PT[bank][:, k * 128:(k + 1) * 128],
                             stg[:, k * 128:(k + 1) * 128],
                             waits=[ev_ln, kb.pt_free[bank],
                                    getattr(kb, "pt_free_act", None),
                                    kb.ev_iden]
                             if k == 0 else [], sig=(k == 7), tag=f"fx{j}")
            dst = XS.rearrange("p (k c) -> p k c", c=256)[
                :, :, half * 128:half * 128 + 128]
            e = kb.cp("dve" if bank == 0 else "act", dst, PT[bank][:, 0:1024],
                      waits=[last, getattr(kb, "xs_busy", None),
                             kb.bz("xs_dbg")], sig=True)
            kb.pt_free[bank] = e
            kb.set_xtt_busy(j % 2, e)
            ev_x.append(e)
        if s == 0 and li == 0 and tt == 0:
            ed = kb.dbg("xs0", XS[:, 0:2048], waits=ev_x)
            if ed:
                kb.setbz("xs_dbg", ed)
        if FFN_STAGE in (16, 17):
            continue
        if FFN_STAGE == 15:
            kb.xs_busy = ev_x[-2]
            continue
        if not hasattr(kb, "fc_slot_free"):
            kb.fc_slot_free = [None] * 4
        ev_h = [None] * 32
        for fc_ in range(32):
            slot = fc_ % 4
            pa = PA[slot]
            po = 0
            evm = None
            for dc in range(NDC):
                w = [ev_w[dc]] + ([kb.fc_slot_free[slot]] + ev_x if dc == 0 else [])
                evm = kb.mm(pa[:, po:po + 256],
                            FCW[:, dc * 4096 + fc_ * 128:
                                dc * 4096 + (fc_ + 1) * 128],
                            XS[:, dc * 256:(dc + 1) * 256],
                            dc == 0, dc == NDC - 1, waits=w,
                            sig=(dc == NDC - 1), tag=f"fc{tt}.{fc_}")
            eb = kb.act(HB[:, fc_ * 256:(fc_ + 1) * 256], pa[:, po:po + 256],
                        AF.Gelu, bias=T["BIAS"][:, bfc[1] + fc_:bfc[1] + fc_ + 1],
                        waits=[evm, bfc[0]], sig=True, tag=f"gl{tt}.{fc_}")
            kb.fc_slot_free[slot] = eb
            ev_h[fc_] = eb
            if fc_ == 31:
                kb.xs_busy = evm
                last_fc_mm = evm
        if s == 0 and li == 0 and tt == 0:
            ed = kb.dbg("hb0", HB[:, 0:8192], waits=list(ev_h))
            if ed:
                for q in range(4):
                    kb.fc_slot_free[q] = ed
        if FFN_STAGE <= 2:
            continue
        ev_y = []
        for d2 in range(NDC):
            pa = PA[4 + (d2 % 2)]
            evm = None
            for fc_ in range(32):
                w = [ev_w[NDC + d2], ev_h[fc_]]
                if fc_ == 0:
                    w.append(kb.pa_free[4 + (d2 % 2)])
                evm = kb.mm(pa[:, 0:256],
                            PJW[:, d2 * 4096 + fc_ * 128:
                                d2 * 4096 + (fc_ + 1) * 128],
                            HB[:, fc_ * 256:(fc_ + 1) * 256], fc_ == 0,
                            fc_ == 31, waits=w, sig=(fc_ == 31),
                            tag=f"pj{tt}.{d2}")
            ey = kb.ets1b(4 + (d2 % 2), YS[:, d2 * 256:(d2 + 1) * 256],
                          pa[:, 0:256],
                          T["BIAS"][:, bpj[1] + d2:bpj[1] + d2 + 1],
                          waits=[evm, bpj[0], getattr(kb, "ys_busy", None),
                                 kb.bz("ys_dbg")],
                          sig=True, tag=f"pje{tt}.{d2}")
            kb.pa_free[4 + (d2 % 2)] = ey
            ev_y.append(ey)
            if d2 == NDC - 1:
                last_pj_mm = evm
        if s == 0 and li == 0 and tt == 0:
            ed = kb.dbg("ys0", YS[:, 0:2048], waits=list(ev_y))
            if ed:
                kb.setbz("ys_dbg", ed)
        if FFN_STAGE <= 3:
            continue
        for half in range(2):
            j = tt * 2 + half

            def piece(k, half=half):
                return YS[:, k * 256 + half * 128: k * 256 + half * 128 + 128]
            er, eq = post_sub_tile(kb, j, piece, waits=ev_y, tag=f"fp{tt}")
            kb.ys_busy = er
    if FFN_STAGE >= 4 and FFN_STAGE not in (15, 16, 17):
        finish_stats(kb, [eq])
        kb.setbz("sub_exit", er)
    if FFN_STAGE in (15, 16, 17):
        kb.ffnw_free = list(wl)
    elif FFN_STAGE == 2:
        kb.ffnw_free = [last_fc_mm]
    else:
        kb.ffnw_free = [last_fc_mm, last_pj_mm]



def prep_l0_attn(kb):
    P, T, dram = kb.P, kb.t, kb.dram
    AR, PA, PT = T["ARENA"], T["PA"], T["PT"]
    ARF = AR[:].bitcast(F32)
    ROWS, SCL = T["ROWS"], T["SCL"]

    e_dt = kb.dma("sp", ROWS[0:1, 0:1],
                  dram["L0_attn_log_dt"].rearrange("(a b) -> a b", a=1),
                  waits=[kb.bz("rows_low")])
    e_dt2 = kb.act(ROWS[0:1, 1:2], ROWS[0:1, 0:1], AF.Exp, waits=[e_dt], sig=True)
    dtc = kb.scl_cursor
    kb.scl_cursor += 1
    ev_dtb = kb.bcast_val(ROWS[0:1, 1:2], SCL[:, dtc:dtc + 1], waits=[e_dt2])
    e_l1 = kb.dma("sp", ROWS[0:64, 2:3],
                  dram["L0_attn_log_Lam_real"].rearrange("(n o) -> n o", o=1),
                  waits=[kb.bz("rows_low")])
    e_l2 = kb.dma("sp", ROWS[0:64, 3:4],
                  dram["L0_attn_Lam_imag"].rearrange("(n o) -> n o", o=1),
                  waits=[kb.bz("rows_low")])
    e_m = kb.act(ROWS[0:64, 4:5], ROWS[0:64, 2:3], AF.Exp, waits=[e_l1], sig=True)
    kb.ev_lnm = kb.ts2(ROWS[0:64, 4:5], ROWS[0:64, 4:5],
                       SCL[0:64, dtc:dtc + 1], -1.0, ALU.mult, ALU.mult,
                       waits=[e_m, ev_dtb], sig=True)
    kb.ev_th2 = kb.ts2(ROWS[0:64, 5:6], ROWS[0:64, 3:4],
                       SCL[0:64, dtc:dtc + 1], 1.0 / (2.0 * np.pi),
                       ALU.mult, ALU.mult, waits=[e_l2, ev_dtb], sig=True)

    WLD = ARF[:, CR // 2:CR // 2 + 4096]
    stg_bf = AR[:, WD0:WD0 + 1024]
    ev_w = getattr(kb, "wld_free", None)
    evc = None
    for comp, nm in ((0, "L0_attn_B_real"), (1, "L0_attn_B_imag")):
        e1 = kb.dma("sp", WLD[0:64, 0:1024], dram[nm], waits=[ev_w])
        e2 = kb.ts1(stg_bf[0:64, 0:1024], WLD[0:64, 0:1024],
                    SCL[0:64, dtc:dtc + 1], ALU.mult,
                    waits=[e1, ev_dtb, kb.bz("bstg")], sig=True)
        evt = None
        for dc in range(NDC):
            evt = kb.tr(PT[0][:, dc * 64:dc * 64 + 64],
                        stg_bf[0:64, dc * 128:(dc + 1) * 128],
                        waits=([e2, kb.pt_free[0], kb.ev_iden] if dc == 0 else []),
                        sig=(dc == NDC - 1))
        evc = kb.cp("dve",
                    T["BT_sb"].rearrange("p (c w) -> p c w", w=128)[
                        :, :, comp * 64:comp * 64 + 64],
                    PT[0][:, 0:512].rearrange("p (c w) -> p c w", w=64),
                    waits=[evt], sig=True)
        kb.pt_free[0] = evc
        kb.setbz("bstg", evc)
        ev_w = e2
    kb.ev_bt = evc

    for comp, nm in ((0, "L0_attn_C_real"), (1, "L0_attn_C_imag")):
        dstld = WLD[:, 0:512].rearrange("p (hc n) -> p hc n", n=64)
        srcld = dram[nm].rearrange("(hc p) n -> p hc n", p=128)
        e1 = kb.dma("sp", dstld, srcld, waits=[ev_w])
        sgn = 1.0 if comp == 0 else -1.0
        e2 = kb.ts1(stg_bf[:, 0:512], WLD[:, 0:512], sgn, ALU.mult,
                    waits=[e1, kb.bz("bstg")], sig=True)
        evt = None
        for hc in range(8):
            evt = kb.tr(PT[0][0:64, hc * 128:hc * 128 + 128],
                        stg_bf[:, hc * 64:hc * 64 + 64],
                        waits=([e2, kb.pt_free[0]] if hc == 0 else []),
                        sig=(hc == 7))
        evc = kb.cp("dve", T["CT_sb"][comp * 64:comp * 64 + 64, :],
                    PT[0][0:64, 0:1024], waits=[evt], sig=True)
        kb.pt_free[0] = evc
        kb.setbz("bstg", e2)
        ev_w = e2
    kb.ev_ct = evc
    kb.wld_free = ev_w

    W = dram["L0_attn_W_out"]
    qtmp = AR[:, WD0:WD0 + 1024]
    qstg = AR[:, GR:GR + 1024]
    ev_d = None
    for rc in range(8):
        ev_l = kb.dma("sp", WLD[:, 0:1024], W[rc * 128:(rc + 1) * 128, :],
                      waits=[kb.wld_free])
        e2 = kb.cp("dve", qtmp[:, 0:1024], WLD[:, 0:1024],
                   waits=[ev_l, kb.bz("bstg")], sig=True)
        kb.wld_free = e2
        kb.setbz("bstg", e2)
        last = None
        for jj in range(8):
            last = kb.tr(PT[1][:, jj * 128:(jj + 1) * 128],
                         qtmp[:, jj * 128:(jj + 1) * 128],
                         waits=([e2, kb.pt_free[1]] if jj == 0 else []),
                         sig=(jj == 7))
        ev_e = kb.cp("dve", qstg[:, 0:1024], PT[1][:, 0:1024],
                     waits=[last, kb.bz("qstg2")], sig=True)
        kb.pt_free[1] = ev_e
        dst = dram["wst_wout"][0:8, :, rc * 128:(rc + 1) * 128].transpose(
            [1, 0, 2])
        ev_d = kb.dma("pop", dst,
                      qstg[:, 0:1024].rearrange("p (g c) -> p g c", g=8),
                      waits=[ev_e])
        kb.setbz("qstg2", ev_d)
    kb.ev_wout = ev_d

    kb.wprep["L0_b_out"] = (kb.load_bias(dram["L0_attn_b_out"],
                                         kb.bias_cursor, 8), kb.bias_cursor)
    kb.bias_cursor += 8
    kb.wprep["L0_D"] = (kb.load_bias(dram["L0_attn_D"], kb.bias_cursor, 8),
                        kb.bias_cursor)
    kb.bias_cursor += 8

    prep_scan_tables(kb)


def diag_ap(ddiag, c):
    return bass.AP(ddiag.tensor, c * 65536, [[1040, 64], [1, 16]])


def rr_sin(kb, dst, u, gi, scr, waits, n=2048):
    e1 = kb.ts1(scr[0:64, 0:n], u[0:64, 0:n], RNE_MAGIC, ALU.add,
                waits=waits, sig=True)
    e2 = kb.ts1(scr[0:64, 0:n], scr[0:64, 0:n], RNE_MAGIC, ALU.subtract,
                waits=[e1], sig=True)
    e3 = kb.tt(scr[0:64, 0:n], u[0:64, 0:n], scr[0:64, 0:n], ALU.subtract,
               waits=[e2], sig=True)
    eh = kb.ts1(scr[0:64, 0:n], scr[0:64, 0:n], 2.0 * np.pi, ALU.mult,
                waits=[e3], sig=True)
    return kb.act(dst[0:64, 0:n], scr[0:64, 0:n], AF.Sin, waits=[eh], sig=True)


def prep_scan_tables(kb):
    P, T, dram = kb.P, kb.t, kb.dram
    AR = T["ARENA"]
    ARF = AR[:].bitcast(F32)
    ROWS = T["ROWS"]
    lnm = ROWS[0:64, 4:5]
    th2 = ROWS[0:64, 5:6]
    base = CR // 2
    G_d = ARF[:, base:base + 2048]
    G_p = ARF[:, base + 2048:base + 4096]
    G_u = ARF[:, base + 4096:base + 6144]
    G_i = AR[:].bitcast(I32)[:, base + 6144:base + 8192]
    G_e = ARF[:, base + 8192:base + 10240]
    G_s = ARF[:, base + 10240:base + 12288]
    G_c = ARF[:, base + 12288:base + 14336]
    MRB = AR[:, GR + 1024:GR + 3072]
    MIB = AR[:, GR + 3072:GR + 5120]

    w0 = [kb.wld_free, kb.bz("bstg"), kb.bz("qstg2"), kb.ev_lnm, kb.ev_th2]
    d_i = None
    for ch in range(8):
        q0 = ch * 16
        e_d = P.rec("pop", lambda e, q0=q0: e.iota(
            G_d[0:64, 0:2048], [[-1, 16], [1, 128]], base=-q0,
            channel_multiplier=0, allow_small_or_imprecise_dtypes=True),
            waits=w0 + [kb.bz("gd")], sig=True)
        e_p = kb.ts2(G_p[0:64, :], G_d[0:64, :], 0.0, -1e30, ALU.is_lt,
                     ALU.mult, waits=[e_d, kb.bz("gp")], sig=True)
        e_a = kb.stt(G_e[0:64, :], G_d[0:64, :], lnm, G_p[0:64, :], ALU.mult,
                     ALU.add, waits=[e_p, kb.bz("ge")], sig=True)
        e_E = kb.act(G_e[0:64, :], G_e[0:64, :], AF.Exp, waits=[e_a], sig=True)
        e_u = kb.ts1(G_u[0:64, :], G_d[0:64, :], th2, ALU.mult,
                     waits=[e_d, kb.bz("gu")], sig=True)
        e_s = rr_sin(kb, G_s, G_u, G_i, G_p, [e_u, e_p])
        e_u2 = kb.ts1(G_u[0:64, :], G_u[0:64, :], 0.25, ALU.add, waits=[e_s],
                      sig=True)
        e_c = rr_sin(kb, G_c, G_u, G_i, G_p, [e_u2])
        kb.setbz("gu", e_c)
        kb.setbz("gp", e_c)
        e_mr = kb.tt(MRB[0:64, 0:2048], G_e[0:64, :], G_c[0:64, :], ALU.mult,
                     waits=[e_E, e_c, kb.bz("mrb")], sig=True)
        e_mi = kb.tt(MIB[0:64, 0:2048], G_e[0:64, :], G_s[0:64, :], ALU.mult,
                     waits=[e_E, e_s, kb.bz("mib")], sig=True)
        kb.setbz("ge", e_mi)
        kb.setbz("gd", e_mi)
        d_r = kb.dma("pop", dram["wst_scan"][:, 0, q0:q0 + 16, :],
            MRB[0:64, 0:2048].rearrange("n (q p) -> n q p", p=128),
            waits=[e_mr])
        d_i = kb.dma("pop", dram["wst_scan"][:, 1, q0:q0 + 16, :],
            MIB[0:64, 0:2048].rearrange("n (q p) -> n q p", p=128),
            waits=[e_mi])
        kb.setbz("mrb", d_r)
        kb.setbz("mib", d_i)
        w0 = []
    kb.ev_scan_st = d_i

    e_d = P.rec("pop", lambda e: e.iota(
        G_d[0:64, 0:128], [[1, 128]], base=1, channel_multiplier=0,
        allow_small_or_imprecise_dtypes=True),
        waits=[kb.bz("gd"), kb.bz("gu"), kb.bz("gp"), kb.bz("ge")], sig=True)
    e_a = kb.ts1(G_e[0:64, 0:128], G_d[0:64, 0:128], lnm, ALU.mult,
                 waits=[e_d], sig=True)
    e_E = kb.act(G_e[0:64, 0:128], G_e[0:64, 0:128], AF.Exp, waits=[e_a],
                 sig=True)
    e_u = kb.ts1(G_u[0:64, 0:128], G_d[0:64, 0:128], th2, ALU.mult,
                 waits=[e_d], sig=True)
    e_s = rr_sin(kb, G_s, G_u, G_i, G_p, [e_u], n=128)
    e_u2 = kb.ts1(G_u[0:64, 0:128], G_u[0:64, 0:128], 0.25, ALU.add,
                  waits=[e_s], sig=True)
    e_c = rr_sin(kb, G_c, G_u, G_i, G_p, [e_u2], n=128)
    eqr = kb.tt(T["QPR"][:, 0:128], G_e[0:64, 0:128], G_c[0:64, 0:128],
                ALU.mult, waits=[e_E, e_c], sig=True)
    eqi = kb.tt(T["QPI"][:, 0:128], G_e[0:64, 0:128], G_s[0:64, 0:128],
                ALU.mult, waits=[e_E, e_s], sig=True)
    eqm = kb.ts1(T["QPMI"][:, 0:128], T["QPI"][:, 0:128], -1.0, ALU.mult,
                 waits=[eqi], sig=True)
    eq1 = kb.cp("dve", T["Q128"][:, 0:1], T["QPR"][0:64, 127:128],
                waits=[eqr], sig=True)
    eq2 = kb.cp("dve", T["Q128"][:, 1:2], T["QPI"][0:64, 127:128],
                waits=[eqi], sig=True)
    eq3 = kb.ts1(T["Q128"][:, 3:4], T["Q128"][:, 1:2], -1.0, ALU.mult,
                 waits=[eq2], sig=True)
    kb.ev_qp = [eqr, eqi, eqm, eq1, eq2, eq3]
    kb.setbz("cr_free", eq3)
    kb.setbz("cr_free2", eqm)


def sub_l0attn(kb, s):
    T, dram = kb.t, kb.dram
    AR, PA, PT = T["ARENA"], T["PA"], T["PT"]
    XF, ev_xf = make_xf(kb, XFR, waits=[kb.bz("xf_w")])
    BUS = AR[:, VFR:VFR + 2048]
    BUT = AR[:, VFR + 2048:VFR + 5120]
    SSB = AR[:, VFR + 5120:VFR + 7168]
    STb = AR[:, VFR + 7168:VFR + 9216]
    YMIX = AR[:, GR:GR + 16384]
    WOUTB = AR[:, CR:CR + 8192]
    MD = AR[:, CR + 8192:CR + 12288]
    bD = kb.wprep["L0_D"]
    bO = kb.wprep["L0_b_out"]

    ev_wo = []
    for dc in range(NDC):
        ev_wo.append(kb.dma("sp", WOUTB[:, dc * 1024:(dc + 1) * 1024],
                            dram["wst_wout"][dc],
                            waits=[kb.ev_wout, kb.bz("woutb"),
                                   kb.bz("cr_free"), kb.bz("cr_free2"),
                                   kb.bz("sub_exit")],
                            tag=f"wo{dc}"))

    ev_bus = []
    for tsl in range(4):
        pa = PA[4 + tsl % 2]
        evm = None
        for dc in range(NDC):
            w = [kb.ev_bt]
            if dc == 0:
                w += [kb.pa_free[4 + tsl % 2]] + list(ev_xf)
            evm = kb.mm(pa[:, 0:512], T["BT_sb"][:, dc * 128:(dc + 1) * 128],
                        XF[:, dc * 2048 + tsl * 512: dc * 2048 + tsl * 512 + 512],
                        dc == 0, dc == NDC - 1, waits=w, sig=(dc == NDC - 1),
                        tag=f"bu{tsl}")
        eb = kb.ecp(4 + tsl % 2, BUS[:, tsl * 512:(tsl + 1) * 512],
                    pa[:, 0:512], waits=[evm, kb.bz("bus")], sig=True)
        kb.pa_free[4 + tsl % 2] = eb
        ev_bus.append(eb)

    ev_bt2 = []
    for g0 in range(0, NCH, 8):
        bank = (g0 // 8) % 2
        last = None
        for jj in range(8):
            j = g0 + jj
            last = kb.tr(PT[bank][:, jj * 128:(jj + 1) * 128],
                         BUS[:, j * 128:(j + 1) * 128],
                         waits=(ev_bus + [kb.pt_free[bank], kb.ev_iden])
                         if jj == 0 else [], sig=(jj == 7))
        dst = BUT.rearrange("p (c w) -> p c w", w=192)[:, g0:g0 + 8, 0:128]
        e = kb.ecp(bank, dst, PT[bank][:, 0:1024],
                   waits=[last, kb.bz("but")], sig=True)
        kb.pt_free[bank] = e
        ev_bt2.append(e)
    butv = BUT.rearrange("p (c w) -> p c w", w=192)
    ev_bneg = kb.ts1(butv[:, :, 128:192], butv[:, :, 64:128], -1.0, ALU.mult,
                     waits=list(ev_bt2), sig=True)
    kb.setbz("but", ev_bneg)
    kb.setbz("bus", ev_bt2[-1])

    ev_md = [None] * NST

    def md_load(n, extra=()):
        s = n % 16
        dst = MD[:, s * 256:(s + 1) * 256].rearrange("p (c k) -> p c k",
                                                     k=128)
        w0 = [kb.ev_scan_st] + list(extra)
        if n == 0:
            w0 += [kb.bz("cr_free"), kb.bz("cr_free2"), kb.bz("sub_exit")]
        ev_md[n] = kb.dma("sp", dst, dram["wst_scan"][n].transpose([1, 0, 2]),
                          waits=w0, tag=f"md{n}")
    for n in range(16):
        md_load(n)
    bank_last = [None] * 4
    for n in range(NST):
        s = n % 16
        md0 = MD[:, s * 256:s * 256 + 128]
        md1 = MD[:, s * 256 + 128:s * 256 + 256]
        b = n // 16
        pa = PA[b]
        off = (n * 32) % 512
        butr = butv[:, :, n:n + 1].rearrange("p c o -> p (c o)")
        buti = butv[:, :, 64 + n:65 + n].rearrange("p c o -> p (c o)")
        butm = butv[:, :, 128 + n:129 + n].rearrange("p c o -> p (c o)")
        w0 = [ev_md[n], ev_bneg]
        if n % 16 == 0:
            w0 += [kb.pa_free[b],
                   kb.fc_slot_free[b] if hasattr(kb, "fc_slot_free") else None]
        kb.mm(pa[:, off:off + 16], md0, butr, True, False, waits=w0,
              tag=f"sc{n}")
        kb.mm(pa[:, off:off + 16], md1, butm, False, True, waits=[],
              tag=f"sc{n}b")
        kb.mm(pa[:, off + 16:off + 32], md0, buti, True, False, waits=[],
              tag=f"sc{n}c")
        e4 = kb.mm(pa[:, off + 16:off + 32], md1, butr, False, True, waits=[],
                   sig=True, tag=f"sc{n}d")
        if n + 16 < NST:
            md_load(n + 16, extra=[e4])
        if n % 16 == 15:
            bank_last[b] = e4

    ev_cp = []
    for b in range(4):
        e = kb.cp("dve", T["TMP32"][96:128, b * 512:(b + 1) * 512],
                  PA[b][96:128, 0:512],
                  waits=[bank_last[b], kb.bz("tmplo"), kb.bz("sqdum")],
                  sig=True)
        ev_cp.append(e)
    kb.setbz("tmplo", ev_cp[-1])
    kb.setbz("sqdum", ev_cp[-1])
    srcv = T["TMP32"][127:128, 0:2048].rearrange("o (n c j) -> o n c j",
                                                 c=2, j=16)
    e_cr = kb.dma("pop", T["CRWR"][:, 0:16].unsqueeze(0) if False else
                  T["CRWR"][:, 0:16], srcv[0:1, :, 0, :], waits=ev_cp)
    e_ci = kb.dma("pop", T["CRWI"][:, 0:16], srcv[0:1, :, 1, :], waits=ev_cp)

    Q = T["Q128"]
    em1 = kb.P.rec("dve", lambda e: e.memset(T["CCR"][:, 0:1], 0.0),
                   waits=[e_cr, kb.bz("ccr")], sig=True)
    em2 = kb.P.rec("dve", lambda e: e.memset(T["CCI"][:, 0:1], 0.0),
                   waits=[e_ci, kb.bz("ccr")], sig=True)
    prev = [em1, em2] + kb.ev_qp
    for j in range(1, NCH):
        t1 = kb.stt(T["TMP32"][0:64, 0:1], T["CCR"][:, j - 1:j], Q[:, 0:1],
                    T["CRWR"][:, j - 1:j], ALU.mult, ALU.add, waits=prev,
                    sig=True)
        e1 = kb.stt(T["CCR"][:, j:j + 1], T["CCI"][:, j - 1:j], Q[:, 3:4],
                    T["TMP32"][0:64, 0:1], ALU.mult, ALU.add, waits=[t1],
                    sig=True)
        t2 = kb.stt(T["TMP32"][0:64, 1:2], T["CCR"][:, j - 1:j], Q[:, 1:2],
                    T["CRWI"][:, j - 1:j], ALU.mult, ALU.add, waits=[e1],
                    sig=True)
        e2 = kb.stt(T["CCI"][:, j:j + 1], T["CCI"][:, j - 1:j], Q[:, 0:1],
                    T["TMP32"][0:64, 1:2], ALU.mult, ALU.add, waits=[t2],
                    sig=True)
        prev = [e2]
    ecb1 = kb.cp("dve", T["CCB"][:, 0:16], T["CCR"][:, 0:16], waits=prev,
                 sig=True)
    ecb2 = kb.cp("dve", T["CCB"][:, 16:32], T["CCI"][:, 0:16], waits=[ecb1],
                 sig=True)
    kb.setbz("ccr", ecb2)

    DIAGS = AR[:, VFR + 9216:VFR + 11264]
    e_z = kb.P.rec("dve", lambda e: e.memset(DIAGS[0:64, 0:2048], 0.0),
                   waits=[kb.bz("diags"), kb.bz("sub_exit")], sig=True)
    e_d1 = kb.dma("pop", dram["dscr_c"], T["CCB"][:, 0:32], waits=[ecb2])
    e_d2 = kb.dma("pop", dram["ddiag"][0], DIAGS[0:64, 0:1024], waits=[e_z])
    e_d3 = kb.dma("pop", dram["ddiag"][1], DIAGS[0:64, 1024:2048], waits=[e_z])
    e_w1 = kb.P.rec("pop", lambda e: e.dma_start(
        out=diag_ap(dram["ddiag"], 0), in_=dram["dscr_c"][:, 0:16]),
        waits=[e_d1, e_d2], dma=True)
    e_w2 = kb.P.rec("pop", lambda e: e.dma_start(
        out=diag_ap(dram["ddiag"], 1), in_=dram["dscr_c"][:, 16:32]),
        waits=[e_d1, e_d3], dma=True)
    e_l1 = kb.dma("pop", DIAGS[0:64, 0:1024], dram["ddiag"][0], waits=[e_w1])
    e_l2 = kb.dma("pop", DIAGS[0:64, 1024:2048], dram["ddiag"][1], waits=[e_w2])
    kb.setbz("diags", None)

    ev_ssb = []
    e4 = None
    for b in range(4):
        pa_c = PA[4 + b % 2]
        dcr = DIAGS[0:64, 256 * b:256 * b + 256]
        dci = DIAGS[0:64, 1024 + 256 * b:1024 + 256 * b + 256]
        outr, outi = pa_c[:, 0:256], pa_c[:, 256:512]
        w0 = [e_l1, e_l2, kb.ev_qp[0], kb.ev_qp[1], kb.ev_qp[2],
              kb.pa_free[4 + b % 2]]
        kb.mm(outr, T["QPR"][0:64, :], dcr, True, False, waits=w0,
              tag=f"cyr{b}")
        kb.mm(outi, T["QPI"][0:64, :], dcr, False, False, waits=[],
              tag=f"cyi{b}")
        kb.mm(outr, T["QPMI"][0:64, :], dci, False, False, waits=[],
              tag=f"cyr{b}b")
        e4 = kb.mm(outi, T["QPR"][0:64, :], dci, False, True, waits=[],
                   sig=True, tag=f"cyi{b}b")
        dst3 = SSB.rearrange("p (ch co nn) -> p nn co ch", co=2, nn=64)[
            :, 16 * b:16 * b + 16, :, :]
        e0 = kb.cp("dve", dst3, PA[b][:, 0:512].rearrange(
            "p (nn co ch) -> p nn co ch", nn=16, co=2),
            waits=[bank_last[b], kb.bz("ssb")], sig=True)
        e = kb.tt(dst3, dst3,
                  pa_c[:, 0:512].rearrange("p (co nn ch) -> p nn co ch",
                                           nn=16, co=2),
                  ALU.add, waits=[e0, e4], sig=True)
        ev_ssb.append(e)
        kb.pa_free[b] = e
        kb.pa_free[4 + b % 2] = e
        if hasattr(kb, "fc_slot_free"):
            kb.fc_slot_free[b] = e
    kb.setbz("ssb", ev_ssb[-1])
    kb.setbz("diags", e4)

    ev_st = []
    for g0 in range(0, NCH, 8):
        bank = (g0 // 8) % 2
        last = None
        for jj in range(8):
            j = g0 + jj
            last = kb.tr(PT[bank][:, jj * 128:(jj + 1) * 128],
                         SSB[:, j * 128:(j + 1) * 128],
                         waits=(ev_ssb + [kb.pt_free[bank]]) if jj == 0 else [],
                         sig=(jj == 7))
        e = kb.ecp(bank, STb[:, g0 * 128:(g0 + 8) * 128],
                   PT[bank][:, 0:1024], waits=[last, kb.bz("st")],
                   sig=True)
        kb.pt_free[bank] = e
        ev_st.append(e)
    kb.setbz("st", ev_st[-1])

    ev_ymix = []
    for hc in range(NDC):
        for tsl in range(4):
            g = hc * 4 + tsl
            pb = PA[4 + g % 2]
            evm = kb.mm(pb[:, 0:512], T["CT_sb"][:, hc * 128:(hc + 1) * 128],
                        STb[:, tsl * 512:(tsl + 1) * 512], True, True,
                        waits=[kb.ev_ct, kb.pa_free[4 + g % 2]] + ev_st,
                        sig=True, tag=f"cp{hc}.{tsl}")
            ey = kb.stt(YMIX[:, hc * 2048 + tsl * 512:
                             hc * 2048 + tsl * 512 + 512],
                        XF[:, hc * 2048 + tsl * 512:
                           hc * 2048 + tsl * 512 + 512],
                        T["BIAS"][:, bD[1] + hc:bD[1] + hc + 1], pb[:, 0:512],
                        ALU.mult, ALU.add,
                        waits=[evm, bD[0], kb.bz("ymix")], sig=True,
                        tag=f"ym{hc}.{tsl}")
            kb.pa_free[4 + g % 2] = ey
            ev_ymix.append(ey)

    ev_yout = []
    for oc in range(NDC):
        for tsl in range(4):
            g = oc * 4 + tsl
            pa = PA[4 + g % 2]
            evm = None
            for dc in range(NDC):
                w = [ev_wo[dc], ev_ymix[dc * 4 + tsl]]
                if dc == 0:
                    w.append(kb.pa_free[4 + g % 2])
                evm = kb.mm(pa[:, 0:512],
                            WOUTB[:, dc * 1024 + oc * 128:
                                  dc * 1024 + (oc + 1) * 128],
                            YMIX[:, dc * 2048 + tsl * 512:
                                 dc * 2048 + tsl * 512 + 512],
                            dc == 0, dc == NDC - 1, waits=w,
                            sig=(dc == NDC - 1), tag=f"wm{oc}.{tsl}")
            ey = kb.ets1b(4 + g % 2, XF[:, oc * 2048 + tsl * 512:
                                        oc * 2048 + tsl * 512 + 512],
                          pa[:, 0:512],
                          T["BIAS"][:, bO[1] + oc:bO[1] + oc + 1],
                          waits=[evm, bO[0], ev_ymix[-1]], sig=True,
                          tag=f"yo{oc}.{tsl}")
            kb.pa_free[4 + g % 2] = ey
            ev_yout.append(ey)
    kb.setbz("ymix", ev_yout[-1])
    kb.setbz("woutb", ev_yout[-1])

    er = eq = None
    for j in range(NTIL):
        def piece(k, j=j):
            return XF[:, k * 2048 + j * 128: k * 2048 + (j + 1) * 128]
        er, eq = post_sub_tile(kb, j, piece, waits=ev_yout, tag="l0p")
        kb.setbz("xf_w", er)
    finish_stats(kb, [eq])
    kb.setbz("sub_exit", er)
    kb.ffnw_free = [er]


CV_VS = 0
CV_AS = 2048
CV_AT = 10240
CV_XS = 2048
CV_Y = 6272
CV_BS = 0
CV_YFR = 4224
CV_BT = 12416
CV_YCH = 12928
CV_H = 10496


def arena_ap(kb, col, dims, npart=128):
    return bass.AP(kb.t["ARENA"][:].tensor, col, [[ARN, npart]] + dims)


def conv_fwd_chunk(kb, VFc, XB, ev_v, tag):
    T = kb.t
    AR, PA, PT = T["ARENA"], T["PA"], T["PT"]
    VS = AR[:, XB + CV_VS:XB + CV_VS + 2048]
    AS = AR[:, XB + CV_AS:XB + CV_AS + 8192]
    ATb = AR[:, XB + CV_AT:XB + CV_AT + 4224]
    ev_vs = []
    for g0 in range(0, 16, 8):
        bank = (g0 // 8) % 2
        last = None
        for jj in range(8):
            g = g0 + jj
            last = kb.tr(PT[bank][:, jj * 128:(jj + 1) * 128],
                         VFc[:, g * 128:(g + 1) * 128],
                         waits=([kb.pt_free[bank], kb.ev_iden] + list(ev_v))
                         if jj == 0 else [], sig=(jj == 7), tag=f"{tag}it")
        e = kb.ecp(bank, VS[:, g0 * 128:(g0 + 8) * 128],
                   PT[bank][:, 0:1024],
                   waits=[last, kb.bz("cv_vs")], sig=True)
        kb.pt_free[bank] = e
        ev_vs.append(e)
    kb.setbz("cv_vs", ev_vs[-1])
    ev_as = []
    for d in range(4):
        for nsl in range(4):
            pa = PA[d]
            evm = kb.mm(pa[0:66, 0:512], T["D1_sb"][32 * d:32 * d + 32, 0:66],
                        VS[32 * d:32 * d + 32, nsl * 512:(nsl + 1) * 512],
                        True, True,
                        waits=ev_vs + [kb.ev_d1, kb.pa_free[d]], sig=True,
                        tp=(32 * d, 0), tag=f"{tag}s1")
            dst = arena_ap(kb, XB + CV_AS + 16 * nsl + d,
                           [[4, 4], [64, 128]], 66)
            e = kb.ecp(d, dst,
                       pa[0:66, 0:512].rearrange("p (gl ch) -> p gl ch", gl=4),
                       waits=[evm, kb.bz("cv_as")], sig=True)
            kb.pa_free[d] = e
            ev_as.append(e)
    kb.setbz("cv_as", ev_as[-1])
    ev_at = []
    for w0 in range(0, 64, 8):
        bank = (w0 // 8) % 2
        last = None
        for jj in range(8):
            w = w0 + jj
            last = kb.tr(PT[bank][:, jj * 128:jj * 128 + 66],
                         AS[0:66, w * 128:(w + 1) * 128],
                         waits=(ev_as + [kb.pt_free[bank]]) if jj == 0 else [],
                         sig=(jj == 7), tag=f"{tag}mt")
        e = kb.ecp(bank,
                   ATb.rearrange("p (w q) -> p w q", q=66)[:, w0:w0 + 8, :],
                   PT[bank][:, 0:1024].rearrange("p (w q) -> p w q",
                                                 q=128)[:, :, 0:66],
                   waits=[last, kb.bz("cv_at"), kb.bz("cv_h")], sig=True)
        kb.pt_free[bank] = e
        ev_at.append(e)
    kb.setbz("cv_at", ev_at[-1])
    atv = ATb.rearrange("p (w q) -> p w q", q=66)
    ev_xs = []
    for s in range(66):
        comp, q = s // 33, s % 33
        bk = (s // 8) % 2
        b = PA[bk]
        off = (s % 8) * 64
        first = (s % 8 == 0)
        movr = atv[:, 0:64, q]
        movi = atv[:, 0:64, 33 + q]
        if comp == 0:
            l1, l2 = kb.fwd_st["tc"], kb.fwd_st["mts"]
        else:
            l1, l2 = kb.fwd_st["ts"], kb.fwd_st["tc"]
        w0 = ev_at + kb.fwd_ev
        if first:
            w0.append(kb.pa_free[bk])
        kb.mm(b[:, off:off + 64], l1[:, q * 128:(q + 1) * 128], movr,
              first, False, waits=w0, tag=f"{tag}s2")
        em = kb.mm(b[:, off:off + 64], l2[:, q * 128:(q + 1) * 128], movi,
                   False, (s % 8 == 7) or (s == 65),
                   sig=(s % 8 == 7) or (s == 65), waits=[], tag=f"{tag}s2b")
        if (s % 8 == 7) or s == 65:
            n = (s % 8) + 1
            s0 = s - n + 1
            dst = arena_ap(kb, XB + CV_XS + s0, [[1, n], [66, 64]])
            e = kb.ecp(bk, dst,
                       b[:, 0:n * 64].rearrange("p (sl w) -> p sl w", w=64),
                       waits=[em, kb.bz("cv_xs")], sig=True)
            kb.pa_free[bk] = e
            kb.setbz("cv_xs", e)
            ev_xs.append(e)
    return ev_xs


def conv_bwd_chunk(kb, XB, hsrc, Gc, VFc, ev_xs, ev_g, tag):
    T = kb.t
    AR, PA, PT = T["ARENA"], T["PA"], T["PT"]
    XS = AR[:, XB + CV_XS:XB + CV_XS + 4224]
    Yb = AR[:, XB + CV_Y:XB + CV_Y + 4224]
    BS = AR[:, XB + CV_BS:XB + CV_BS + 4224]
    BT = AR[:, XB + CV_BT:XB + CV_BT + 528]
    YFR = AR[:, XB + CV_YFR:XB + CV_YFR + 8192]
    YCH = AR[:, XB + CV_YCH:XB + CV_YCH + 2048]
    Hb = AR[:, XB + CV_H:XB + CV_H + 4224]
    eh = kb.dma("sp", Hb[:, 0:2112], hsrc[:, 0:2112],
                waits=[kb.bz("cv_h")] + list(ev_xs), tag=f"{tag}h")
    eh2 = kb.dma("pop", Hb[:, 2112:4224], hsrc[:, 2112:4224],
                 waits=[kb.bz("cv_h")] + list(ev_xs), tag=f"{tag}h2")
    X3 = XS.rearrange("p (w q) -> p w q", q=66)
    Y3 = Yb.rearrange("p (w q) -> p w q", q=66)
    H3 = Hb.rearrange("p (w q) -> p w q", q=66)
    Xr, Xi = X3[:, :, 0:33], X3[:, :, 33:66]
    Yr, Yi = Y3[:, :, 0:33], Y3[:, :, 33:66]
    Hr, Hi = H3[:, :, 0:33], H3[:, :, 33:66]
    w0 = [eh, eh2] + list(ev_xs)
    e1 = kb.tt(Yr, Xr, Hr, ALU.mult, waits=w0 + [kb.bz("cv_y")], sig=True)
    e2 = kb.tt(Yi, Xr, Hi, ALU.mult, waits=[e1], sig=True)
    e3 = kb.tt(Xr, Xi, Hi, ALU.mult, waits=[e2], sig=True)
    e4 = kb.tt(Yr, Yr, Xr, ALU.subtract, waits=[e3], sig=True)
    e5 = kb.tt(Xr, Xi, Hr, ALU.mult, waits=[e4], sig=True)
    e6 = kb.tt(Yi, Yi, Xr, ALU.add, waits=[e5], sig=True)
    kb.setbz("cv_y", e6)
    kb.setbz("cv_h", e6)
    ev_y = [e4, e6]
    yvf = Yb.rearrange("p (w q) -> p w q", q=66)
    ev_bs = []
    for s in range(66):
        comp, q = s // 33, s % 33
        bk = 2 + (s // 8) % 2
        b = PA[bk]
        off = (s % 8) * 64
        first = (s % 8 == 0)
        movr = yvf[:, 0:64, q]
        movi = yvf[:, 0:64, 33 + q]
        if comp == 0:
            l1s = kb.i1_st["ec"][:, q * 128:(q + 1) * 128]
            l2s = i1_mes_slice(kb, q)
        else:
            l1s = kb.i1_st["es"][:, q * 128:(q + 1) * 128]
            l2s = kb.i1_st["ec"][:, q * 128:(q + 1) * 128]
        w0 = list(ev_y) + kb.i1_ev
        if first:
            w0.append(kb.pa_free[bk])
        kb.mm(b[:, off:off + 64], l1s, movr,
              first, False, waits=w0, tag=f"{tag}i1")
        em = kb.mm(b[:, off:off + 64], l2s, movi,
                   False, (s % 8 == 7) or (s == 65),
                   sig=(s % 8 == 7) or (s == 65), waits=[], tag=f"{tag}i1b")
        if (s % 8 == 7) or s == 65:
            n = (s % 8) + 1
            s0 = s - n + 1
            dst = arena_ap(kb, XB + CV_BS + s0, [[1, n], [66, 64]])
            e = kb.ecp(bk, dst,
                       b[:, 0:n * 64].rearrange("p (sl w) -> p sl w", w=64),
                       waits=[em, kb.bz("cv_bs")], sig=True)
            kb.pa_free[bk] = e
            kb.setbz("cv_bs", e)
            ev_bs.append(e)
    ev_fr = []
    for w0_ in range(0, 64, 4):
        bank = (w0_ // 4) % 2
        pb = PA[4 + (w0_ // 4) % 2]
        last = None
        for jj in range(4):
            w = w0_ + jj
            last = kb.tr(PT[bank][0:66, jj * 128:(jj + 1) * 128],
                         BS[:, w * 66:(w + 1) * 66],
                         waits=(ev_bs + [kb.pt_free[bank]]) if jj == 0 else [],
                         sig=(jj == 3), tag=f"{tag}t2")
        eb = kb.ecp(bank,
                    BT[0:66, 0:512].rearrange("p (j c) -> p j c", c=128),
                    PT[bank][0:66, 0:512].rearrange("p (j c) -> p j c", c=128),
                    waits=[last, kb.bz("cv_bt")], sig=True)
        kb.pt_free[bank] = eb
        evm = None
        for jj in range(4):
            first = (jj == 0)
            evm = kb.mm(pb[0:32, jj * 128:(jj + 1) * 128], T["VRVI"][0:66, :],
                        BT[0:66, jj * 128:(jj + 1) * 128], first, jj == 3,
                        waits=[eb, kb.ev_vrvi] +
                        ([kb.pa_free[4 + bank]] if first else []),
                        sig=(jj == 3), tag=f"{tag}i2")
        kb.setbz("cv_bt", evm)
        dst = bass.AP(T["ARENA"][:].tensor, XB + CV_YFR + w0_ * 2,
                      [[ARN, 32], [2, 4], [128, 64], [1, 2]])
        e = kb.ecp(4 + bank, dst,
                   pb[0:32, 0:512].rearrange("p (j b c) -> p j b c", b=64, c=2),
                   waits=[evm, kb.bz("cv_yfr")], sig=True)
        kb.pa_free[4 + bank] = e
        kb.setbz("cv_yfr", e)
        ev_fr.append(e)
    ev_ych = []
    for b0 in range(0, 64, 8):
        bank = (b0 // 8) % 2
        last = None
        for jj in range(8):
            bb = b0 + jj
            last = kb.tr(PT[bank][:, jj * 32:(jj + 1) * 32],
                         YFR[0:32, bb * 128:(bb + 1) * 128],
                         waits=(ev_fr + [kb.pt_free[bank]]) if jj == 0 else [],
                         sig=(jj == 7), tag=f"{tag}et")
        dst = bass.AP(T["ARENA"][:].tensor, XB + CV_YCH + b0,
                      [[ARN, 128], [1, 8], [64, 32]])
        e = kb.ecp(bank, dst,
                   PT[bank][:, 0:256].rearrange("p (j c) -> p j c", c=32),
                   waits=[last, kb.bz("cv_ych")], sig=True)
        kb.pt_free[bank] = e
        kb.setbz("cv_ych", e)
        ev_ych.append(e)
    dstv = VFc.rearrange("p (a c) -> p c a", c=32)
    ev = kb.tt(dstv, YCH.rearrange("p (c a) -> p c a", a=64),
               Gc.rearrange("p (c a) -> p c a", a=64), ALU.mult,
               waits=ev_ych + list(ev_g), sig=True, tag=f"{tag}gate")
    kb.setbz("cv_ych", ev)
    kb.setbz("cv_yfr", ev)
    kb.setbz("cv_vs", ev)
    kb.setbz("cv_as", ev)
    kb.setbz("cv_xs", ev)
    kb.setbz("cv_at", ev)
    return ev


def load_fwd_set(kb, waits=()):
    T, dram = kb.t, kb.dram
    AR = T["ARENA"]
    evs = []
    st = {}
    for k, nm in (("tc", "c_fft_tc"), ("ts", "c_fft_ts"), ("mts", "c_fft_mts")):
        base = CR + {"tc": 0, "ts": 4224, "mts": 8448}[k]
        ap = AR[:, base:base + 4224]
        e = kb.dma("sp", ap.rearrange("p (q c) -> p q c", c=128),
                   dram[nm].transpose([1, 0, 2]),
                   waits=list(waits) + [kb.bz("crw")], tag=f"fw{k}")
        st[k] = ap
        evs.append(e)
    kb.fwd_st, kb.fwd_ev = st, evs


def load_i1_set(kb, waits=()):
    T, dram = kb.t, kb.dram
    AR = T["ARENA"]
    evs = []
    for k, nm in (("ec", "c_fft_ec"), ("es", "c_fft_es")):
        base = CR + 12672 + {"ec": 0, "es": 4224}[k]
        ap = AR[:, base:base + 4224]
        e = kb.dma("sp", ap.rearrange("p (q c) -> p q c", c=128),
                   dram[nm].transpose([1, 0, 2]),
                   waits=list(waits) + [kb.bz("crw")], tag=f"i1{k}")
        if k == "ec":
            kb.i1_st = {"ec": ap}
        else:
            kb.i1_st["es"] = ap
        evs.append(e)
    kb.i1_mesA = AR[:, CR + 21120:CR + 24576]
    kb.i1_mesB = AR[:, WD0 + 1056:WD0 + 1824]
    e1 = kb.dma("sp", kb.i1_mesA.rearrange("p (q c) -> p q c", c=128),
                dram["c_fft_mes"][0:27].transpose([1, 0, 2]),
                waits=list(waits) + [kb.bz("crw")], tag="i1mesA")
    e2 = kb.dma("sp", kb.i1_mesB.rearrange("p (q c) -> p q c", c=128),
                dram["c_fft_mes"][27:33].transpose([1, 0, 2]),
                waits=list(waits) + [kb.bz("wd0t")], tag="i1mesB")
    kb.i1_ev = evs + [e1, e2]


def i1_mes_slice(kb, q):
    if q < 27:
        return kb.i1_mesA[:, q * 128:(q + 1) * 128]
    return kb.i1_mesB[:, (q - 27) * 128:(q - 27 + 1) * 128]


def gate_pipeline(kb, s, i, ev_ctrl, ev_prev, tag):
    T, dram = kb.t, kb.dram
    AR, PA = T["ARENA"], T["PA"]
    XC = AR[:, CR:CR + 16384]
    GW = AR[:, CR + 16384:CR + 24576]
    GRb = AR[:, GR:GR + 16384]
    bw = kb.wprep
    cb = bw[f"conv_b{i}"]
    gb = bw[f"gate_b{i}"]
    k0, k1_, k2_ = bw[f"cw{i}0"], bw[f"cw{i}1"], bw[f"cw{i}2"]
    ev_gw = []
    for cc in range(NDC):
        ev_gw.append(kb.dma("pop", GW[:, cc * 1024:(cc + 1) * 1024],
                            dram["L1_attn_gate_w"][i, cc * 128:(cc + 1) * 128, :],
                            waits=list(ev_prev) + [kb.bz("crw2")],
                            tag=f"{tag}gw"))
    ev_xc = []
    for cc in range(NDC):
        c0 = GRb[:, cc * 2048:(cc + 1) * 2048]
        x0 = XC[:, cc * 2048:(cc + 1) * 2048]
        w0 = list(ev_ctrl) + list(ev_prev) + \
            [cb[0], k0[0], k1_[0], k2_[0], kb.bz("crw")]
        e1 = kb.ts2(x0, c0, T["BIAS"][:, k1_[1] + cc:k1_[1] + cc + 1],
                    T["BIAS"][:, cb[1] + cc:cb[1] + cc + 1],
                    ALU.mult, ALU.add, waits=w0, sig=True, tag=f"{tag}x1")
        e2 = kb.stt(x0[:, 1:2048], c0[:, 0:2047],
                    T["BIAS"][:, k0[1] + cc:k0[1] + cc + 1], x0[:, 1:2048],
                    ALU.mult, ALU.add, waits=[e1], sig=True, tag=f"{tag}x2")
        e3 = kb.stt(x0[:, 0:2047], c0[:, 1:2048],
                    T["BIAS"][:, k2_[1] + cc:k2_[1] + cc + 1], x0[:, 0:2047],
                    ALU.mult, ALU.add, waits=[e2], sig=True, tag=f"{tag}x3")
        ev_xc.append(e3)
    ev_g = [None] * NDC
    for oc in range(NDC):
        for tsl in range(4):
            g = oc * 4 + tsl
            pa = PA[4 + g % 2]
            evm = None
            for cc in range(NDC):
                w0 = [ev_gw[cc], ev_xc[cc]]
                if cc == 0:
                    w0.append(kb.pa_free[4 + g % 2])
                evm = kb.mm(pa[:, 0:512],
                            GW[:, cc * 1024 + oc * 128:cc * 1024 + (oc + 1) * 128],
                            XC[:, cc * 2048 + tsl * 512:cc * 2048 + tsl * 512 + 512],
                            cc == 0, cc == NDC - 1, waits=w0,
                            sig=(cc == NDC - 1), tag=f"{tag}gm")
            es = kb.act(GRb[:, oc * 2048 + tsl * 512:oc * 2048 + tsl * 512 + 512],
                        pa[:, 0:512], AF.Sigmoid,
                        bias=T["BIAS"][:, gb[1] + oc:gb[1] + oc + 1],
                        waits=[evm, gb[0]] + ev_xc, sig=True, tag=f"{tag}sg")
            eg = kb.tt(GRb[:, oc * 2048 + tsl * 512:oc * 2048 + tsl * 512 + 512],
                       GRb[:, oc * 2048 + tsl * 512:oc * 2048 + tsl * 512 + 512],
                       XC[:, oc * 2048 + tsl * 512:oc * 2048 + tsl * 512 + 512],
                       ALU.mult, waits=[es], sig=True, tag=f"{tag}gx")
            kb.pa_free[4 + g % 2] = eg
            ev_g[oc] = eg
    kb.setbz("crw", ev_g[-1])
    kb.setbz("crw2", ev_g[-1])
    return ev_g


def rr_sin2(kb, dst, u, gi, scr, waits, n=2048, npart=64):
    e1 = kb.ts1(scr[0:npart, 0:n], u[0:npart, 0:n], RNE_MAGIC, ALU.add,
                waits=waits, sig=True)
    e2 = kb.ts1(scr[0:npart, 0:n], scr[0:npart, 0:n], RNE_MAGIC,
                ALU.subtract, waits=[e1], sig=True)
    e3 = kb.tt(scr[0:npart, 0:n], u[0:npart, 0:n], scr[0:npart, 0:n],
               ALU.subtract, waits=[e2], sig=True)
    eh = kb.ts1(scr[0:npart, 0:n], scr[0:npart, 0:n], 2.0 * np.pi, ALU.mult,
                waits=[e3], sig=True)
    return kb.act(dst[0:npart, 0:n], scr[0:npart, 0:n], AF.Sin, waits=[eh],
                  sig=True)


def prep_l1_attn(kb):
    P, T, dram = kb.P, kb.t, kb.dram
    AR, PA, PT = T["ARENA"], T["PA"], T["PT"]
    for nm, key, n in (("L1_attn_in_proj_b", "ipb", 24),
                       ("L1_attn_out_proj_b", "opb", 8)):
        kb.wprep[key] = (kb.load_bias(dram[nm], kb.bias_cursor, n),
                         kb.bias_cursor)
        kb.bias_cursor += n
    for i in range(2):
        for nm, key in (("L1_attn_gate_b", f"gate_b{i}"),
                        ("L1_attn_conv_b", f"conv_b{i}")):
            kb.wprep[key] = (kb.load_bias(dram[nm][i], kb.bias_cursor, 8),
                             kb.bias_cursor)
            kb.bias_cursor += 8
        for k in range(3):
            kb.wprep[f"cw{i}{k}"] = (
                kb.load_bias(dram["L1_attn_conv_w"][i, k, 0, :],
                             kb.bias_cursor, 8), kb.bias_cursor)
            kb.bias_cursor += 8
        bc = kb.bias_cursor
        ev = kb.load_bias(dram["L1_attn_decay"][i], bc, 8)
        ea = kb.act(T["BIAS"][:, bc:bc + 8], T["BIAS"][:, bc:bc + 8], AF.Abs,
                    waits=[ev], sig=True)
        en = kb.ts1(T["BIAS"][:, bc:bc + 8], T["BIAS"][:, bc:bc + 8],
                    -float(L), ALU.mult, waits=[ea], sig=True)
        kb.wprep[f"ndec{i}"] = (en, bc)
        kb.bias_cursor += 8

    load_fwd_set(kb, waits=[getattr(kb, "wld_free", None),
                            kb.bz("cr_free"), kb.bz("cr_free2")])
    w_gr = [kb.bz("mrb"), kb.bz("mib"), kb.bz("qstg2")]

    ev_tl = kb.dma("sp", T["TMP32"][0:1, 0:L], dram["c_tlin"],
                   waits=[kb.bz("tmplo"), kb.bz("sqdum")])
    TBC = AR[:].bitcast(F32)[:, GR // 2:GR // 2 + 2048]
    ev_tb = []
    for tsl in range(4):
        pa = PA[5]
        e = kb.mm(pa[:, 0:512], T["ones_sb"][0:1, :],
                  T["TMP32"][0:1, tsl * 512:(tsl + 1) * 512], True, True,
                  waits=[ev_tl, kb.ev_ones, kb.pa_free[5]], sig=True)
        e2 = kb.cp("dve", TBC[:, tsl * 512:(tsl + 1) * 512], pa[:, 0:512],
                   waits=[e] + (w_gr if tsl == 0 else []), sig=True)
        kb.pa_free[5] = e2
        ev_tb.append(e2)

    ARF = AR[:].bitcast(F32)
    VFF = ARF[:, VFR // 2:VFR // 2 + 2048]
    G_i = AR[:].bitcast(I32)[:, (VFR + 4096) // 2:(VFR + 4096) // 2 + 2048]
    G_s = ARF[:, (VFR + 8192) // 2:(VFR + 8192) // 2 + 2048]
    ENC = AR[:, VFR + 12288:VFR + 12288 + 2048]
    w1b = AR[:, VFR + 14336:VFR + 14336 + 64]
    w2b = AR[:, VFR + 14400:VFR + 14400 + 64]
    w3b = AR[:, VFR + 14464:VFR + 14464 + 1024]
    H3 = AR[:, GR + 4096:GR + 4096 + 2048]
    WIN = ARF[:, (GR + 6144) // 2:(GR + 6144) // 2 + 2048]
    HVF = AR[:, GR + 10240:GR + 10240 + 2048]
    H1 = AR[:, GR + 12288:GR + 12288 + 2048]
    H2b = AR[:, GR + 14336:GR + 14336 + 2048]
    e_st = None
    for i in range(2):
        evf = kb.dma("sp", T["ROWS"][0:3, 8:9],
                     dram["L1_attn_freqs"][i].rearrange("(n o) -> n o", o=1),
                     waits=[kb.bz("rows8")])
        evf2 = kb.dma("sp", T["ROWS"][3:6, 8:9],
                      dram["L1_attn_freqs"][i].rearrange("(n o) -> n o", o=1),
                      waits=[evf])
        evo = kb.dma("sp", T["ROWS"][0:6, 11:12], dram["c_encoff"],
                     waits=[kb.bz("rows8")])
        eu = kb.ts2(VFF[0:6, 0:2048], TBC[0:6, 0:2048], T["ROWS"][0:6, 8:9],
                    T["ROWS"][0:6, 11:12], ALU.mult, ALU.add,
                    waits=[evf, evf2, evo, kb.bz("h1")] + ev_tb +
                    [kb.bz("vff")], sig=True)
        kb.setbz("rows8", eu)
        es_ = rr_sin2(kb, ENC[0:6, :], VFF[0:6, 0:2048], G_i, G_s,
                      [eu, kb.bz("h1")], n=2048, npart=6)
        ec_ = es_
        kb.setbz("vff", es_)
        et_ = kb.dma("pop", ENC[6:7, 0:2048], dram["c_tlin"],
                     waits=[kb.bz("enct"), kb.bz("h1")], tag="enct")
        kb.setbz("enct", et_)
        ew1 = kb.dma("pop", w1b[0:7, :], dram["L1_attn_mlp_w1"][i],
                     waits=[kb.bz("w1b")])
        eb1 = kb.dma("sp", T["ROWS"][0:64, 9:10],
                     dram["L1_attn_mlp_b1"][i].rearrange("(n o) -> n o", o=1),
                     waits=[kb.bz("rows9")])
        ev_h1 = []
        for tsl in range(4):
            pa = PA[0]
            em = kb.mm(pa[0:64, 0:512], w1b[0:7, 0:64],
                       ENC[0:7, tsl * 512:(tsl + 1) * 512], True, True,
                       waits=[ew1, es_, ec_, et_, kb.pa_free[0]], sig=True)
            eg = kb.act(H1[0:64, tsl * 512:(tsl + 1) * 512], pa[0:64, 0:512],
                        AF.Gelu, bias=T["ROWS"][0:64, 9:10],
                        waits=[em, eb1, kb.bz("h1")] +
                        (w_gr if (i == 0 and tsl == 0) else []), sig=True)
            kb.pa_free[0] = eg
            ev_h1.append(eg)
        kb.setbz("h1", ev_h1[-1])
        kb.setbz("w1b", ev_h1[-1])
        kb.setbz("rows9", ev_h1[-1])
        ew2 = kb.dma("pop", w2b[0:64, :], dram["L1_attn_mlp_w2"][i],
                     waits=[kb.bz("w2b")])
        eb2 = kb.dma("sp", T["ROWS"][0:64, 10:11],
                     dram["L1_attn_mlp_b2"][i].rearrange("(n o) -> n o", o=1),
                     waits=[kb.bz("rows10")])
        ev_h2 = []
        for tsl in range(4):
            pa = PA[1]
            em = kb.mm(pa[0:64, 0:512], w2b[0:64, 0:64],
                       H1[0:64, tsl * 512:(tsl + 1) * 512], True, True,
                       waits=[ew2, ev_h1[tsl], kb.pa_free[1]], sig=True)
            eg = kb.act(H2b[0:64, tsl * 512:(tsl + 1) * 512], pa[0:64, 0:512],
                        AF.Gelu, bias=T["ROWS"][0:64, 10:11],
                        waits=[em, eb2, kb.bz("h2")], sig=True)
            kb.pa_free[1] = eg
            ev_h2.append(eg)
        kb.setbz("h2", ev_h2[-1])
        kb.setbz("w2b", ev_h2[-1])
        kb.setbz("rows10", ev_h2[-1])
        ew3 = kb.dma("pop", w3b[0:64, :], dram["L1_attn_mlp_w3"][i],
                     waits=[kb.bz("w3b")])
        b3 = kb.load_bias(dram["L1_attn_mlp_b3"][i], kb.bias_cursor, 8)
        b3c = kb.bias_cursor
        kb.bias_cursor += 8
        nd = kb.wprep[f"ndec{i}"]
        for cc in range(NDC):
            ev_h3 = []
            for tsl in range(4):
                pa = PA[2]
                em = kb.mm(pa[:, 0:512], w3b[0:64, cc * 128:(cc + 1) * 128],
                           H2b[0:64, tsl * 512:(tsl + 1) * 512], True, True,
                           waits=[ew3] + ev_h2 + [kb.pa_free[2]], sig=True)
                e3_ = kb.ts1(H3[:, tsl * 512:(tsl + 1) * 512], pa[:, 0:512],
                             T["BIAS"][:, b3c + cc:b3c + cc + 1], ALU.add,
                             waits=[em, b3, kb.bz("h3")], sig=True)
                kb.pa_free[2] = e3_
                ev_h3.append(e3_)
            kb.setbz("h3", ev_h3[-1])
            ew_ = kb.ts1(WIN[:, 0:2048], TBC[:, 0:2048],
                         T["BIAS"][:, nd[1] + cc:nd[1] + cc + 1], ALU.mult,
                         waits=[nd[0], kb.bz("win")] + ev_tb, sig=True)
            ex_ = kb.act(WIN[:, 0:2048], WIN[:, 0:2048], AF.Exp, waits=[ew_],
                         sig=True)
            dstv = HVF.rearrange("p (a c) -> p c a", c=32)
            eh_ = kb.tt(dstv, H3.rearrange("p (c a) -> p c a", a=64),
                        WIN.rearrange("p (c a) -> p c a", a=64), ALU.mult,
                        waits=[ex_] + ev_h3 +
                        [kb.bz("hvf"), kb.bz("cv_vs")], sig=True)
            kb.setbz("win", eh_)
            kb.setbz("hvf", eh_)
            ev_xs = conv_fwd_chunk(kb, HVF, XFR, [eh_], f"hs{i}.{cc}")
            e_st = kb.dma("pop", dram["hspec"][i, cc],
                          AR[:, XFR + CV_XS:XFR + CV_XS + 4224],
                          waits=ev_xs, tag=f"hst{i}.{cc}")
            kb.setbz("cv_xs", e_st)
        kb.setbz("w3b", kb.bz("h3"))
    kb.setbz("xf_w", e_st)
    kb.setbz("sub_exit", e_st)


def sub_l1attn(kb, s):
    T, dram = kb.t, kb.dram
    AR, PA, PT = T["ARENA"], T["PA"], T["PT"]
    kb.setbz("hr0", kb.bz("sub_exit"))
    kb.setbz("hr1", kb.bz("sub_exit"))
    kb.setbz("wd0t", kb.bz("sub_exit"))
    XF, ev_xf = make_xf(kb, XFR, waits=[kb.bz("xf_w")])
    VFb = AR[:, VFR:VFR + 16384]
    GRb = AR[:, GR:GR + 16384]
    IPW = AR[:, CR:CR + 24576]
    ipb = kb.wprep["ipb"]
    opb = kb.wprep["opb"]

    ev_ipw = []
    for dc in range(NDC):
        ev_ipw.append(kb.dma("pop", IPW[:, dc * 3072:(dc + 1) * 3072],
                             dram["L1_attn_in_proj_w"][dc * 128:(dc + 1) * 128, :],
                             waits=[kb.bz("crw"), kb.bz("crw2"),
                                    kb.bz("cr_free"), kb.bz("cr_free2"),
                                    kb.bz("woutb"), kb.bz("sub_exit")],
                             tag=f"ipw{dc}"))
    ev_v = [None] * NDC
    ev_c0 = [[] for _ in range(NDC)]
    stg = AR[:, WD0 + 2048:WD0 + 3072]
    ev_c1st = None
    ee = None
    for oc in range(24):
        for tsl in range(4):
            g = oc * 4 + tsl
            pa = PA[4 + g % 2]
            evm = None
            for dc in range(NDC):
                w0 = [ev_ipw[dc]]
                if dc == 0:
                    w0 += [kb.pa_free[4 + g % 2]] + list(ev_xf)
                evm = kb.mm(pa[:, 0:512],
                            IPW[:, dc * 3072 + oc * 128:dc * 3072 + (oc + 1) * 128],
                            XF[:, dc * 2048 + tsl * 512:dc * 2048 + tsl * 512 + 512],
                            dc == 0, dc == NDC - 1, waits=w0,
                            sig=(dc == NDC - 1), tag=f"ip{oc}.{tsl}")
            bias = T["BIAS"][:, ipb[1] + oc:ipb[1] + oc + 1]
            if oc < 8:
                dst = bass.AP(T["ARENA"][:].tensor,
                              VFR + oc * 2048 + 8 * tsl,
                              [[ARN, 128], [1, 8], [32, 64]])
                ee = kb.ets1b(4 + g % 2, dst, pa[:, 0:512].rearrange(
                    "p (c a) -> p c a", a=64), bias,
                    waits=[evm, ipb[0], kb.bz("vfw")], sig=True,
                    tag=f"ipv{oc}")
                ev_v[oc] = ee
                kb.setbz("vfw", ee)
            elif oc < 16:
                k = oc - 8
                ee = kb.ets1b(4 + g % 2, GRb[:, k * 2048 + tsl * 512:
                                             k * 2048 + tsl * 512 + 512],
                              pa[:, 0:512], bias,
                              waits=[evm, ipb[0], kb.bz("grw")], sig=True,
                              tag=f"ipc{oc}")
                ev_c0[k].append(ee)
                kb.setbz("grw", ee)
            else:
                k = oc - 16
                sg = stg[:, (g % 2) * 512:(g % 2) * 512 + 512]
                ee = kb.ets1b(4 + g % 2, sg, pa[:, 0:512], bias,
                              waits=[evm, ipb[0], kb.bz(f"c1s{g % 2}")],
                              sig=True, tag=f"ipd{oc}")
                ed = kb.dma("pop", dram["ctrl1_sp"][k][:, tsl * 512:
                                                       (tsl + 1) * 512],
                            sg, waits=[ee], tag=f"c1d{oc}")
                kb.setbz(f"c1s{g % 2}", ed)
                ev_c1st = ed
            kb.pa_free[4 + g % 2] = ee

    ev_g = gate_pipeline(kb, s, 0, [e for l in ev_c0 for e in l], [ee], "g0")
    load_fwd_set(kb)
    load_i1_set(kb)
    ev_vf_new = [None] * NDC
    for cc in range(NDC):
        ev_xs = conv_fwd_chunk(kb, VFb[:, cc * 2048:(cc + 1) * 2048], XFR,
                               [ev_v[cc]], f"c0.{cc}")
        ev_vf_new[cc] = conv_bwd_chunk(
            kb, XFR, dram["hspec"][0, cc],
            GRb[:, cc * 2048:(cc + 1) * 2048],
            VFb[:, cc * 2048:(cc + 1) * 2048], ev_xs,
            [ev_g[cc]], f"c0.{cc}")
    kb.setbz("crw", ev_vf_new[-1])
    kb.setbz("crw2", ev_vf_new[-1])
    kb.setbz("wd0t", ev_vf_new[-1])
    ev_c1 = []
    for k in range(NDC):
        ev_c1.append(kb.dma("pop", GRb[:, k * 2048:(k + 1) * 2048],
                            dram["ctrl1_sp"][k],
                            waits=[ev_c1st, ev_vf_new[k]],
                            tag=f"c1l{k}"))
    ev_g1 = gate_pipeline(kb, s, 1, ev_c1, [ev_vf_new[-1]], "g1")
    load_fwd_set(kb)
    load_i1_set(kb)
    for cc in range(NDC):
        ev_xs = conv_fwd_chunk(kb, VFb[:, cc * 2048:(cc + 1) * 2048], XFR,
                               [ev_vf_new[cc]], f"c1.{cc}")
        ev_vf_new[cc] = conv_bwd_chunk(
            kb, XFR, dram["hspec"][1, cc],
            GRb[:, cc * 2048:(cc + 1) * 2048],
            VFb[:, cc * 2048:(cc + 1) * 2048], ev_xs,
            [ev_g1[cc]], f"c1.{cc}")
    kb.setbz("crw", ev_vf_new[-1])
    kb.setbz("crw2", ev_vf_new[-1])
    kb.setbz("wd0t", ev_vf_new[-1])
    OPW = AR[:, CR:CR + 8192]
    ev_opw = []
    for dc in range(NDC):
        ev_opw.append(kb.dma("pop", OPW[:, dc * 1024:(dc + 1) * 1024],
                             dram["L1_attn_out_proj_w"][dc * 128:(dc + 1) * 128, :],
                             waits=[kb.bz("crw"), kb.bz("crw2")],
                             tag=f"opw{dc}"))
    ev_yout = []
    for oc in range(NDC):
        for fsl in range(4):
            g = oc * 4 + fsl
            pa = PA[4 + g % 2]
            evm = None
            for dc in range(NDC):
                w0 = [ev_opw[dc], ev_vf_new[dc]]
                if dc == 0:
                    w0.append(kb.pa_free[4 + g % 2])
                evm = kb.mm(pa[:, 0:512],
                            OPW[:, dc * 1024 + oc * 128:dc * 1024 + (oc + 1) * 128],
                            VFb[:, dc * 2048 + fsl * 512:dc * 2048 + fsl * 512 + 512],
                            dc == 0, dc == NDC - 1, waits=w0,
                            sig=(dc == NDC - 1), tag=f"op{oc}.{fsl}")
            dst = bass.AP(T["ARENA"][:].tensor, GR + oc * 2048 + 16 * fsl,
                          [[ARN, 128], [1, 16], [64, 32]])
            ee2 = kb.ets1b(4 + g % 2, dst,
                           pa[:, 0:512].rearrange("p (a c) -> p a c", c=32),
                           T["BIAS"][:, opb[1] + oc:opb[1] + oc + 1],
                           waits=[evm, opb[0], kb.bz("grw")],
                           sig=True, tag=f"ope{oc}.{fsl}")
            kb.pa_free[4 + g % 2] = ee2
            ev_yout.append(ee2)
    kb.setbz("grw", ev_yout[-1])
    kb.setbz("crw", ev_yout[-1])
    er = eq = None
    for j in range(NTIL):
        def piece(k, j=j):
            return GRb[:, k * 2048 + j * 128: k * 2048 + (j + 1) * 128]
        er, eq = post_sub_tile(kb, j, piece, waits=ev_yout, tag="l1p")
        kb.setbz("xf_w", er)
        kb.setbz("grw", er)
    finish_stats(kb, [eq])
    kb.setbz("sub_exit", er)
    kb.ffnw_free = [er]



_NC_CACHE = {}


def kernel(**inputs):
    subset = inputs.pop("_subset", SUB_ALL)
    key = tuple(subset)
    if key not in _NC_CACHE:
        _NC_CACHE[key] = build_program(subset)
    nc = _NC_CACHE[key]
    C = host_constants()
    x = np.ascontiguousarray(inputs["L0_attn_x"])
    in_maps = []
    for core in range(8):
        m = {"x": np.ascontiguousarray(x[core * NSEQ:(core + 1) * NSEQ])}
        for nm, _ in INPUT_SPECS:
            m[nm] = np.ascontiguousarray(inputs[nm])
        for nm, arr in C.items():
            m["c_" + nm] = arr
        in_maps.append(m)
    res = run_bass_kernel_spmd(nc, in_maps, core_ids=list(range(8)))
    return np.concatenate([r["out"] for r in res.results], axis=0)


# version 07 (final)
# speedup vs baseline: 1.0533x; 1.0533x over previous
import contextlib
import numpy as np
import ml_dtypes
import concourse.bass as bass
import concourse.mybir as mybir
from concourse.bass_utils import run_bass_kernel_spmd

F32 = mybir.dt.float32
BF16 = mybir.dt.bfloat16
I32 = mybir.dt.int32
AF = mybir.ActivationFunctionType
ALU = mybir.AluOpType
AX = mybir.AxisListType
NBF = ml_dtypes.bfloat16

NSEQ, L, DM, DF, NST = 4, 2048, 1024, 4096, 64
RNE_MAGIC = float(np.float32(12582912.0))
NCH, NTIL, NDC = 16, 16, 8
EPS = 1e-5
SUB_ALL = ("l0attn", "l0ffn", "l1attn", "l1ffn")
DEBUG_DUMPS = []
FFN_STAGE = 4


ENG_SEMS = {"pe": "pe", "dve": "dve", "act": "act", "pop": "popc"}
DMA_SEMS = {"sp": "spd", "pop": "popd"}
POP_SEMS = ("popd", "popd1", "popd2", "popd3")
SP_SEMS = ("spd", "spd1", "spd2", "spd3")
ALL_ENG = ("pe", "dve", "act", "sp", "pop")
RACE_STRICT = True


class Prog:
    def __init__(self, nc):
        self.nc = nc
        self.items = {e: [] for e in ALL_ENG}
        self.cnt = {s: 0 for s in list(ENG_SEMS.values()) + list(DMA_SEMS.values())
                    + list(POP_SEMS) + list(SP_SEMS)}
        self.pop_rot = 0
        self.sp_rot = 0
        self.last_wait = {e: {} for e in ALL_ENG}
        self.sems = {}

    def rec(self, eng, fn, waits=(), sig=False, dma=False, tag=""):
        w2 = []
        my_sem = ENG_SEMS.get(eng)
        lw = self.last_wait[eng]

        def add(s, v):
            if lw.get(s, -1) >= v:
                return
            lw[s] = v
            w2.append((s, v))

        for ev in waits:
            if ev is None:
                continue
            s, v = ev
            if s == my_sem and not RACE_STRICT:
                continue
            add(s, v)
        ev = None
        inc = 16 if dma else (1 if sig else 0)
        if inc:
            if dma and eng == "pop":
                s = POP_SEMS[self.pop_rot]
                self.pop_rot = (self.pop_rot + 1) % len(POP_SEMS)
            elif dma and eng == "sp":
                s = SP_SEMS[self.sp_rot]
                self.sp_rot = (self.sp_rot + 1) % len(SP_SEMS)
            else:
                s = DMA_SEMS[eng] if dma else ENG_SEMS[eng]
            if dma:
                add(s, self.cnt[s])
            self.cnt[s] += inc
            ev = (s, self.cnt[s])
        self.items[eng].append((w2, fn, ev, inc))
        return ev

    def emit(self, block):
        sems = self.sems

        def runner(eng):
            def run(e):
                for (waits, fn, ev, inc) in self.items[eng]:
                    for (s, v) in waits:
                        e.wait_ge(sems[s], v)
                    ins = fn(e)
                    if inc:
                        ins.then_inc(sems[ev[0]], inc)
            return run
        block.tensor(runner("pe"))
        block.vector(runner("dve"))
        block.scalar(runner("act"))
        block.sync(runner("sp"))
        block.gpsimd(runner("pop"))



def host_constants():
    c = {}
    c["iden"] = np.eye(128, dtype=NBF)
    c["iden32"] = np.eye(128, dtype=np.float32)
    c["ones"] = np.ones((1, 128), np.float32)
    k1v = np.arange(33)
    ang = -2.0 * np.pi * np.outer(np.arange(32), k1v) / 64.0
    d1 = np.zeros((32, 66), np.float32)
    d1[:, 0:33] = np.cos(ang)
    d1[:, 33:66] = np.sin(ang)
    c["fft_d1"] = np.tile(d1, (4, 1)).astype(NBF)
    a = np.arange(64)
    k2 = np.arange(64)
    tcbd = np.zeros((33, 128, 128), np.float32)
    tsbd = np.zeros((33, 128, 128), np.float32)
    mtsbd = np.zeros((33, 128, 128), np.float32)
    for q in range(33):
        t = np.exp(-2j * np.pi * a * q / 4096.0)[:, None] * \
            np.exp(-2j * np.pi * np.outer(a, k2) / 64.0)
        for chb in range(2):
            rows = chb * 64 + a
            cols = k2 * 2 + chb
            tcbd[q][np.ix_(rows, cols)] = t.real
            tsbd[q][np.ix_(rows, cols)] = t.imag
            mtsbd[q][np.ix_(rows, cols)] = -t.imag
    c["fft_tc"] = tcbd.astype(NBF)
    c["fft_ts"] = tsbd.astype(NBF)
    c["fft_mts"] = mtsbd.astype(NBF)
    b = np.arange(64)
    ecb = np.zeros((33, 128, 128), np.float32)
    esb = np.zeros((33, 128, 128), np.float32)
    mesb = np.zeros((33, 128, 128), np.float32)
    for q in range(33):
        t = np.exp(2j * np.pi * np.outer(k2, b) / 64.0) * \
            np.exp(2j * np.pi * b * q / 4096.0)[None, :]
        for chb in range(2):
            rows = k2 * 2 + chb
            cols = b * 2 + chb
            ecb[q][np.ix_(rows, cols)] = t.real
            esb[q][np.ix_(rows, cols)] = t.imag
            mesb[q][np.ix_(rows, cols)] = -t.imag
    c["fft_ec"] = ecb.astype(NBF)
    c["fft_es"] = esb.astype(NBF)
    c["fft_mes"] = mesb.astype(NBF)
    J = np.arange(32)
    f = np.full(33, 2.0)
    f[0] = 1.0
    f[32] = 1.0
    vrvi = np.zeros((66, 32), np.float32)
    vrvi[0:33] = f[:, None] * np.cos(2 * np.pi * np.outer(k1v, J) / 64.0) / 4096.0
    vrvi[33:66] = -f[:, None] * np.sin(2 * np.pi * np.outer(k1v, J) / 64.0) / 4096.0
    c["fft_v"] = vrvi.astype(NBF)
    c["tlin"] = np.linspace(0.0, 1.0, L).astype(np.float32).reshape(1, L)
    c["encoff"] = np.array([0, 0, 0, .25, .25, .25],
                           np.float32).reshape(6, 1)
    return c



class KB:
    def __init__(self, nc, P, subset):
        self.nc, self.P, self.subset = nc, P, subset
        self.t, self.dram = {}, {}
        self.out_evs = []
        self.scl_cursor = 0
        self.bias_cursor = 0
        self.pt_free = [None, None]
        self.pa_free = [None] * 6
        self.wprep = {}
        self.busy = {}

    def dbg(self, name, ap, waits=(), pieces=1):
        if name not in DEBUG_DUMPS:
            return None
        shp = list(ap.shape)
        t = self.nc.dram_tensor("dbg_" + name, shp, ap.dtype,
                                kind="ExternalOutput").ap()
        ev = self.dma("pop", t, ap, waits=waits, tag="dbg")
        self.out_evs.append(ev)
        return ev

    def load_bias(self, src_ap, bc0, n):
        T = self.t
        stg = T["ROWS"][0:n, 128:256]
        pa = T["PA"][5]
        e1 = self.dma("sp", stg, src_ap.rearrange("(c p) -> c p", p=128),
                      waits=[self.bz("rows_stg")])
        e2 = self.P.rec(
            "pe", lambda e: e.transpose(pa[0:128, 0:n], stg,
                                        T["iden32"][0:n, 0:n]),
            waits=[e1, self.ev_iden32, self.pa_free[5]], sig=True)
        e3 = self.cp("dve", T["BIAS"][:, bc0:bc0 + n], pa[0:128, 0:n],
                     waits=[e2], sig=True)
        self.pa_free[5] = e3
        self.setbz("rows_stg", e3)
        return e3

    def bz(self, key):
        return self.busy.get(key)

    def setbz(self, key, ev):
        self.busy[key] = ev

    def dma(self, eng, dst, src, waits=(), tag="", slow=False):
        kw = {"allow_slow_non_contiguous": True} if slow else {}
        return self.P.rec(eng, lambda e: e.dma_start(out=dst, in_=src, **kw),
                          waits=waits, dma=True, tag=tag)

    def mm(self, out, lhsT, rhs, start, stop, waits=(), sig=False, tp=None, tag=""):
        kw = {"tile_position": tp} if tp is not None else {}
        return self.P.rec(
            "pe", lambda e: e.matmul(out, lhsT, rhs, start=start, stop=stop, **kw),
            waits=waits, sig=sig, tag=tag)

    def tr(self, out_psum, in_ap, waits=(), sig=False, tag=""):
        iden = self.t["iden_sb"]
        p = in_ap.shape[0]
        return self.P.rec(
            "pe", lambda e: e.transpose(out_psum, in_ap, iden[0:p, 0:p]),
            waits=waits, sig=sig, tag=tag)

    def cp(self, eng, out, src, waits=(), sig=False, tag=""):
        if eng == "act":
            return self.P.rec(
                "act", lambda e: e.activation(out, src, AF.Copy),
                waits=waits, sig=sig, tag=tag)
        return self.P.rec(eng, lambda e: e.tensor_copy(out, src),
                          waits=waits, sig=sig, tag=tag)

    def tt(self, out, a, b, op, waits=(), sig=False, tag=""):
        return self.P.rec("dve", lambda e: e.tensor_tensor(out, a, b, op),
                          waits=waits, sig=sig, tag=tag)

    def ts1(self, out, a, s1, op0, waits=(), sig=False, eng="dve", tag=""):
        return self.P.rec(eng, lambda e: e.tensor_single_scalar(out, a, s1, op0),
                          waits=waits, sig=sig, tag=tag)

    def ts2(self, out, a, s1, s2, op0, op1, waits=(), sig=False, accum=None, tag=""):
        kw = {"accum_out": accum} if accum is not None else {}
        return self.P.rec(
            "dve", lambda e: e.tensor_scalar(out, a, s1, s2, op0, op1, **kw),
            waits=waits, sig=sig, tag=tag)

    def stt(self, out, in0, scalar, in1, op0, op1, waits=(), sig=False,
            accum=None, tag=""):
        kw = {"accum_out": accum} if accum is not None else {}
        return self.P.rec(
            "dve", lambda e: e.scalar_tensor_tensor(out, in0, scalar, in1,
                                                    op0, op1, **kw),
            waits=waits, sig=sig, tag=tag)

    def act(self, out, a, func, bias=0.0, scale=1.0, waits=(), sig=False,
            accum=None, tag=""):
        kw = {"accum_out": accum} if accum is not None else {}
        return self.P.rec(
            "act", lambda e: e.activation(out, a, func, bias=bias, scale=scale,
                                          **kw),
            waits=waits, sig=sig, tag=tag)

    def red(self, out, a, op=ALU.add, absval=False, waits=(), sig=False, tag=""):
        return self.P.rec(
            "dve", lambda e: e.tensor_reduce(out, a, AX.X, op,
                                             apply_absolute_value=absval),
            waits=waits, sig=sig, tag=tag)

    def recip(self, out, a, waits=(), sig=True):
        return self.P.rec("dve", lambda e: e.reciprocal(out, a), waits=waits,
                          sig=sig)

    EVAC_ENG = {0: "dve", 1: "act", 2: "dve", 3: "act", 4: "dve", 5: "act"}

    def ecp(self, bank, out, src, waits=(), sig=True, tag=""):
        return self.cp(self.EVAC_ENG[bank], out, src, waits=waits, sig=sig,
                       tag=tag)

    def ets1b(self, bank, out, src, bias_ap, waits=(), sig=True, tag=""):
        if self.EVAC_ENG[bank] == "dve":
            return self.ts1(out, src, bias_ap, ALU.add, waits=waits, sig=sig,
                            tag=tag)
        return self.act(out, src, AF.Identity, bias=bias_ap, waits=waits,
                        sig=sig, tag=tag)

    def bcast_val(self, src11, dst_col, waits=(), tag="bc"):
        pa = self.t["PA"][5]
        ev = self.P.rec(
            "pe", lambda e: e.matmul(pa[:, 2:3], self.t["ones_sb"][0:1, :],
                                     src11, start=True, stop=True),
            waits=list(waits) + [self.pa_free[5],
                                 getattr(self, "ev_ones", None)],
            sig=True, tag=tag)
        ev2 = self.cp("dve", dst_col, pa[:, 2:3], waits=[ev], sig=True, tag=tag)
        self.pa_free[5] = ev2
        return ev2



INPUT_SPECS = [
    ("L0_attn_log_Lam_real", (NST,)), ("L0_attn_Lam_imag", (NST,)),
    ("L0_attn_B_real", (NST, DM)), ("L0_attn_B_imag", (NST, DM)),
    ("L0_attn_C_real", (DM, NST)), ("L0_attn_C_imag", (DM, NST)),
    ("L0_attn_D", (DM,)), ("L0_attn_log_dt", (1,)),
    ("L0_attn_W_out", (DM, DM)), ("L0_attn_b_out", (DM,)),
    ("L0_ffn_W_fc", (DF, DM)), ("L0_ffn_b_fc", (DF,)),
    ("L0_ffn_A_fc", (16, DM)), ("L0_ffn_B_fc", (DF, 16)),
    ("L0_ffn_W_proj", (DM, DF)), ("L0_ffn_b_proj", (DM,)),
    ("L0_ffn_A_proj", (16, DF)), ("L0_ffn_B_proj", (DM, 16)),
    ("L1_attn_in_proj_w", (DM, 3 * DM)), ("L1_attn_in_proj_b", (3 * DM,)),
    ("L1_attn_out_proj_w", (DM, DM)), ("L1_attn_out_proj_b", (DM,)),
    ("L1_attn_freqs", (2, 3)),
    ("L1_attn_mlp_w1", (2, 7, 64)), ("L1_attn_mlp_b1", (2, 64)),
    ("L1_attn_mlp_w2", (2, 64, 64)), ("L1_attn_mlp_b2", (2, 64)),
    ("L1_attn_mlp_w3", (2, 64, DM)), ("L1_attn_mlp_b3", (2, DM)),
    ("L1_attn_decay", (2, DM)), ("L1_attn_conv_w", (2, 3, 1, DM)),
    ("L1_attn_conv_b", (2, DM)), ("L1_attn_gate_w", (2, DM, DM)),
    ("L1_attn_gate_b", (2, DM)),
    ("L1_ffn_W_fc", (DF, DM)), ("L1_ffn_b_fc", (DF,)),
    ("L1_ffn_A_fc", (16, DM)), ("L1_ffn_B_fc", (DF, 16)),
    ("L1_ffn_W_proj", (DM, DF)), ("L1_ffn_b_proj", (DM,)),
    ("L1_ffn_A_proj", (16, DF)), ("L1_ffn_B_proj", (DM, 16)),
]

ARN = 77824

XFR = 0
VFR = 16384
GR = 32768
CR = 49152
WD0 = 73728


def build_program(subset=SUB_ALL):
    nc = bass.Bass("TRN2", target_bir_lowering=False, debug=False)
    C = host_constants()
    P = Prog(nc)
    kb = KB(nc, P, subset)
    kb.C = C
    dram = kb.dram

    def din(name, shape, dtype=F32):
        dram[name] = nc.dram_tensor(name, list(shape), dtype,
                                    kind="ExternalInput").ap()

    def dint(name, shape, dtype=BF16):
        dram[name] = nc.dram_tensor(name, list(shape), dtype, kind="Internal").ap()

    din("x", (NSEQ, L, DM))
    for nm, shp in INPUT_SPECS:
        din(nm, shp)
    for nm, arr in C.items():
        din("c_" + nm, arr.shape, BF16 if arr.dtype == NBF else F32)
    dram["out"] = nc.dram_tensor("out", [NSEQ, L, DM], F32,
                                 kind="ExternalOutput").ap()

    dint("wst_fc", (2, NDC, 128, DF))
    dint("wst_pj", (2, NDC, 32, 128, 128))
    dint("wst_wout", (NDC, 128, DM))
    dint("wst_scan", (NST, 2, 128, 128))
    dint("ctrl1_sp", (NDC, 128, L))
    dint("dscr_c", (64, 32))
    dint("ddiag", (2, 64, 1024))
    dint("hspec", (2, NDC, 128, 33 * 64 * 2))

    stack = contextlib.ExitStack()
    with stack:
        T = kb.t

        def sb(name, shape, dtype):
            T[name] = stack.enter_context(nc.sbuf_tensor(name, list(shape), dtype))

        sb("R", [128, NTIL * DM], BF16)
        sb("ARENA", [128, ARN], BF16)
        sb("XTT", [128, 2 * DM], BF16)
        sb("TMP32", [128, 2 * DM], F32)
        sb("STAT", [128, 160], F32)
        sb("SCL", [128, 32], F32)
        sb("ROWS", [128, 256], F32)
        sb("COL1", [128, 1], F32)
        sb("iden_sb", [128, 128], BF16)
        sb("iden32", [128, 128], F32)
        sb("ones_sb", [1, 128], F32)
        sb("BT_sb", [128, NDC * 128], BF16)
        sb("CT_sb", [128, DM], BF16)
        sb("QPR", [64, 128], BF16)
        sb("QPI", [64, 128], BF16)
        sb("QPMI", [64, 128], BF16)
        sb("Q128", [64, 4], F32)
        sb("CCR", [64, NCH], F32)
        sb("CCI", [64, NCH], F32)
        sb("CRWR", [64, NCH], F32)
        sb("CRWI", [64, NCH], F32)
        sb("CCB", [64, 32], BF16)
        sb("BIAS", [128, 256], F32)
        sb("D1_sb", [128, 66], BF16)
        sb("VRVI", [66, 32], BF16)
        T["PA"] = [stack.enter_context(nc.psum_tensor(f"PA{i}", [128, 512], F32))
                   for i in range(6)]
        T["PT"] = [stack.enter_context(nc.psum_tensor(f"PT{i}", [128, 1024], BF16))
                   for i in range(2)]

        for s in (list(ENG_SEMS.values()) + list(DMA_SEMS.values())
                  + list(POP_SEMS[1:]) + list(SP_SEMS[1:])):
            P.sems[s] = stack.enter_context(nc.semaphore(s))

        with nc.Block() as block:
            build_all(kb)
            P.emit(block)
    return nc



def build_all(kb):
    T, dram = kb.t, kb.dram
    kb.ev_iden = kb.dma("sp", T["iden_sb"][:], dram["c_iden"])
    kb.ev_iden32 = kb.dma("sp", T["iden32"][:],
                          dram["c_iden"].bitcast(BF16) if False else dram["c_iden32"])
    kb.ev_ones = kb.dma("sp", T["ones_sb"][:], dram["c_ones"])
    kb.ev_d1 = kb.dma("sp", T["D1_sb"][:], dram["c_fft_d1"])
    kb.ev_vrvi = kb.dma("sp", T["VRVI"][:], dram["c_fft_v"])
    c0 = kb.bias_cursor
    kb.bias_cursor += 2
    m0 = kb.P.rec("dve", lambda e: e.memset(T["BIAS"][:, c0:c0 + 1],
                                            RNE_MAGIC), sig=True)
    m1 = kb.P.rec("dve", lambda e: e.memset(T["BIAS"][:, c0 + 1:c0 + 2],
                                            -RNE_MAGIC), sig=True)
    kb.magic_ap = (T["BIAS"][:, c0:c0 + 1], T["BIAS"][:, c0 + 1:c0 + 2])
    kb.ev_magic = [m0, m1]

    if "l0ffn" in kb.subset:
        prep_ffn_weights(kb, 0)
    if "l1ffn" in kb.subset:
        prep_ffn_weights(kb, 1)
    if "l0attn" in kb.subset:
        prep_l0_attn(kb)
    if "l1attn" in kb.subset:
        prep_l1_attn(kb)

    for s in range(NSEQ):
        build_seq(kb, s)
    kb.P.rec("pop", lambda e: e.nop(), waits=kb.out_evs)


def build_seq(kb, s):
    T, dram = kb.t, kb.dram
    R = T["R"]
    src = dram["x"][s].rearrange("(j p) d -> p j d", p=128)
    ev_r = kb.dma("pop", R[:].rearrange("p (j d) -> p j d", d=DM), src,
                  waits=getattr(kb, "r_free", []), tag=f"ldx{s}")
    kb.r_tile_ready = [ev_r] * NTIL
    stats_from_scratch(kb, ev_r)
    if s == 0:
        kb.dbg("stat0", kb.t["STAT"][:, 48:80], waits=[kb.stats_ready,
                                                        kb.stats_ready_d])

    if "l0attn" in kb.subset:
        sub_l0attn(kb, s)
    if "l0ffn" in kb.subset:
        sub_ffn(kb, s, 0)
    if "l1attn" in kb.subset:
        sub_l1attn(kb, s)
    if "l1ffn" in kb.subset:
        sub_ffn(kb, s, 1)

    dst = dram["out"][s].rearrange("(j p) d -> p j d", p=128)
    evs = [e for e in kb.r_tile_ready if e is not None]
    ev_o = kb.dma("pop", dst, R[:].rearrange("p (j d) -> p j d", d=DM),
                  waits=evs + [kb.stats_ready], tag=f"sto{s}")
    kb.out_evs.append(ev_o)
    kb.r_free = [ev_o]



def stats_from_scratch(kb, ev_r):
    T = kb.t
    R, STAT = T["R"], T["STAT"]
    SX, SQ = STAT[:, 16:32], STAT[:, 32:48]
    e1 = e2 = None
    for j in range(NTIL):
        tile = R[:, j * DM:(j + 1) * DM]
        e1 = kb.red(SX[:, j:j + 1], tile, ALU.add,
                    waits=[ev_r, kb.bz("statw")], sig=True)
        e2 = kb.act(T["TMP32"][:, DM:2 * DM], tile, AF.Square,
                    waits=[ev_r, e2, kb.bz("statw")],
                    accum=SQ[:, j:j + 1], sig=True)
    finish_stats(kb, [e1, e2])


def finish_stats(kb, evs):
    T = kb.t
    STAT = T["STAT"]
    SX, SQ = STAT[:, 16:32], STAT[:, 32:48]
    MU, RS = STAT[:, 48:64], STAT[:, 64:80]
    V1, V2 = STAT[:, 80:96], STAT[:, 96:112]
    w0 = list(evs) + [kb.bz("lnread")]
    ea = kb.ts1(MU, SX, 1.0 / DM, ALU.mult, waits=w0, sig=True)
    eb = kb.ts1(V1, SQ, 1.0 / DM, ALU.mult, waits=w0 + [ea], sig=True)
    ec = kb.tt(V2, MU, MU, ALU.mult, waits=[ea, eb], sig=True)
    ed = kb.tt(V1, V1, V2, ALU.subtract, waits=[ec], sig=True)
    e5 = kb.ts1(V2, V1, EPS, ALU.add, waits=[ed], sig=True)
    e6 = kb.recip(V1, V2, waits=[e5])
    e7 = kb.act(RS, V1, AF.Sqrt, waits=[e6], sig=True)
    kb.stats_ready = e7
    kb.stats_ready_d = e6
    kb.MU, kb.RS = MU, RS
    kb.setbz("statw", e7)


def ln_tile(kb, j, dst, waits=()):
    T = kb.t
    ev = kb.ts2(dst, T["R"][:, j * DM:(j + 1) * DM], kb.MU[:, j:j + 1],
                kb.RS[:, j:j + 1], ALU.subtract, ALU.mult,
                waits=list(waits) + [kb.stats_ready, kb.r_tile_ready[j]],
                sig=True, tag=f"ln{j}")
    kb.setbz("lnread", ev)
    return ev


def make_xf(kb, dst_region, waits=()):
    T = kb.t
    XF = T["ARENA"][:, dst_region:dst_region + 16384]
    PT = T["PT"]
    evs2 = [None]
    for j in range(NTIL):
        stg = T["XTT"][:, (j % 2) * DM:(j % 2 + 1) * DM]
        ev_ln = ln_tile(kb, j, stg,
                        waits=list(waits) + [kb.xtt_busy(j % 2)])
        bank = j % 2
        last = None
        for k in range(8):
            last = kb.tr(PT[bank][:, k * 128:(k + 1) * 128],
                         stg[:, k * 128:(k + 1) * 128],
                         waits=[ev_ln, kb.pt_free[bank],
                                getattr(kb, "pt_free_act", None), kb.ev_iden]
                         if k == 0 else [],
                         sig=(k == 7), tag=f"mx{j}")
        dst = XF.rearrange("p (k c) -> p k c", c=2048)[
            :, :, j * 128:(j + 1) * 128]
        ev = kb.cp("dve" if bank == 0 else "act", dst, PT[bank][:, 0:1024],
                   waits=[last, kb.bz("xf_w")], sig=True, tag=f"mxe{j}")
        kb.pt_free[bank] = ev
        kb.set_xtt_busy(j % 2, ev)
        evs2.append(ev)
    return XF, evs2[-2:]


def _xtt_busy(kb, i):
    return getattr(kb, "_xtt", {}).get(i)


def _set_xtt_busy(kb, i, ev):
    if not hasattr(kb, "_xtt"):
        kb._xtt = {}
    kb._xtt[i] = ev


KB.xtt_busy = _xtt_busy
KB.set_xtt_busy = _set_xtt_busy


def post_sub_tile(kb, j, piece, waits=(), tag="ps"):
    T = kb.t
    R, STAT, PT = T["R"], T["STAT"], T["PT"]
    bank = j % 2
    stg = T["XTT"][:, (j % 2) * DM:(j % 2 + 1) * DM]
    last = None
    for k in range(8):
        w = list(waits) + [kb.pt_free[bank], kb.ev_iden] if k == 0 else []
        last = kb.tr(PT[bank][:, k * 128:(k + 1) * 128], piece(k), waits=w,
                     sig=(k == 7), tag=f"{tag}T{j}")
    ev_e = kb.cp("dve", stg, PT[bank][:, 0:1024],
                 waits=[last, kb.xtt_busy(j % 2)], sig=True)
    kb.pt_free[bank] = ev_e
    kb.set_xtt_busy(j % 2, ev_e)
    SYX, SYQ = STAT[:, 144:145], STAT[:, 145:146]
    ps = kb.bz("poststat")
    e1 = kb.red(SYX, stg, ALU.add, waits=[ev_e, ps], sig=True)
    e2 = kb.act(T["TMP32"][:, DM:2 * DM], stg, AF.Square,
                waits=[ev_e, ps, kb.bz("sqdum")], accum=SYQ, sig=True)
    kb.setbz("sqdum", e2)
    MYU, MYR = STAT[:, 146:147], STAT[:, 147:148]
    A1, A2 = STAT[:, 148:149], STAT[:, 149:150]
    ea = kb.ts1(MYU, SYX, 1.0 / DM, ALU.mult, waits=[e1, ps], sig=True)
    eb = kb.ts1(A1, SYQ, 1.0 / DM, ALU.mult, waits=[e2, ps], sig=True)
    ec = kb.tt(A2, MYU, MYU, ALU.mult, waits=[ea], sig=True)
    ed = kb.tt(A1, A1, A2, ALU.subtract, waits=[eb, ec], sig=True)
    e5 = kb.ts1(A2, A1, EPS, ALU.add, waits=[ed], sig=True)
    e6 = kb.recip(A1, A2, waits=[e5])
    e7 = kb.act(MYR, A1, AF.Sqrt, waits=[e6, ps], sig=True)
    kb.setbz("poststat", e7)
    et = kb.ts2(T["TMP32"][:, 0:DM], stg, MYU, MYR, ALU.subtract, ALU.mult,
                waits=[e7, kb.bz("tmplo")], sig=True)
    kb.setbz("tmplo", et)
    kb.set_xtt_busy(j % 2, et)
    SX, SQ = STAT[:, 16:32], STAT[:, 32:48]
    er = kb.stt(R[:, j * DM:(j + 1) * DM], T["TMP32"][:, 0:DM], 1.0,
                R[:, j * DM:(j + 1) * DM], ALU.mult, ALU.add,
                waits=[et, kb.r_tile_ready[j], kb.stats_ready],
                accum=SX[:, j:j + 1], sig=True)
    kb.setbz("tmplo", er)
    eq = kb.act(T["TMP32"][:, DM:2 * DM], R[:, j * DM:(j + 1) * DM], AF.Square,
                waits=[er, kb.bz("sqdum"), kb.stats_ready], accum=SQ[:, j:j + 1],
                sig=True)
    kb.setbz("sqdum", eq)
    kb.r_tile_ready[j] = er
    return er, eq



def prep_ffn_weights(kb, li):
    lay = f"L{li}"
    for which in ("fc", "proj"):
        prep_one_ffn_w(kb, li, lay, which)
    for fcp, n in (("fc", 32), ("proj", 8)):
        bname = f"{lay}_ffn_b_{fcp}"
        bc0 = kb.bias_cursor
        kb.bias_cursor += n
        evb = kb.load_bias(kb.dram[bname], bc0, n)
        kb.wprep[bname] = (evb, bc0)


def prep_one_ffn_w(kb, li, lay, which):
    P, T, dram = kb.P, kb.t, kb.dram
    fcp = "fc" if which == "fc" else "proj"
    W, A, B = (dram[f"{lay}_ffn_W_{fcp}"], dram[f"{lay}_ffn_A_{fcp}"],
               dram[f"{lay}_ffn_B_{fcp}"])
    rows, cols = (DF, DM) if which == "fc" else (DM, DF)
    nrc = rows // 128
    STAT, SCL, AR, PA, PT = T["STAT"], T["SCL"], T["ARENA"], T["PA"], T["PT"]
    ARF = AR[:].bitcast(F32)
    WLD = ARF[:, CR // 2:CR // 2 + 4096]
    WI32 = AR[:].bitcast(I32)[:, (CR + 8192) // 2:(CR + 8192) // 2 + 1024]
    bt_sb = AR[:, CR + 10240:CR + 14336]
    a_sb = AR[:, CR + 14336:CR + 18432]
    qtmp = AR[:, WD0:WD0 + 4096]
    qstg = AR[:, CR + 18432:CR + 19456]
    scol = kb.scl_cursor
    kb.scl_cursor += 2

    run, tmpc = STAT[:, 0:1], STAT[:, 1:2]
    ev_mx = getattr(kb, "wld_free", None)
    for rc in range(nrc):
        dst = WLD[:, 0:cols]
        ev = kb.dma("sp", dst, W[rc * 128:(rc + 1) * 128, :], waits=[ev_mx],
                    tag=f"q1{li}{which}")
        if rc == 0:
            ev_mx = kb.red(run, dst, ALU.max, absval=True, waits=[ev], sig=True)
        else:
            e1 = kb.red(tmpc, dst, ALU.max, absval=True, waits=[ev], sig=True)
            ev_mx = kb.tt(run, run, tmpc, ALU.max, waits=[e1], sig=True)
    ev_cp = kb.cp("dve", T["COL1"][:, 0:1], run, waits=[ev_mx], sig=True)
    ev_c = kb.dma("pop", T["ROWS"][0:1, 0:128], T["COL1"][:, 0:1], waits=[ev_cp])
    ev_m2 = kb.red(T["ROWS"][0:1, 128:129], T["ROWS"][0:1, 0:128], ALU.max,
                   waits=[ev_c], sig=True)
    ev_s = kb.ts1(T["ROWS"][0:1, 129:130], T["ROWS"][0:1, 128:129], 1.0 / 127.0,
                  ALU.mult, waits=[ev_m2], sig=True)
    ev_i = kb.recip(T["ROWS"][0:1, 130:131], T["ROWS"][0:1, 129:130],
                    waits=[ev_s])
    ev_bs = kb.bcast_val(T["ROWS"][0:1, 129:130], SCL[:, scol:scol + 1],
                         waits=[ev_s])
    ev_bi = kb.bcast_val(T["ROWS"][0:1, 130:131], SCL[:, scol + 1:scol + 2],
                         waits=[ev_i])
    kb.setbz("rows_low", ev_bi)

    prev = None
    evs_bt = []
    for rc in range(nrc):
        stg = T["XTT"][:, 0:16]
        ev = kb.dma("pop", stg, B[rc * 128:(rc + 1) * 128, :], waits=[prev])
        evt = kb.tr(PT[0][0:16, 0:128], stg, waits=[ev, kb.ev_iden,
                                                    kb.pt_free[0]], sig=True)
        prev = kb.cp("dve", bt_sb[0:16, rc * 128:(rc + 1) * 128],
                     PT[0][0:16, 0:128], waits=[evt], sig=True)
        kb.pt_free[0] = prev
        evs_bt.append(prev)
    ev_a = kb.dma("pop", a_sb[0:16, 0:cols], A)
    ev_a2 = kb.ts1(a_sb[0:16, 0:cols], a_sb[0:16, 0:cols], 2.0, ALU.mult,
                   waits=[ev_a], sig=True)

    stash_fc = dram["wst_fc"]
    stash_pj = dram["wst_pj"]
    ev_wfree = ev_mx
    for rc in range(nrc):
        wld = WLD[:, 0:cols]
        ev_l = kb.dma("sp", wld, W[rc * 128:(rc + 1) * 128, :],
                      waits=[ev_wfree], tag=f"q2{li}{which}")
        ev_sl = []
        for g in range(cols // DM):
            Ta = T["TMP32"][:, 0:DM]
            Tb = T["TMP32"][:, DM:2 * DM]
            e1 = kb.ts1(Ta, wld[:, g * DM:(g + 1) * DM],
                        SCL[:, scol + 1:scol + 2], ALU.mult,
                        waits=[ev_l, ev_bi] + ev_sl[-1:] +
                        [kb.bz("sqdum"), kb.bz("tmplo")], sig=True)
            ec = kb.cp("dve", WI32[:, 0:DM], Ta, waits=[e1], sig=True)
            eb_ = kb.tt(Tb, Ta, WI32[:, 0:DM], ALU.subtract, waits=[ec],
                        sig=True)
            ed_ = kb.ts1(Ta, Tb, 0.5, ALU.is_gt, waits=[eb_], sig=True)
            ee_ = kb.tt(Ta, Ta, WI32[:, 0:DM], ALU.add, waits=[ed_], sig=True)
            ef_ = kb.ts1(Tb, Tb, -0.5, ALU.is_lt, waits=[ee_], sig=True)
            e2 = kb.tt(Ta, Ta, Tb, ALU.subtract, waits=[ef_], sig=True)
            kb.setbz("sqdum", e2)
            kb.setbz("tmplo", e2)
            for h in range(2):
                sl = g * 2 + h
                pa = PA[sl % 2]
                evm = kb.mm(pa[:, 0:512], bt_sb[0:16, rc * 128:(rc + 1) * 128],
                            a_sb[0:16, sl * 512:(sl + 1) * 512], True, True,
                            waits=[evs_bt[rc], ev_a2, kb.pa_free[sl % 2]],
                            sig=True)
                eo = kb.stt(qtmp[:, sl * 512:(sl + 1) * 512],
                            T["TMP32"][:, h * 512:(h + 1) * 512],
                            SCL[:, scol:scol + 1], pa[:, 0:512],
                            ALU.mult, ALU.add,
                            waits=[e2, evm, ev_bs,
                                   getattr(kb, "qtmp_busy", None)],
                            sig=True)
                kb.pa_free[sl % 2] = eo
                ev_sl.append(eo)
        ev_wfree = ev_sl[-1]
        nct = cols // 128
        for g0 in range(0, nct, 8):
            gn = min(8, nct - g0)
            bank = (g0 // 8) % 2
            last = None
            for jj in range(gn):
                i = g0 + jj
                last = kb.tr(PT[bank][:, jj * 128:(jj + 1) * 128],
                             qtmp[:, i * 128:(i + 1) * 128],
                             waits=([ev_sl[-1], kb.pt_free[bank], kb.ev_iden]
                                    if jj == 0 else []),
                             sig=(jj == gn - 1))
            ev_e = kb.cp("dve", qstg[:, 0:gn * 128], PT[bank][:, 0:gn * 128],
                         waits=[last, getattr(kb, "qstg_busy", None)], sig=True)
            kb.pt_free[bank] = ev_e
            if which == "fc":
                dst = stash_fc[li, g0:g0 + gn, :, rc * 128:(rc + 1) * 128]
                dst = dst.transpose([1, 0, 2])
                ev_d = kb.dma("pop", dst,
                              qstg[:, 0:gn * 128].rearrange("p (g c) -> p g c",
                                                            g=gn),
                              waits=[ev_e])
            else:
                dst = stash_pj[li, rc, g0:g0 + gn, :, :].transpose([1, 0, 2])
                ev_d = kb.dma("pop", dst,
                              qstg[:, 0:gn * 128].rearrange("p (g c) -> p g c",
                                                            g=gn),
                              waits=[ev_e])
            kb.qstg_busy = ev_d
        kb.qtmp_busy = ev_sl[-1]
    kb.wld_free = ev_wfree
    kb.wprep[f"{lay}_{which}"] = kb.qstg_busy



def sub_ffn(kb, s, li):
    T, dram = kb.t, kb.dram
    lay = f"L{li}"
    AR, PA, PT = T["ARENA"], T["PA"], T["PT"]
    FCW = AR[:, 0:32768]
    PJW = AR[:, 32768:65536]
    HB = AR[:, 65536:73728]
    XS = AR[:, 73728:75776]
    YS = AR[:, 75776:77824]
    ev_fc = kb.wprep[f"{lay}_fc"]
    ev_pj = kb.wprep[f"{lay}_proj"]
    bfc = kb.wprep[f"{lay}_ffn_b_fc"]
    bpj = kb.wprep[f"{lay}_ffn_b_proj"]

    wl = []
    free = getattr(kb, "ffnw_free", []) + [kb.bz("sub_exit")]
    for dc in range(NDC):
        wl.append(kb.dma("sp", FCW[:, dc * 4096:(dc + 1) * 4096],
                         dram["wst_fc"][li, dc], waits=[ev_fc] + free,
                         tag=f"fcw{dc}"))
    for d2 in range(NDC):
        wl.append(kb.dma("sp",
                         PJW[:, d2 * 4096:(d2 + 1) * 4096].rearrange(
                             "p (f c) -> p f c", f=32),
                         dram["wst_pj"][li, d2].transpose([1, 0, 2]),
                         waits=[ev_pj] + free, tag=f"pjw{d2}"))
    ev_w = wl
    if FFN_STAGE <= 1:
        kb.ffnw_free = list(wl)
        return
    if s == 0 and li == 0:
        kb.dbg("fcw0", FCW[:, 0:32768], waits=list(ev_w[:NDC]))
        kb.dbg("pjw0", PJW[:, 0:4096], waits=[ev_w[NDC]])

    for tt in range(8):
        ev_x = []
        for half in range(2):
            j = tt * 2 + half
            stg = T["XTT"][:, (j % 2) * DM:(j % 2 + 1) * DM]
            ev_ln = ln_tile(kb, j, stg, waits=[kb.xtt_busy(j % 2)])
            if FFN_STAGE == 16:
                kb.set_xtt_busy(j % 2, ev_ln)
                ev_x.append(ev_ln)
                continue
            bank = j % 2
            last = None
            for k in range(8):
                last = kb.tr(PT[bank][:, k * 128:(k + 1) * 128],
                             stg[:, k * 128:(k + 1) * 128],
                             waits=[ev_ln, kb.pt_free[bank],
                                    getattr(kb, "pt_free_act", None),
                                    kb.ev_iden]
                             if k == 0 else [], sig=(k == 7), tag=f"fx{j}")
            dst = XS.rearrange("p (k c) -> p k c", c=256)[
                :, :, half * 128:half * 128 + 128]
            e = kb.cp("dve" if bank == 0 else "act", dst, PT[bank][:, 0:1024],
                      waits=[last, getattr(kb, "xs_busy", None),
                             kb.bz("xs_dbg")], sig=True)
            kb.pt_free[bank] = e
            kb.set_xtt_busy(j % 2, e)
            ev_x.append(e)
        if s == 0 and li == 0 and tt == 0:
            ed = kb.dbg("xs0", XS[:, 0:2048], waits=ev_x)
            if ed:
                kb.setbz("xs_dbg", ed)
        if FFN_STAGE in (16, 17):
            continue
        if FFN_STAGE == 15:
            kb.xs_busy = ev_x[-2]
            continue
        if not hasattr(kb, "fc_slot_free"):
            kb.fc_slot_free = [None] * 4
        ev_h = [None] * 32
        for fc_ in range(32):
            slot = fc_ % 4
            pa = PA[slot]
            po = 0
            evm = None
            for dc in range(NDC):
                w = [ev_w[dc]] + ([kb.fc_slot_free[slot]] + ev_x if dc == 0 else [])
                evm = kb.mm(pa[:, po:po + 256],
                            FCW[:, dc * 4096 + fc_ * 128:
                                dc * 4096 + (fc_ + 1) * 128],
                            XS[:, dc * 256:(dc + 1) * 256],
                            dc == 0, dc == NDC - 1, waits=w,
                            sig=(dc == NDC - 1), tag=f"fc{tt}.{fc_}")
            eb = kb.act(HB[:, fc_ * 256:(fc_ + 1) * 256], pa[:, po:po + 256],
                        AF.Gelu, bias=T["BIAS"][:, bfc[1] + fc_:bfc[1] + fc_ + 1],
                        waits=[evm, bfc[0]], sig=True, tag=f"gl{tt}.{fc_}")
            kb.fc_slot_free[slot] = eb
            ev_h[fc_] = eb
            if fc_ == 31:
                kb.xs_busy = evm
                last_fc_mm = evm
        if s == 0 and li == 0 and tt == 0:
            ed = kb.dbg("hb0", HB[:, 0:8192], waits=list(ev_h))
            if ed:
                for q in range(4):
                    kb.fc_slot_free[q] = ed
        if FFN_STAGE <= 2:
            continue
        ev_y = []
        for d2 in range(NDC):
            pa = PA[4 + (d2 % 2)]
            evm = None
            for fc_ in range(32):
                w = [ev_w[NDC + d2], ev_h[fc_]]
                if fc_ == 0:
                    w.append(kb.pa_free[4 + (d2 % 2)])
                evm = kb.mm(pa[:, 0:256],
                            PJW[:, d2 * 4096 + fc_ * 128:
                                d2 * 4096 + (fc_ + 1) * 128],
                            HB[:, fc_ * 256:(fc_ + 1) * 256], fc_ == 0,
                            fc_ == 31, waits=w, sig=(fc_ == 31),
                            tag=f"pj{tt}.{d2}")
            ey = kb.ets1b(4 + (d2 % 2), YS[:, d2 * 256:(d2 + 1) * 256],
                          pa[:, 0:256],
                          T["BIAS"][:, bpj[1] + d2:bpj[1] + d2 + 1],
                          waits=[evm, bpj[0], getattr(kb, "ys_busy", None),
                                 kb.bz("ys_dbg")],
                          sig=True, tag=f"pje{tt}.{d2}")
            kb.pa_free[4 + (d2 % 2)] = ey
            ev_y.append(ey)
            if d2 == NDC - 1:
                last_pj_mm = evm
        if s == 0 and li == 0 and tt == 0:
            ed = kb.dbg("ys0", YS[:, 0:2048], waits=list(ev_y))
            if ed:
                kb.setbz("ys_dbg", ed)
        if FFN_STAGE <= 3:
            continue
        for half in range(2):
            j = tt * 2 + half

            def piece(k, half=half):
                return YS[:, k * 256 + half * 128: k * 256 + half * 128 + 128]
            er, eq = post_sub_tile(kb, j, piece, waits=ev_y, tag=f"fp{tt}")
            kb.ys_busy = er
    if FFN_STAGE >= 4 and FFN_STAGE not in (15, 16, 17):
        finish_stats(kb, [eq])
        kb.setbz("sub_exit", er)
    if FFN_STAGE in (15, 16, 17):
        kb.ffnw_free = list(wl)
    elif FFN_STAGE == 2:
        kb.ffnw_free = [last_fc_mm]
    else:
        kb.ffnw_free = [last_fc_mm, last_pj_mm]



def prep_l0_attn(kb):
    P, T, dram = kb.P, kb.t, kb.dram
    AR, PA, PT = T["ARENA"], T["PA"], T["PT"]
    ARF = AR[:].bitcast(F32)
    ROWS, SCL = T["ROWS"], T["SCL"]

    e_dt = kb.dma("sp", ROWS[0:1, 0:1],
                  dram["L0_attn_log_dt"].rearrange("(a b) -> a b", a=1),
                  waits=[kb.bz("rows_low")])
    e_dt2 = kb.act(ROWS[0:1, 1:2], ROWS[0:1, 0:1], AF.Exp, waits=[e_dt], sig=True)
    dtc = kb.scl_cursor
    kb.scl_cursor += 1
    ev_dtb = kb.bcast_val(ROWS[0:1, 1:2], SCL[:, dtc:dtc + 1], waits=[e_dt2])
    e_l1 = kb.dma("sp", ROWS[0:64, 2:3],
                  dram["L0_attn_log_Lam_real"].rearrange("(n o) -> n o", o=1),
                  waits=[kb.bz("rows_low")])
    e_l2 = kb.dma("sp", ROWS[0:64, 3:4],
                  dram["L0_attn_Lam_imag"].rearrange("(n o) -> n o", o=1),
                  waits=[kb.bz("rows_low")])
    e_m = kb.act(ROWS[0:64, 4:5], ROWS[0:64, 2:3], AF.Exp, waits=[e_l1], sig=True)
    kb.ev_lnm = kb.ts2(ROWS[0:64, 4:5], ROWS[0:64, 4:5],
                       SCL[0:64, dtc:dtc + 1], -1.0, ALU.mult, ALU.mult,
                       waits=[e_m, ev_dtb], sig=True)
    kb.ev_th2 = kb.ts2(ROWS[0:64, 5:6], ROWS[0:64, 3:4],
                       SCL[0:64, dtc:dtc + 1], 1.0 / (2.0 * np.pi),
                       ALU.mult, ALU.mult, waits=[e_l2, ev_dtb], sig=True)

    WLD = ARF[:, CR // 2:CR // 2 + 4096]
    stg_bf = AR[:, WD0:WD0 + 1024]
    ev_w = getattr(kb, "wld_free", None)
    evc = None
    for comp, nm in ((0, "L0_attn_B_real"), (1, "L0_attn_B_imag")):
        e1 = kb.dma("sp", WLD[0:64, 0:1024], dram[nm], waits=[ev_w])
        e2 = kb.ts1(stg_bf[0:64, 0:1024], WLD[0:64, 0:1024],
                    SCL[0:64, dtc:dtc + 1], ALU.mult,
                    waits=[e1, ev_dtb, kb.bz("bstg")], sig=True)
        evt = None
        for dc in range(NDC):
            evt = kb.tr(PT[0][:, dc * 64:dc * 64 + 64],
                        stg_bf[0:64, dc * 128:(dc + 1) * 128],
                        waits=([e2, kb.pt_free[0], kb.ev_iden] if dc == 0 else []),
                        sig=(dc == NDC - 1))
        evc = kb.cp("dve",
                    T["BT_sb"].rearrange("p (c w) -> p c w", w=128)[
                        :, :, comp * 64:comp * 64 + 64],
                    PT[0][:, 0:512].rearrange("p (c w) -> p c w", w=64),
                    waits=[evt], sig=True)
        kb.pt_free[0] = evc
        kb.setbz("bstg", evc)
        ev_w = e2
    kb.ev_bt = evc

    for comp, nm in ((0, "L0_attn_C_real"), (1, "L0_attn_C_imag")):
        dstld = WLD[:, 0:512].rearrange("p (hc n) -> p hc n", n=64)
        srcld = dram[nm].rearrange("(hc p) n -> p hc n", p=128)
        e1 = kb.dma("sp", dstld, srcld, waits=[ev_w])
        sgn = 1.0 if comp == 0 else -1.0
        e2 = kb.ts1(stg_bf[:, 0:512], WLD[:, 0:512], sgn, ALU.mult,
                    waits=[e1, kb.bz("bstg")], sig=True)
        evt = None
        for hc in range(8):
            evt = kb.tr(PT[0][0:64, hc * 128:hc * 128 + 128],
                        stg_bf[:, hc * 64:hc * 64 + 64],
                        waits=([e2, kb.pt_free[0]] if hc == 0 else []),
                        sig=(hc == 7))
        evc = kb.cp("dve", T["CT_sb"][comp * 64:comp * 64 + 64, :],
                    PT[0][0:64, 0:1024], waits=[evt], sig=True)
        kb.pt_free[0] = evc
        kb.setbz("bstg", e2)
        ev_w = e2
    kb.ev_ct = evc
    kb.wld_free = ev_w

    W = dram["L0_attn_W_out"]
    qtmp = AR[:, WD0:WD0 + 1024]
    qstg = AR[:, GR:GR + 1024]
    ev_d = None
    for rc in range(8):
        ev_l = kb.dma("sp", WLD[:, 0:1024], W[rc * 128:(rc + 1) * 128, :],
                      waits=[kb.wld_free])
        e2 = kb.cp("dve", qtmp[:, 0:1024], WLD[:, 0:1024],
                   waits=[ev_l, kb.bz("bstg")], sig=True)
        kb.wld_free = e2
        kb.setbz("bstg", e2)
        last = None
        for jj in range(8):
            last = kb.tr(PT[1][:, jj * 128:(jj + 1) * 128],
                         qtmp[:, jj * 128:(jj + 1) * 128],
                         waits=([e2, kb.pt_free[1]] if jj == 0 else []),
                         sig=(jj == 7))
        ev_e = kb.cp("dve", qstg[:, 0:1024], PT[1][:, 0:1024],
                     waits=[last, kb.bz("qstg2")], sig=True)
        kb.pt_free[1] = ev_e
        dst = dram["wst_wout"][0:8, :, rc * 128:(rc + 1) * 128].transpose(
            [1, 0, 2])
        ev_d = kb.dma("pop", dst,
                      qstg[:, 0:1024].rearrange("p (g c) -> p g c", g=8),
                      waits=[ev_e])
        kb.setbz("qstg2", ev_d)
    kb.ev_wout = ev_d

    kb.wprep["L0_b_out"] = (kb.load_bias(dram["L0_attn_b_out"],
                                         kb.bias_cursor, 8), kb.bias_cursor)
    kb.bias_cursor += 8
    kb.wprep["L0_D"] = (kb.load_bias(dram["L0_attn_D"], kb.bias_cursor, 8),
                        kb.bias_cursor)
    kb.bias_cursor += 8

    prep_scan_tables(kb)


def diag_ap(ddiag, c):
    return bass.AP(ddiag.tensor, c * 65536, [[1040, 64], [1, 16]])


def rr_sin(kb, dst, u, gi, scr, waits, n=2048):
    return rr_sin2(kb, dst, u, gi, scr, waits, n=n, npart=64)


def prep_scan_tables(kb):
    P, T, dram = kb.P, kb.t, kb.dram
    AR = T["ARENA"]
    ARF = AR[:].bitcast(F32)
    ROWS = T["ROWS"]
    lnm = ROWS[0:64, 4:5]
    th2 = ROWS[0:64, 5:6]
    base = CR // 2
    G_d = ARF[:, base:base + 2048]
    G_p = ARF[:, base + 2048:base + 4096]
    G_u = ARF[:, base + 4096:base + 6144]
    G_i = AR[:].bitcast(I32)[:, base + 6144:base + 8192]
    G_e = ARF[:, base + 8192:base + 10240]
    G_s = ARF[:, base + 10240:base + 12288]
    G_c = ARF[:, base + 12288:base + 14336]
    MRB = AR[:, GR + 1024:GR + 3072]
    MIB = AR[:, GR + 3072:GR + 5120]

    w0 = [kb.wld_free, kb.bz("bstg"), kb.bz("qstg2"), kb.ev_lnm, kb.ev_th2]
    d_i = None
    for ch in range(8):
        q0 = ch * 16
        e_d = P.rec("pop", lambda e, q0=q0: e.iota(
            G_d[0:64, 0:2048], [[-1, 16], [1, 128]], base=-q0,
            channel_multiplier=0, allow_small_or_imprecise_dtypes=True),
            waits=w0 + [kb.bz("gd")], sig=True)
        e_p = kb.ts2(G_p[0:64, :], G_d[0:64, :], 0.0, -1e30, ALU.is_lt,
                     ALU.mult, waits=[e_d, kb.bz("gp")], sig=True)
        e_a = kb.stt(G_e[0:64, :], G_d[0:64, :], lnm, G_p[0:64, :], ALU.mult,
                     ALU.add, waits=[e_p, kb.bz("ge")], sig=True)
        e_E = kb.act(G_e[0:64, :], G_e[0:64, :], AF.Exp, waits=[e_a], sig=True)
        e_u = kb.ts1(G_u[0:64, :], G_d[0:64, :], th2, ALU.mult,
                     waits=[e_d, kb.bz("gu")], sig=True)
        e_s = rr_sin(kb, G_s, G_u, G_i, G_p, [e_u, e_p])
        e_u2 = kb.ts1(G_u[0:64, :], G_u[0:64, :], 0.25, ALU.add, waits=[e_s],
                      sig=True)
        e_c = rr_sin(kb, G_c, G_u, G_i, G_p, [e_u2])
        kb.setbz("gu", e_c)
        kb.setbz("gp", e_c)
        e_mr = kb.tt(MRB[0:64, 0:2048], G_e[0:64, :], G_c[0:64, :], ALU.mult,
                     waits=[e_E, e_c, kb.bz("mrb")], sig=True)
        e_mi = kb.tt(MIB[0:64, 0:2048], G_e[0:64, :], G_s[0:64, :], ALU.mult,
                     waits=[e_E, e_s, kb.bz("mib")], sig=True)
        kb.setbz("ge", e_mi)
        kb.setbz("gd", e_mi)
        d_r = kb.dma("pop", dram["wst_scan"][:, 0, q0:q0 + 16, :],
            MRB[0:64, 0:2048].rearrange("n (q p) -> n q p", p=128),
            waits=[e_mr])
        d_i = kb.dma("pop", dram["wst_scan"][:, 1, q0:q0 + 16, :],
            MIB[0:64, 0:2048].rearrange("n (q p) -> n q p", p=128),
            waits=[e_mi])
        kb.setbz("mrb", d_r)
        kb.setbz("mib", d_i)
        w0 = []
    kb.ev_scan_st = d_i

    e_d = P.rec("pop", lambda e: e.iota(
        G_d[0:64, 0:128], [[1, 128]], base=1, channel_multiplier=0,
        allow_small_or_imprecise_dtypes=True),
        waits=[kb.bz("gd"), kb.bz("gu"), kb.bz("gp"), kb.bz("ge")], sig=True)
    e_a = kb.ts1(G_e[0:64, 0:128], G_d[0:64, 0:128], lnm, ALU.mult,
                 waits=[e_d], sig=True)
    e_E = kb.act(G_e[0:64, 0:128], G_e[0:64, 0:128], AF.Exp, waits=[e_a],
                 sig=True)
    e_u = kb.ts1(G_u[0:64, 0:128], G_d[0:64, 0:128], th2, ALU.mult,
                 waits=[e_d], sig=True)
    e_s = rr_sin(kb, G_s, G_u, G_i, G_p, [e_u], n=128)
    e_u2 = kb.ts1(G_u[0:64, 0:128], G_u[0:64, 0:128], 0.25, ALU.add,
                  waits=[e_s], sig=True)
    e_c = rr_sin(kb, G_c, G_u, G_i, G_p, [e_u2], n=128)
    eqr = kb.tt(T["QPR"][:, 0:128], G_e[0:64, 0:128], G_c[0:64, 0:128],
                ALU.mult, waits=[e_E, e_c], sig=True)
    eqi = kb.tt(T["QPI"][:, 0:128], G_e[0:64, 0:128], G_s[0:64, 0:128],
                ALU.mult, waits=[e_E, e_s], sig=True)
    eqm = kb.ts1(T["QPMI"][:, 0:128], T["QPI"][:, 0:128], -1.0, ALU.mult,
                 waits=[eqi], sig=True)
    eq1 = kb.cp("dve", T["Q128"][:, 0:1], T["QPR"][0:64, 127:128],
                waits=[eqr], sig=True)
    eq2 = kb.cp("dve", T["Q128"][:, 1:2], T["QPI"][0:64, 127:128],
                waits=[eqi], sig=True)
    eq3 = kb.ts1(T["Q128"][:, 3:4], T["Q128"][:, 1:2], -1.0, ALU.mult,
                 waits=[eq2], sig=True)
    kb.ev_qp = [eqr, eqi, eqm, eq1, eq2, eq3]
    kb.setbz("cr_free", eq3)
    kb.setbz("cr_free2", eqm)


def sub_l0attn(kb, s):
    T, dram = kb.t, kb.dram
    AR, PA, PT = T["ARENA"], T["PA"], T["PT"]
    XF, ev_xf = make_xf(kb, XFR, waits=[kb.bz("xf_w")])
    BUS = AR[:, VFR:VFR + 2048]
    BUT = AR[:, VFR + 2048:VFR + 5120]
    SSB = AR[:, VFR + 5120:VFR + 7168]
    STb = AR[:, VFR + 7168:VFR + 9216]
    YMIX = AR[:, GR:GR + 16384]
    WOUTB = AR[:, CR:CR + 8192]
    MD = AR[:, CR + 8192:CR + 12288]
    bD = kb.wprep["L0_D"]
    bO = kb.wprep["L0_b_out"]

    ev_wo = []
    for dc in range(NDC):
        ev_wo.append(kb.dma("sp", WOUTB[:, dc * 1024:(dc + 1) * 1024],
                            dram["wst_wout"][dc],
                            waits=[kb.ev_wout, kb.bz("woutb"),
                                   kb.bz("cr_free"), kb.bz("cr_free2"),
                                   kb.bz("sub_exit")],
                            tag=f"wo{dc}"))

    ev_bus = []
    for tsl in range(4):
        pa = PA[4 + tsl % 2]
        evm = None
        for dc in range(NDC):
            w = [kb.ev_bt]
            if dc == 0:
                w += [kb.pa_free[4 + tsl % 2]] + list(ev_xf)
            evm = kb.mm(pa[:, 0:512], T["BT_sb"][:, dc * 128:(dc + 1) * 128],
                        XF[:, dc * 2048 + tsl * 512: dc * 2048 + tsl * 512 + 512],
                        dc == 0, dc == NDC - 1, waits=w, sig=(dc == NDC - 1),
                        tag=f"bu{tsl}")
        eb = kb.ecp(4 + tsl % 2, BUS[:, tsl * 512:(tsl + 1) * 512],
                    pa[:, 0:512], waits=[evm, kb.bz("bus")], sig=True)
        kb.pa_free[4 + tsl % 2] = eb
        ev_bus.append(eb)

    ev_bt2 = []
    for g0 in range(0, NCH, 8):
        bank = (g0 // 8) % 2
        last = None
        for jj in range(8):
            j = g0 + jj
            last = kb.tr(PT[bank][:, jj * 128:(jj + 1) * 128],
                         BUS[:, j * 128:(j + 1) * 128],
                         waits=(ev_bus + [kb.pt_free[bank], kb.ev_iden])
                         if jj == 0 else [], sig=(jj == 7))
        dst = BUT.rearrange("p (c w) -> p c w", w=192)[:, g0:g0 + 8, 0:128]
        e = kb.ecp(bank, dst, PT[bank][:, 0:1024],
                   waits=[last, kb.bz("but")], sig=True)
        kb.pt_free[bank] = e
        ev_bt2.append(e)
    butv = BUT.rearrange("p (c w) -> p c w", w=192)
    ev_bneg = kb.ts1(butv[:, :, 128:192], butv[:, :, 64:128], -1.0, ALU.mult,
                     waits=list(ev_bt2), sig=True)
    kb.setbz("but", ev_bneg)
    kb.setbz("bus", ev_bt2[-1])

    ev_md = [None] * NST

    def md_load(n, extra=()):
        s = n % 16
        dst = MD[:, s * 256:(s + 1) * 256].rearrange("p (c k) -> p c k",
                                                     k=128)
        w0 = [kb.ev_scan_st] + list(extra)
        if n == 0:
            w0 += [kb.bz("cr_free"), kb.bz("cr_free2"), kb.bz("sub_exit")]
        ev_md[n] = kb.dma("sp", dst, dram["wst_scan"][n].transpose([1, 0, 2]),
                          waits=w0, tag=f"md{n}")
    for n in range(16):
        md_load(n)
    bank_last = [None] * 4
    for n in range(NST):
        s = n % 16
        md0 = MD[:, s * 256:s * 256 + 128]
        md1 = MD[:, s * 256 + 128:s * 256 + 256]
        b = n // 16
        pa = PA[b]
        off = (n * 32) % 512
        butr = butv[:, :, n:n + 1].rearrange("p c o -> p (c o)")
        buti = butv[:, :, 64 + n:65 + n].rearrange("p c o -> p (c o)")
        butm = butv[:, :, 128 + n:129 + n].rearrange("p c o -> p (c o)")
        w0 = [ev_md[n], ev_bneg]
        if n % 16 == 0:
            w0 += [kb.pa_free[b],
                   kb.fc_slot_free[b] if hasattr(kb, "fc_slot_free") else None]
        kb.mm(pa[:, off:off + 16], md0, butr, True, False, waits=w0,
              tag=f"sc{n}")
        kb.mm(pa[:, off:off + 16], md1, butm, False, True, waits=[],
              tag=f"sc{n}b")
        kb.mm(pa[:, off + 16:off + 32], md0, buti, True, False, waits=[],
              tag=f"sc{n}c")
        e4 = kb.mm(pa[:, off + 16:off + 32], md1, butr, False, True, waits=[],
                   sig=True, tag=f"sc{n}d")
        if n + 16 < NST:
            md_load(n + 16, extra=[e4])
        if n % 16 == 15:
            bank_last[b] = e4

    ev_cp = []
    for b in range(4):
        e = kb.cp("dve", T["TMP32"][96:128, b * 512:(b + 1) * 512],
                  PA[b][96:128, 0:512],
                  waits=[bank_last[b], kb.bz("tmplo"), kb.bz("sqdum")],
                  sig=True)
        ev_cp.append(e)
    kb.setbz("tmplo", ev_cp[-1])
    kb.setbz("sqdum", ev_cp[-1])
    srcv = T["TMP32"][127:128, 0:2048].rearrange("o (n c j) -> o n c j",
                                                 c=2, j=16)
    e_cr = kb.dma("pop", T["CRWR"][:, 0:16].unsqueeze(0) if False else
                  T["CRWR"][:, 0:16], srcv[0:1, :, 0, :], waits=ev_cp)
    e_ci = kb.dma("pop", T["CRWI"][:, 0:16], srcv[0:1, :, 1, :], waits=ev_cp)

    Q = T["Q128"]
    em1 = kb.P.rec("dve", lambda e: e.memset(T["CCR"][:, 0:1], 0.0),
                   waits=[e_cr, kb.bz("ccr")], sig=True)
    em2 = kb.P.rec("dve", lambda e: e.memset(T["CCI"][:, 0:1], 0.0),
                   waits=[e_ci, kb.bz("ccr")], sig=True)
    prev = [em1, em2] + kb.ev_qp
    for j in range(1, NCH):
        t1 = kb.stt(T["TMP32"][0:64, 0:1], T["CCR"][:, j - 1:j], Q[:, 0:1],
                    T["CRWR"][:, j - 1:j], ALU.mult, ALU.add, waits=prev,
                    sig=True)
        e1 = kb.stt(T["CCR"][:, j:j + 1], T["CCI"][:, j - 1:j], Q[:, 3:4],
                    T["TMP32"][0:64, 0:1], ALU.mult, ALU.add, waits=[t1],
                    sig=True)
        t2 = kb.stt(T["TMP32"][0:64, 1:2], T["CCR"][:, j - 1:j], Q[:, 1:2],
                    T["CRWI"][:, j - 1:j], ALU.mult, ALU.add, waits=[e1],
                    sig=True)
        e2 = kb.stt(T["CCI"][:, j:j + 1], T["CCI"][:, j - 1:j], Q[:, 0:1],
                    T["TMP32"][0:64, 1:2], ALU.mult, ALU.add, waits=[t2],
                    sig=True)
        prev = [e2]
    ecb1 = kb.cp("dve", T["CCB"][:, 0:16], T["CCR"][:, 0:16], waits=prev,
                 sig=True)
    ecb2 = kb.cp("dve", T["CCB"][:, 16:32], T["CCI"][:, 0:16], waits=[ecb1],
                 sig=True)
    kb.setbz("ccr", ecb2)

    DIAGS = AR[:, VFR + 9216:VFR + 11264]
    e_z = kb.P.rec("dve", lambda e: e.memset(DIAGS[0:64, 0:2048], 0.0),
                   waits=[kb.bz("diags"), kb.bz("sub_exit")], sig=True)
    e_d1 = kb.dma("pop", dram["dscr_c"], T["CCB"][:, 0:32], waits=[ecb2])
    e_d2 = kb.dma("pop", dram["ddiag"][0], DIAGS[0:64, 0:1024], waits=[e_z])
    e_d3 = kb.dma("pop", dram["ddiag"][1], DIAGS[0:64, 1024:2048], waits=[e_z])
    e_w1 = kb.P.rec("pop", lambda e: e.dma_start(
        out=diag_ap(dram["ddiag"], 0), in_=dram["dscr_c"][:, 0:16]),
        waits=[e_d1, e_d2], dma=True)
    e_w2 = kb.P.rec("pop", lambda e: e.dma_start(
        out=diag_ap(dram["ddiag"], 1), in_=dram["dscr_c"][:, 16:32]),
        waits=[e_d1, e_d3], dma=True)
    e_l1 = kb.dma("pop", DIAGS[0:64, 0:1024], dram["ddiag"][0], waits=[e_w1])
    e_l2 = kb.dma("pop", DIAGS[0:64, 1024:2048], dram["ddiag"][1], waits=[e_w2])
    kb.setbz("diags", None)

    ev_ssb = []
    e4 = None
    for b in range(4):
        pa_c = PA[4 + b % 2]
        dcr = DIAGS[0:64, 256 * b:256 * b + 256]
        dci = DIAGS[0:64, 1024 + 256 * b:1024 + 256 * b + 256]
        outr, outi = pa_c[:, 0:256], pa_c[:, 256:512]
        w0 = [e_l1, e_l2, kb.ev_qp[0], kb.ev_qp[1], kb.ev_qp[2],
              kb.pa_free[4 + b % 2]]
        kb.mm(outr, T["QPR"][0:64, :], dcr, True, False, waits=w0,
              tag=f"cyr{b}")
        kb.mm(outi, T["QPI"][0:64, :], dcr, False, False, waits=[],
              tag=f"cyi{b}")
        kb.mm(outr, T["QPMI"][0:64, :], dci, False, False, waits=[],
              tag=f"cyr{b}b")
        e4 = kb.mm(outi, T["QPR"][0:64, :], dci, False, True, waits=[],
                   sig=True, tag=f"cyi{b}b")
        dst3 = SSB.rearrange("p (ch co nn) -> p nn co ch", co=2, nn=64)[
            :, 16 * b:16 * b + 16, :, :]
        e0 = kb.cp("dve", dst3, PA[b][:, 0:512].rearrange(
            "p (nn co ch) -> p nn co ch", nn=16, co=2),
            waits=[bank_last[b], kb.bz("ssb")], sig=True)
        e = kb.tt(dst3, dst3,
                  pa_c[:, 0:512].rearrange("p (co nn ch) -> p nn co ch",
                                           nn=16, co=2),
                  ALU.add, waits=[e0, e4], sig=True)
        ev_ssb.append(e)
        kb.pa_free[b] = e
        kb.pa_free[4 + b % 2] = e
        if hasattr(kb, "fc_slot_free"):
            kb.fc_slot_free[b] = e
    kb.setbz("ssb", ev_ssb[-1])
    kb.setbz("diags", e4)

    ev_st = []
    for g0 in range(0, NCH, 8):
        bank = (g0 // 8) % 2
        last = None
        for jj in range(8):
            j = g0 + jj
            last = kb.tr(PT[bank][:, jj * 128:(jj + 1) * 128],
                         SSB[:, j * 128:(j + 1) * 128],
                         waits=(ev_ssb + [kb.pt_free[bank]]) if jj == 0 else [],
                         sig=(jj == 7))
        e = kb.ecp(bank, STb[:, g0 * 128:(g0 + 8) * 128],
                   PT[bank][:, 0:1024], waits=[last, kb.bz("st")],
                   sig=True)
        kb.pt_free[bank] = e
        ev_st.append(e)
    kb.setbz("st", ev_st[-1])

    ev_ymix = []
    for hc in range(NDC):
        for tsl in range(4):
            g = hc * 4 + tsl
            pb = PA[4 + g % 2]
            evm = kb.mm(pb[:, 0:512], T["CT_sb"][:, hc * 128:(hc + 1) * 128],
                        STb[:, tsl * 512:(tsl + 1) * 512], True, True,
                        waits=[kb.ev_ct, kb.pa_free[4 + g % 2]] + ev_st,
                        sig=True, tag=f"cp{hc}.{tsl}")
            ey = kb.stt(YMIX[:, hc * 2048 + tsl * 512:
                             hc * 2048 + tsl * 512 + 512],
                        XF[:, hc * 2048 + tsl * 512:
                           hc * 2048 + tsl * 512 + 512],
                        T["BIAS"][:, bD[1] + hc:bD[1] + hc + 1], pb[:, 0:512],
                        ALU.mult, ALU.add,
                        waits=[evm, bD[0], kb.bz("ymix")], sig=True,
                        tag=f"ym{hc}.{tsl}")
            kb.pa_free[4 + g % 2] = ey
            ev_ymix.append(ey)

    ev_yout = []
    for oc in range(NDC):
        for tsl in range(4):
            g = oc * 4 + tsl
            pa = PA[4 + g % 2]
            evm = None
            for dc in range(NDC):
                w = [ev_wo[dc], ev_ymix[dc * 4 + tsl]]
                if dc == 0:
                    w.append(kb.pa_free[4 + g % 2])
                evm = kb.mm(pa[:, 0:512],
                            WOUTB[:, dc * 1024 + oc * 128:
                                  dc * 1024 + (oc + 1) * 128],
                            YMIX[:, dc * 2048 + tsl * 512:
                                 dc * 2048 + tsl * 512 + 512],
                            dc == 0, dc == NDC - 1, waits=w,
                            sig=(dc == NDC - 1), tag=f"wm{oc}.{tsl}")
            ey = kb.ets1b(4 + g % 2, XF[:, oc * 2048 + tsl * 512:
                                        oc * 2048 + tsl * 512 + 512],
                          pa[:, 0:512],
                          T["BIAS"][:, bO[1] + oc:bO[1] + oc + 1],
                          waits=[evm, bO[0], ev_ymix[-1]], sig=True,
                          tag=f"yo{oc}.{tsl}")
            kb.pa_free[4 + g % 2] = ey
            ev_yout.append(ey)
    kb.setbz("ymix", ev_yout[-1])
    kb.setbz("woutb", ev_yout[-1])

    er = eq = None
    for j in range(NTIL):
        def piece(k, j=j):
            return XF[:, k * 2048 + j * 128: k * 2048 + (j + 1) * 128]
        er, eq = post_sub_tile(kb, j, piece, waits=ev_yout, tag="l0p")
        kb.setbz("xf_w", er)
    finish_stats(kb, [eq])
    kb.setbz("sub_exit", er)
    kb.ffnw_free = [er]


CV_VS = 0
CV_AS = 2048
CV_AT = 10240
CV_XS = 2048
CV_Y = 6272
CV_BS = 0
CV_YFR = 4224
CV_BT = 12416
CV_YCH = 12928
CV_H = 10496


def arena_ap(kb, col, dims, npart=128):
    return bass.AP(kb.t["ARENA"][:].tensor, col, [[ARN, npart]] + dims)


def conv_fwd_chunk(kb, VFc, XB, ev_v, tag):
    T = kb.t
    AR, PA, PT = T["ARENA"], T["PA"], T["PT"]
    VS = AR[:, XB + CV_VS:XB + CV_VS + 2048]
    AS = AR[:, XB + CV_AS:XB + CV_AS + 8192]
    ATb = AR[:, XB + CV_AT:XB + CV_AT + 4224]
    ev_vs = []
    for g0 in range(0, 16, 8):
        bank = (g0 // 8) % 2
        last = None
        for jj in range(8):
            g = g0 + jj
            last = kb.tr(PT[bank][:, jj * 128:(jj + 1) * 128],
                         VFc[:, g * 128:(g + 1) * 128],
                         waits=([kb.pt_free[bank], kb.ev_iden] + list(ev_v))
                         if jj == 0 else [], sig=(jj == 7), tag=f"{tag}it")
        e = kb.ecp(bank, VS[:, g0 * 128:(g0 + 8) * 128],
                   PT[bank][:, 0:1024],
                   waits=[last, kb.bz("cv_vs")], sig=True)
        kb.pt_free[bank] = e
        ev_vs.append(e)
    kb.setbz("cv_vs", ev_vs[-1])
    ev_as = []
    for d in range(4):
        for nsl in range(4):
            pa = PA[d]
            evm = kb.mm(pa[0:66, 0:512], T["D1_sb"][32 * d:32 * d + 32, 0:66],
                        VS[32 * d:32 * d + 32, nsl * 512:(nsl + 1) * 512],
                        True, True,
                        waits=ev_vs + [kb.ev_d1, kb.pa_free[d]], sig=True,
                        tp=(32 * d, 0), tag=f"{tag}s1")
            dst = arena_ap(kb, XB + CV_AS + 16 * nsl + d,
                           [[4, 4], [64, 128]], 66)
            e = kb.ecp(d, dst,
                       pa[0:66, 0:512].rearrange("p (gl ch) -> p gl ch", gl=4),
                       waits=[evm, kb.bz("cv_as")], sig=True)
            kb.pa_free[d] = e
            ev_as.append(e)
    kb.setbz("cv_as", ev_as[-1])
    ev_at = []
    for w0 in range(0, 64, 8):
        bank = (w0 // 8) % 2
        last = None
        for jj in range(8):
            w = w0 + jj
            last = kb.tr(PT[bank][:, jj * 128:jj * 128 + 66],
                         AS[0:66, w * 128:(w + 1) * 128],
                         waits=(ev_as + [kb.pt_free[bank]]) if jj == 0 else [],
                         sig=(jj == 7), tag=f"{tag}mt")
        e = kb.ecp(bank,
                   ATb.rearrange("p (w q) -> p w q", q=66)[:, w0:w0 + 8, :],
                   PT[bank][:, 0:1024].rearrange("p (w q) -> p w q",
                                                 q=128)[:, :, 0:66],
                   waits=[last, kb.bz("cv_at"), kb.bz("cv_h")], sig=True)
        kb.pt_free[bank] = e
        ev_at.append(e)
    kb.setbz("cv_at", ev_at[-1])
    atv = ATb.rearrange("p (w q) -> p w q", q=66)
    ev_xs = []
    for s in range(66):
        comp, q = s // 33, s % 33
        bk = (s // 8) % 2
        b = PA[bk]
        off = (s % 8) * 64
        first = (s % 8 == 0)
        movr = atv[:, 0:64, q]
        movi = atv[:, 0:64, 33 + q]
        if comp == 0:
            l1, l2 = kb.fwd_st["tc"], kb.fwd_st["mts"]
        else:
            l1, l2 = kb.fwd_st["ts"], kb.fwd_st["tc"]
        w0 = ev_at + kb.fwd_ev
        if first:
            w0.append(kb.pa_free[bk])
        kb.mm(b[:, off:off + 64], l1[:, q * 128:(q + 1) * 128], movr,
              first, False, waits=w0, tag=f"{tag}s2")
        em = kb.mm(b[:, off:off + 64], l2[:, q * 128:(q + 1) * 128], movi,
                   False, (s % 8 == 7) or (s == 65),
                   sig=(s % 8 == 7) or (s == 65), waits=[], tag=f"{tag}s2b")
        if (s % 8 == 7) or s == 65:
            n = (s % 8) + 1
            s0 = s - n + 1
            dst = arena_ap(kb, XB + CV_XS + s0, [[1, n], [66, 64]])
            e = kb.ecp(bk, dst,
                       b[:, 0:n * 64].rearrange("p (sl w) -> p sl w", w=64),
                       waits=[em, kb.bz("cv_xs")], sig=True)
            kb.pa_free[bk] = e
            kb.setbz("cv_xs", e)
            ev_xs.append(e)
    return ev_xs


def conv_bwd_chunk(kb, XB, hsrc, Gc, VFc, ev_xs, ev_g, tag):
    T = kb.t
    AR, PA, PT = T["ARENA"], T["PA"], T["PT"]
    XS = AR[:, XB + CV_XS:XB + CV_XS + 4224]
    Yb = AR[:, XB + CV_Y:XB + CV_Y + 4224]
    BS = AR[:, XB + CV_BS:XB + CV_BS + 4224]
    BT = AR[:, XB + CV_BT:XB + CV_BT + 528]
    YFR = AR[:, XB + CV_YFR:XB + CV_YFR + 8192]
    YCH = AR[:, XB + CV_YCH:XB + CV_YCH + 2048]
    Hb = AR[:, XB + CV_H:XB + CV_H + 4224]
    eh = kb.dma("sp", Hb[:, 0:2112], hsrc[:, 0:2112],
                waits=[kb.bz("cv_h")] + list(ev_xs), tag=f"{tag}h")
    eh2 = kb.dma("pop", Hb[:, 2112:4224], hsrc[:, 2112:4224],
                 waits=[kb.bz("cv_h")] + list(ev_xs), tag=f"{tag}h2")
    X3 = XS.rearrange("p (w q) -> p w q", q=66)
    Y3 = Yb.rearrange("p (w q) -> p w q", q=66)
    H3 = Hb.rearrange("p (w q) -> p w q", q=66)
    Xr, Xi = X3[:, :, 0:33], X3[:, :, 33:66]
    Yr, Yi = Y3[:, :, 0:33], Y3[:, :, 33:66]
    Hr, Hi = H3[:, :, 0:33], H3[:, :, 33:66]
    w0 = [eh, eh2] + list(ev_xs)
    e1 = kb.tt(Yr, Xr, Hr, ALU.mult, waits=w0 + [kb.bz("cv_y")], sig=True)
    e2 = kb.tt(Yi, Xr, Hi, ALU.mult, waits=[e1], sig=True)
    e3 = kb.tt(Xr, Xi, Hi, ALU.mult, waits=[e2], sig=True)
    e4 = kb.tt(Yr, Yr, Xr, ALU.subtract, waits=[e3], sig=True)
    e5 = kb.tt(Xr, Xi, Hr, ALU.mult, waits=[e4], sig=True)
    e6 = kb.tt(Yi, Yi, Xr, ALU.add, waits=[e5], sig=True)
    kb.setbz("cv_y", e6)
    kb.setbz("cv_h", e6)
    ev_y = [e4, e6]
    yvf = Yb.rearrange("p (w q) -> p w q", q=66)
    ev_bs = []
    for s in range(66):
        comp, q = s // 33, s % 33
        bk = 2 + (s // 8) % 2
        b = PA[bk]
        off = (s % 8) * 64
        first = (s % 8 == 0)
        movr = yvf[:, 0:64, q]
        movi = yvf[:, 0:64, 33 + q]
        if comp == 0:
            l1s = kb.i1_st["ec"][:, q * 128:(q + 1) * 128]
            l2s = i1_mes_slice(kb, q)
        else:
            l1s = kb.i1_st["es"][:, q * 128:(q + 1) * 128]
            l2s = kb.i1_st["ec"][:, q * 128:(q + 1) * 128]
        w0 = list(ev_y) + kb.i1_ev
        if first:
            w0.append(kb.pa_free[bk])
        kb.mm(b[:, off:off + 64], l1s, movr,
              first, False, waits=w0, tag=f"{tag}i1")
        em = kb.mm(b[:, off:off + 64], l2s, movi,
                   False, (s % 8 == 7) or (s == 65),
                   sig=(s % 8 == 7) or (s == 65), waits=[], tag=f"{tag}i1b")
        if (s % 8 == 7) or s == 65:
            n = (s % 8) + 1
            s0 = s - n + 1
            dst = arena_ap(kb, XB + CV_BS + s0, [[1, n], [66, 64]])
            e = kb.ecp(bk, dst,
                       b[:, 0:n * 64].rearrange("p (sl w) -> p sl w", w=64),
                       waits=[em, kb.bz("cv_bs")], sig=True)
            kb.pa_free[bk] = e
            kb.setbz("cv_bs", e)
            ev_bs.append(e)
    ev_fr = []
    for w0_ in range(0, 64, 4):
        bank = (w0_ // 4) % 2
        pb = PA[4 + (w0_ // 4) % 2]
        last = None
        for jj in range(4):
            w = w0_ + jj
            last = kb.tr(PT[bank][0:66, jj * 128:(jj + 1) * 128],
                         BS[:, w * 66:(w + 1) * 66],
                         waits=(ev_bs + [kb.pt_free[bank]]) if jj == 0 else [],
                         sig=(jj == 3), tag=f"{tag}t2")
        eb = kb.ecp(bank,
                    BT[0:66, 0:512].rearrange("p (j c) -> p j c", c=128),
                    PT[bank][0:66, 0:512].rearrange("p (j c) -> p j c", c=128),
                    waits=[last, kb.bz("cv_bt")], sig=True)
        kb.pt_free[bank] = eb
        evm = None
        for jj in range(4):
            first = (jj == 0)
            evm = kb.mm(pb[0:32, jj * 128:(jj + 1) * 128], T["VRVI"][0:66, :],
                        BT[0:66, jj * 128:(jj + 1) * 128], first, jj == 3,
                        waits=[eb, kb.ev_vrvi] +
                        ([kb.pa_free[4 + bank]] if first else []),
                        sig=(jj == 3), tag=f"{tag}i2")
        kb.setbz("cv_bt", evm)
        dst = bass.AP(T["ARENA"][:].tensor, XB + CV_YFR + w0_ * 2,
                      [[ARN, 32], [2, 4], [128, 64], [1, 2]])
        e = kb.ecp(4 + bank, dst,
                   pb[0:32, 0:512].rearrange("p (j b c) -> p j b c", b=64, c=2),
                   waits=[evm, kb.bz("cv_yfr")], sig=True)
        kb.pa_free[4 + bank] = e
        kb.setbz("cv_yfr", e)
        ev_fr.append(e)
    ev_ych = []
    for b0 in range(0, 64, 8):
        bank = (b0 // 8) % 2
        last = None
        for jj in range(8):
            bb = b0 + jj
            last = kb.tr(PT[bank][:, jj * 32:(jj + 1) * 32],
                         YFR[0:32, bb * 128:(bb + 1) * 128],
                         waits=(ev_fr + [kb.pt_free[bank]]) if jj == 0 else [],
                         sig=(jj == 7), tag=f"{tag}et")
        dst = bass.AP(T["ARENA"][:].tensor, XB + CV_YCH + b0,
                      [[ARN, 128], [1, 8], [64, 32]])
        e = kb.ecp(bank, dst,
                   PT[bank][:, 0:256].rearrange("p (j c) -> p j c", c=32),
                   waits=[last, kb.bz("cv_ych")], sig=True)
        kb.pt_free[bank] = e
        kb.setbz("cv_ych", e)
        ev_ych.append(e)
    dstv = VFc.rearrange("p (a c) -> p c a", c=32)
    ev = kb.tt(dstv, YCH.rearrange("p (c a) -> p c a", a=64),
               Gc.rearrange("p (c a) -> p c a", a=64), ALU.mult,
               waits=ev_ych + list(ev_g), sig=True, tag=f"{tag}gate")
    kb.setbz("cv_ych", ev)
    kb.setbz("cv_yfr", ev)
    kb.setbz("cv_vs", ev)
    kb.setbz("cv_as", ev)
    kb.setbz("cv_xs", ev)
    kb.setbz("cv_at", ev)
    return ev


def load_fwd_set(kb, waits=()):
    T, dram = kb.t, kb.dram
    AR = T["ARENA"]
    evs = []
    st = {}
    for k, nm in (("tc", "c_fft_tc"), ("ts", "c_fft_ts"), ("mts", "c_fft_mts")):
        base = CR + {"tc": 0, "ts": 4224, "mts": 8448}[k]
        ap = AR[:, base:base + 4224]
        e = kb.dma("sp", ap.rearrange("p (q c) -> p q c", c=128),
                   dram[nm].transpose([1, 0, 2]),
                   waits=list(waits) + [kb.bz("crw")], tag=f"fw{k}")
        st[k] = ap
        evs.append(e)
    kb.fwd_st, kb.fwd_ev = st, evs


def load_i1_set(kb, waits=()):
    T, dram = kb.t, kb.dram
    AR = T["ARENA"]
    evs = []
    for k, nm in (("ec", "c_fft_ec"), ("es", "c_fft_es")):
        base = CR + 12672 + {"ec": 0, "es": 4224}[k]
        ap = AR[:, base:base + 4224]
        e = kb.dma("sp", ap.rearrange("p (q c) -> p q c", c=128),
                   dram[nm].transpose([1, 0, 2]),
                   waits=list(waits) + [kb.bz("crw")], tag=f"i1{k}")
        if k == "ec":
            kb.i1_st = {"ec": ap}
        else:
            kb.i1_st["es"] = ap
        evs.append(e)
    kb.i1_mesA = AR[:, CR + 21120:CR + 24576]
    kb.i1_mesB = AR[:, WD0 + 1056:WD0 + 1824]
    e1 = kb.dma("sp", kb.i1_mesA.rearrange("p (q c) -> p q c", c=128),
                dram["c_fft_mes"][0:27].transpose([1, 0, 2]),
                waits=list(waits) + [kb.bz("crw")], tag="i1mesA")
    e2 = kb.dma("sp", kb.i1_mesB.rearrange("p (q c) -> p q c", c=128),
                dram["c_fft_mes"][27:33].transpose([1, 0, 2]),
                waits=list(waits) + [kb.bz("wd0t")], tag="i1mesB")
    kb.i1_ev = evs + [e1, e2]


def i1_mes_slice(kb, q):
    if q < 27:
        return kb.i1_mesA[:, q * 128:(q + 1) * 128]
    return kb.i1_mesB[:, (q - 27) * 128:(q - 27 + 1) * 128]


def gate_pipeline(kb, s, i, ev_ctrl, ev_prev, tag):
    T, dram = kb.t, kb.dram
    AR, PA = T["ARENA"], T["PA"]
    XC = AR[:, CR:CR + 16384]
    GW = AR[:, CR + 16384:CR + 24576]
    GRb = AR[:, GR:GR + 16384]
    bw = kb.wprep
    cb = bw[f"conv_b{i}"]
    gb = bw[f"gate_b{i}"]
    k0, k1_, k2_ = bw[f"cw{i}0"], bw[f"cw{i}1"], bw[f"cw{i}2"]
    ev_gw = []
    for cc in range(NDC):
        ev_gw.append(kb.dma("pop", GW[:, cc * 1024:(cc + 1) * 1024],
                            dram["L1_attn_gate_w"][i, cc * 128:(cc + 1) * 128, :],
                            waits=list(ev_prev) + [kb.bz("crw2")],
                            tag=f"{tag}gw"))
    ev_xc = []
    for cc in range(NDC):
        c0 = GRb[:, cc * 2048:(cc + 1) * 2048]
        x0 = XC[:, cc * 2048:(cc + 1) * 2048]
        w0 = list(ev_ctrl) + list(ev_prev) + \
            [cb[0], k0[0], k1_[0], k2_[0], kb.bz("crw")]
        e1 = kb.ts2(x0, c0, T["BIAS"][:, k1_[1] + cc:k1_[1] + cc + 1],
                    T["BIAS"][:, cb[1] + cc:cb[1] + cc + 1],
                    ALU.mult, ALU.add, waits=w0, sig=True, tag=f"{tag}x1")
        e2 = kb.stt(x0[:, 1:2048], c0[:, 0:2047],
                    T["BIAS"][:, k0[1] + cc:k0[1] + cc + 1], x0[:, 1:2048],
                    ALU.mult, ALU.add, waits=[e1], sig=True, tag=f"{tag}x2")
        e3 = kb.stt(x0[:, 0:2047], c0[:, 1:2048],
                    T["BIAS"][:, k2_[1] + cc:k2_[1] + cc + 1], x0[:, 0:2047],
                    ALU.mult, ALU.add, waits=[e2], sig=True, tag=f"{tag}x3")
        ev_xc.append(e3)
    ev_g = [None] * NDC
    for oc in range(NDC):
        for tsl in range(4):
            g = oc * 4 + tsl
            pa = PA[4 + g % 2]
            evm = None
            for cc in range(NDC):
                w0 = [ev_gw[cc], ev_xc[cc]]
                if cc == 0:
                    w0.append(kb.pa_free[4 + g % 2])
                evm = kb.mm(pa[:, 0:512],
                            GW[:, cc * 1024 + oc * 128:cc * 1024 + (oc + 1) * 128],
                            XC[:, cc * 2048 + tsl * 512:cc * 2048 + tsl * 512 + 512],
                            cc == 0, cc == NDC - 1, waits=w0,
                            sig=(cc == NDC - 1), tag=f"{tag}gm")
            es = kb.act(GRb[:, oc * 2048 + tsl * 512:oc * 2048 + tsl * 512 + 512],
                        pa[:, 0:512], AF.Sigmoid,
                        bias=T["BIAS"][:, gb[1] + oc:gb[1] + oc + 1],
                        waits=[evm, gb[0]] + ev_xc, sig=True, tag=f"{tag}sg")
            eg = kb.tt(GRb[:, oc * 2048 + tsl * 512:oc * 2048 + tsl * 512 + 512],
                       GRb[:, oc * 2048 + tsl * 512:oc * 2048 + tsl * 512 + 512],
                       XC[:, oc * 2048 + tsl * 512:oc * 2048 + tsl * 512 + 512],
                       ALU.mult, waits=[es], sig=True, tag=f"{tag}gx")
            kb.pa_free[4 + g % 2] = eg
            ev_g[oc] = eg
    kb.setbz("crw", ev_g[-1])
    kb.setbz("crw2", ev_g[-1])
    return ev_g


def rr_sin2(kb, dst, u, gi, scr, waits, n=2048, npart=64):
    ec = kb.cp("dve", gi[0:npart, 0:n], u[0:npart, 0:n], waits=waits, sig=True)
    eb = kb.tt(scr[0:npart, 0:n], u[0:npart, 0:n], gi[0:npart, 0:n],
               ALU.subtract, waits=[ec], sig=True)
    ed = kb.ts1(dst[0:npart, 0:n], scr[0:npart, 0:n], 0.5, ALU.is_gt,
                waits=[eb], sig=True)
    ee = kb.tt(scr[0:npart, 0:n], scr[0:npart, 0:n], dst[0:npart, 0:n],
               ALU.subtract, waits=[ed], sig=True)
    ef = kb.ts1(dst[0:npart, 0:n], scr[0:npart, 0:n], -0.5, ALU.is_lt,
                waits=[ee], sig=True)
    eg = kb.tt(scr[0:npart, 0:n], scr[0:npart, 0:n], dst[0:npart, 0:n],
               ALU.add, waits=[ef], sig=True)
    eh = kb.ts1(scr[0:npart, 0:n], scr[0:npart, 0:n], 2.0 * np.pi, ALU.mult,
                waits=[eg], sig=True)
    return kb.act(dst[0:npart, 0:n], scr[0:npart, 0:n], AF.Sin, waits=[eh],
                  sig=True)


def prep_l1_attn(kb):
    P, T, dram = kb.P, kb.t, kb.dram
    AR, PA, PT = T["ARENA"], T["PA"], T["PT"]
    for nm, key, n in (("L1_attn_in_proj_b", "ipb", 24),
                       ("L1_attn_out_proj_b", "opb", 8)):
        kb.wprep[key] = (kb.load_bias(dram[nm], kb.bias_cursor, n),
                         kb.bias_cursor)
        kb.bias_cursor += n
    for i in range(2):
        for nm, key in (("L1_attn_gate_b", f"gate_b{i}"),
                        ("L1_attn_conv_b", f"conv_b{i}")):
            kb.wprep[key] = (kb.load_bias(dram[nm][i], kb.bias_cursor, 8),
                             kb.bias_cursor)
            kb.bias_cursor += 8
        for k in range(3):
            kb.wprep[f"cw{i}{k}"] = (
                kb.load_bias(dram["L1_attn_conv_w"][i, k, 0, :],
                             kb.bias_cursor, 8), kb.bias_cursor)
            kb.bias_cursor += 8
        bc = kb.bias_cursor
        ev = kb.load_bias(dram["L1_attn_decay"][i], bc, 8)
        ea = kb.act(T["BIAS"][:, bc:bc + 8], T["BIAS"][:, bc:bc + 8], AF.Abs,
                    waits=[ev], sig=True)
        en = kb.ts1(T["BIAS"][:, bc:bc + 8], T["BIAS"][:, bc:bc + 8],
                    -float(L), ALU.mult, waits=[ea], sig=True)
        kb.wprep[f"ndec{i}"] = (en, bc)
        kb.bias_cursor += 8

    load_fwd_set(kb, waits=[getattr(kb, "wld_free", None),
                            kb.bz("cr_free"), kb.bz("cr_free2")])
    w_gr = [kb.bz("mrb"), kb.bz("mib"), kb.bz("qstg2")]

    ev_tl = kb.dma("sp", T["TMP32"][0:1, 0:L], dram["c_tlin"],
                   waits=[kb.bz("tmplo"), kb.bz("sqdum")])
    TBC = AR[:].bitcast(F32)[:, GR // 2:GR // 2 + 2048]
    ev_tb = []
    for tsl in range(4):
        pa = PA[5]
        e = kb.mm(pa[:, 0:512], T["ones_sb"][0:1, :],
                  T["TMP32"][0:1, tsl * 512:(tsl + 1) * 512], True, True,
                  waits=[ev_tl, kb.ev_ones, kb.pa_free[5]], sig=True)
        e2 = kb.cp("dve", TBC[:, tsl * 512:(tsl + 1) * 512], pa[:, 0:512],
                   waits=[e] + (w_gr if tsl == 0 else []), sig=True)
        kb.pa_free[5] = e2
        ev_tb.append(e2)

    ARF = AR[:].bitcast(F32)
    VFF = ARF[:, VFR // 2:VFR // 2 + 2048]
    G_i = AR[:].bitcast(I32)[:, (VFR + 4096) // 2:(VFR + 4096) // 2 + 2048]
    G_s = ARF[:, (VFR + 8192) // 2:(VFR + 8192) // 2 + 2048]
    ENC = AR[:, VFR + 12288:VFR + 12288 + 2048]
    w1b = AR[:, VFR + 14336:VFR + 14336 + 64]
    w2b = AR[:, VFR + 14400:VFR + 14400 + 64]
    w3b = AR[:, VFR + 14464:VFR + 14464 + 1024]
    H3 = AR[:, GR + 4096:GR + 4096 + 2048]
    WIN = ARF[:, (GR + 6144) // 2:(GR + 6144) // 2 + 2048]
    HVF = AR[:, GR + 10240:GR + 10240 + 2048]
    H1 = AR[:, GR + 12288:GR + 12288 + 2048]
    H2b = AR[:, GR + 14336:GR + 14336 + 2048]
    e_st = None
    for i in range(2):
        evf = kb.dma("sp", T["ROWS"][0:3, 8:9],
                     dram["L1_attn_freqs"][i].rearrange("(n o) -> n o", o=1),
                     waits=[kb.bz("rows8")])
        evf2 = kb.dma("sp", T["ROWS"][3:6, 8:9],
                      dram["L1_attn_freqs"][i].rearrange("(n o) -> n o", o=1),
                      waits=[evf])
        evo = kb.dma("sp", T["ROWS"][0:6, 11:12], dram["c_encoff"],
                     waits=[kb.bz("rows8")])
        eu = kb.ts2(VFF[0:6, 0:2048], TBC[0:6, 0:2048], T["ROWS"][0:6, 8:9],
                    T["ROWS"][0:6, 11:12], ALU.mult, ALU.add,
                    waits=[evf, evf2, evo, kb.bz("h1")] + ev_tb +
                    [kb.bz("vff")], sig=True)
        kb.setbz("rows8", eu)
        es_ = rr_sin2(kb, ENC[0:6, :], VFF[0:6, 0:2048], G_i, G_s,
                      [eu, kb.bz("h1")], n=2048, npart=6)
        ec_ = es_
        kb.setbz("vff", es_)
        et_ = kb.dma("pop", ENC[6:7, 0:2048], dram["c_tlin"],
                     waits=[kb.bz("enct"), kb.bz("h1")], tag="enct")
        kb.setbz("enct", et_)
        ew1 = kb.dma("pop", w1b[0:7, :], dram["L1_attn_mlp_w1"][i],
                     waits=[kb.bz("w1b")])
        eb1 = kb.dma("sp", T["ROWS"][0:64, 9:10],
                     dram["L1_attn_mlp_b1"][i].rearrange("(n o) -> n o", o=1),
                     waits=[kb.bz("rows9")])
        ev_h1 = []
        for tsl in range(4):
            pa = PA[0]
            em = kb.mm(pa[0:64, 0:512], w1b[0:7, 0:64],
                       ENC[0:7, tsl * 512:(tsl + 1) * 512], True, True,
                       waits=[ew1, es_, ec_, et_, kb.pa_free[0]], sig=True)
            eg = kb.act(H1[0:64, tsl * 512:(tsl + 1) * 512], pa[0:64, 0:512],
                        AF.Gelu, bias=T["ROWS"][0:64, 9:10],
                        waits=[em, eb1, kb.bz("h1")] +
                        (w_gr if (i == 0 and tsl == 0) else []), sig=True)
            kb.pa_free[0] = eg
            ev_h1.append(eg)
        kb.setbz("h1", ev_h1[-1])
        kb.setbz("w1b", ev_h1[-1])
        kb.setbz("rows9", ev_h1[-1])
        ew2 = kb.dma("pop", w2b[0:64, :], dram["L1_attn_mlp_w2"][i],
                     waits=[kb.bz("w2b")])
        eb2 = kb.dma("sp", T["ROWS"][0:64, 10:11],
                     dram["L1_attn_mlp_b2"][i].rearrange("(n o) -> n o", o=1),
                     waits=[kb.bz("rows10")])
        ev_h2 = []
        for tsl in range(4):
            pa = PA[1]
            em = kb.mm(pa[0:64, 0:512], w2b[0:64, 0:64],
                       H1[0:64, tsl * 512:(tsl + 1) * 512], True, True,
                       waits=[ew2, ev_h1[tsl], kb.pa_free[1]], sig=True)
            eg = kb.act(H2b[0:64, tsl * 512:(tsl + 1) * 512], pa[0:64, 0:512],
                        AF.Gelu, bias=T["ROWS"][0:64, 10:11],
                        waits=[em, eb2, kb.bz("h2")], sig=True)
            kb.pa_free[1] = eg
            ev_h2.append(eg)
        kb.setbz("h2", ev_h2[-1])
        kb.setbz("w2b", ev_h2[-1])
        kb.setbz("rows10", ev_h2[-1])
        ew3 = kb.dma("pop", w3b[0:64, :], dram["L1_attn_mlp_w3"][i],
                     waits=[kb.bz("w3b")])
        b3 = kb.load_bias(dram["L1_attn_mlp_b3"][i], kb.bias_cursor, 8)
        b3c = kb.bias_cursor
        kb.bias_cursor += 8
        nd = kb.wprep[f"ndec{i}"]
        for cc in range(NDC):
            ev_h3 = []
            for tsl in range(4):
                pa = PA[2]
                em = kb.mm(pa[:, 0:512], w3b[0:64, cc * 128:(cc + 1) * 128],
                           H2b[0:64, tsl * 512:(tsl + 1) * 512], True, True,
                           waits=[ew3] + ev_h2 + [kb.pa_free[2]], sig=True)
                e3_ = kb.ts1(H3[:, tsl * 512:(tsl + 1) * 512], pa[:, 0:512],
                             T["BIAS"][:, b3c + cc:b3c + cc + 1], ALU.add,
                             waits=[em, b3, kb.bz("h3")], sig=True)
                kb.pa_free[2] = e3_
                ev_h3.append(e3_)
            kb.setbz("h3", ev_h3[-1])
            ew_ = kb.ts1(WIN[:, 0:2048], TBC[:, 0:2048],
                         T["BIAS"][:, nd[1] + cc:nd[1] + cc + 1], ALU.mult,
                         waits=[nd[0], kb.bz("win")] + ev_tb, sig=True)
            ex_ = kb.act(WIN[:, 0:2048], WIN[:, 0:2048], AF.Exp, waits=[ew_],
                         sig=True)
            dstv = HVF.rearrange("p (a c) -> p c a", c=32)
            eh_ = kb.tt(dstv, H3.rearrange("p (c a) -> p c a", a=64),
                        WIN.rearrange("p (c a) -> p c a", a=64), ALU.mult,
                        waits=[ex_] + ev_h3 +
                        [kb.bz("hvf"), kb.bz("cv_vs")], sig=True)
            kb.setbz("win", eh_)
            kb.setbz("hvf", eh_)
            ev_xs = conv_fwd_chunk(kb, HVF, XFR, [eh_], f"hs{i}.{cc}")
            e_st = kb.dma("pop", dram["hspec"][i, cc],
                          AR[:, XFR + CV_XS:XFR + CV_XS + 4224],
                          waits=ev_xs, tag=f"hst{i}.{cc}")
            kb.setbz("cv_xs", e_st)
            kb.setbz("cv_as", e_st)
        kb.setbz("w3b", kb.bz("h3"))
    kb.setbz("xf_w", e_st)
    kb.setbz("sub_exit", e_st)


def sub_l1attn(kb, s):
    T, dram = kb.t, kb.dram
    AR, PA, PT = T["ARENA"], T["PA"], T["PT"]
    kb.setbz("hr0", kb.bz("sub_exit"))
    kb.setbz("hr1", kb.bz("sub_exit"))
    kb.setbz("wd0t", kb.bz("sub_exit"))
    XF, ev_xf = make_xf(kb, XFR, waits=[kb.bz("xf_w")])
    VFb = AR[:, VFR:VFR + 16384]
    GRb = AR[:, GR:GR + 16384]
    IPW = AR[:, CR:CR + 24576]
    ipb = kb.wprep["ipb"]
    opb = kb.wprep["opb"]

    ev_ipw = []
    for dc in range(NDC):
        ev_ipw.append(kb.dma("pop", IPW[:, dc * 3072:(dc + 1) * 3072],
                             dram["L1_attn_in_proj_w"][dc * 128:(dc + 1) * 128, :],
                             waits=[kb.bz("crw"), kb.bz("crw2"),
                                    kb.bz("cr_free"), kb.bz("cr_free2"),
                                    kb.bz("woutb"), kb.bz("sub_exit")],
                             tag=f"ipw{dc}"))
    ev_v = [None] * NDC
    ev_c0 = [[] for _ in range(NDC)]
    stg = AR[:, WD0 + 2048:WD0 + 3072]
    ev_c1st = None
    ee = None
    for oc in range(24):
        for tsl in range(4):
            g = oc * 4 + tsl
            pa = PA[4 + g % 2]
            evm = None
            for dc in range(NDC):
                w0 = [ev_ipw[dc]]
                if dc == 0:
                    w0 += [kb.pa_free[4 + g % 2]] + list(ev_xf)
                evm = kb.mm(pa[:, 0:512],
                            IPW[:, dc * 3072 + oc * 128:dc * 3072 + (oc + 1) * 128],
                            XF[:, dc * 2048 + tsl * 512:dc * 2048 + tsl * 512 + 512],
                            dc == 0, dc == NDC - 1, waits=w0,
                            sig=(dc == NDC - 1), tag=f"ip{oc}.{tsl}")
            bias = T["BIAS"][:, ipb[1] + oc:ipb[1] + oc + 1]
            if oc < 8:
                dst = bass.AP(T["ARENA"][:].tensor,
                              VFR + oc * 2048 + 8 * tsl,
                              [[ARN, 128], [1, 8], [32, 64]])
                ee = kb.ets1b(4 + g % 2, dst, pa[:, 0:512].rearrange(
                    "p (c a) -> p c a", a=64), bias,
                    waits=[evm, ipb[0], kb.bz("vfw")], sig=True,
                    tag=f"ipv{oc}")
                ev_v[oc] = ee
                kb.setbz("vfw", ee)
            elif oc < 16:
                k = oc - 8
                ee = kb.ets1b(4 + g % 2, GRb[:, k * 2048 + tsl * 512:
                                             k * 2048 + tsl * 512 + 512],
                              pa[:, 0:512], bias,
                              waits=[evm, ipb[0], kb.bz("grw")], sig=True,
                              tag=f"ipc{oc}")
                ev_c0[k].append(ee)
                kb.setbz("grw", ee)
            else:
                k = oc - 16
                sg = stg[:, (g % 2) * 512:(g % 2) * 512 + 512]
                ee = kb.ets1b(4 + g % 2, sg, pa[:, 0:512], bias,
                              waits=[evm, ipb[0], kb.bz(f"c1s{g % 2}")],
                              sig=True, tag=f"ipd{oc}")
                ed = kb.dma("pop", dram["ctrl1_sp"][k][:, tsl * 512:
                                                       (tsl + 1) * 512],
                            sg, waits=[ee], tag=f"c1d{oc}")
                kb.setbz(f"c1s{g % 2}", ed)
                ev_c1st = ed
            kb.pa_free[4 + g % 2] = ee

    ev_g = gate_pipeline(kb, s, 0, [e for l in ev_c0 for e in l], [ee], "g0")
    load_fwd_set(kb)
    load_i1_set(kb)
    ev_vf_new = [None] * NDC
    for cc in range(NDC):
        ev_xs = conv_fwd_chunk(kb, VFb[:, cc * 2048:(cc + 1) * 2048], XFR,
                               [ev_v[cc]], f"c0.{cc}")
        ev_vf_new[cc] = conv_bwd_chunk(
            kb, XFR, dram["hspec"][0, cc],
            GRb[:, cc * 2048:(cc + 1) * 2048],
            VFb[:, cc * 2048:(cc + 1) * 2048], ev_xs,
            [ev_g[cc]], f"c0.{cc}")
    kb.setbz("crw", ev_vf_new[-1])
    kb.setbz("crw2", ev_vf_new[-1])
    kb.setbz("wd0t", ev_vf_new[-1])
    ev_c1 = []
    for k in range(NDC):
        ev_c1.append(kb.dma("pop", GRb[:, k * 2048:(k + 1) * 2048],
                            dram["ctrl1_sp"][k],
                            waits=[ev_c1st, ev_vf_new[k]],
                            tag=f"c1l{k}"))
    ev_g1 = gate_pipeline(kb, s, 1, ev_c1, [ev_vf_new[-1]], "g1")
    load_fwd_set(kb)
    load_i1_set(kb)
    for cc in range(NDC):
        ev_xs = conv_fwd_chunk(kb, VFb[:, cc * 2048:(cc + 1) * 2048], XFR,
                               [ev_vf_new[cc]], f"c1.{cc}")
        ev_vf_new[cc] = conv_bwd_chunk(
            kb, XFR, dram["hspec"][1, cc],
            GRb[:, cc * 2048:(cc + 1) * 2048],
            VFb[:, cc * 2048:(cc + 1) * 2048], ev_xs,
            [ev_g1[cc]], f"c1.{cc}")
    kb.setbz("crw", ev_vf_new[-1])
    kb.setbz("crw2", ev_vf_new[-1])
    kb.setbz("wd0t", ev_vf_new[-1])
    OPW = AR[:, CR:CR + 8192]
    ev_opw = []
    for dc in range(NDC):
        ev_opw.append(kb.dma("pop", OPW[:, dc * 1024:(dc + 1) * 1024],
                             dram["L1_attn_out_proj_w"][dc * 128:(dc + 1) * 128, :],
                             waits=[kb.bz("crw"), kb.bz("crw2")],
                             tag=f"opw{dc}"))
    ev_yout = []
    for oc in range(NDC):
        for fsl in range(4):
            g = oc * 4 + fsl
            pa = PA[4 + g % 2]
            evm = None
            for dc in range(NDC):
                w0 = [ev_opw[dc], ev_vf_new[dc]]
                if dc == 0:
                    w0.append(kb.pa_free[4 + g % 2])
                evm = kb.mm(pa[:, 0:512],
                            OPW[:, dc * 1024 + oc * 128:dc * 1024 + (oc + 1) * 128],
                            VFb[:, dc * 2048 + fsl * 512:dc * 2048 + fsl * 512 + 512],
                            dc == 0, dc == NDC - 1, waits=w0,
                            sig=(dc == NDC - 1), tag=f"op{oc}.{fsl}")
            dst = bass.AP(T["ARENA"][:].tensor, GR + oc * 2048 + 16 * fsl,
                          [[ARN, 128], [1, 16], [64, 32]])
            ee2 = kb.ets1b(4 + g % 2, dst,
                           pa[:, 0:512].rearrange("p (a c) -> p a c", c=32),
                           T["BIAS"][:, opb[1] + oc:opb[1] + oc + 1],
                           waits=[evm, opb[0], kb.bz("grw")],
                           sig=True, tag=f"ope{oc}.{fsl}")
            kb.pa_free[4 + g % 2] = ee2
            ev_yout.append(ee2)
    kb.setbz("grw", ev_yout[-1])
    kb.setbz("crw", ev_yout[-1])
    er = eq = None
    for j in range(NTIL):
        def piece(k, j=j):
            return GRb[:, k * 2048 + j * 128: k * 2048 + (j + 1) * 128]
        er, eq = post_sub_tile(kb, j, piece, waits=ev_yout, tag="l1p")
        kb.setbz("xf_w", er)
        kb.setbz("grw", er)
    finish_stats(kb, [eq])
    kb.setbz("sub_exit", er)
    kb.ffnw_free = [er]



_NC_CACHE = {}


def kernel(**inputs):
    subset = inputs.pop("_subset", SUB_ALL)
    key = tuple(subset)
    if key not in _NC_CACHE:
        _NC_CACHE[key] = build_program(subset)
    nc = _NC_CACHE[key]
    C = host_constants()
    x = np.ascontiguousarray(inputs["L0_attn_x"])
    in_maps = []
    for core in range(8):
        m = {"x": np.ascontiguousarray(x[core * NSEQ:(core + 1) * NSEQ])}
        for nm, _ in INPUT_SPECS:
            m[nm] = np.ascontiguousarray(inputs[nm])
        for nm, arr in C.items():
            m["c_" + nm] = arr
        in_maps.append(m)
    res = run_bass_kernel_spmd(nc, in_maps, core_ids=list(range(8)))
    return np.concatenate([r["out"] for r in res.results], axis=0)
